# version 21
# baseline (speedup 1.0000x reference)
import contextlib
import numpy as np
import ml_dtypes

import concourse.bass as bass
import concourse.mybir as mybir
from concourse.bass_utils import run_bass_kernel_spmd

F32 = mybir.dt.float32
BF16 = mybir.dt.bfloat16
AF = mybir.ActivationFunctionType
ALU = mybir.AluOpType
AX = mybir.AxisListType

NCORES = 8
D = 1024
NCH = 8
NHEAD = 16
DH = 64
CH = 128
EPS_LN = 1e-5
EPS_DEN = 1e-6

BF = ml_dtypes.bfloat16



class Prog:
    COMPUTE_ENGINES = ("tensor", "vector", "scalar", "gpsimd")
    QUEUES = ("sync", "gpsimd", "scalar")

    def __init__(self, nc):
        self.nc = nc
        self.ops = []
        self.lastw = {}
        self.readers = {}
        self.qcount = {q: 0 for q in self.QUEUES}

    def _deps(self, reads, writes):
        deps = set()
        gb = self.lastw.get("__mem")
        if gb is not None:
            deps.add(gb)
        for k in reads:
            w = self.lastw.get(k)
            if w is not None:
                deps.add(w)
        for k in writes:
            w = self.lastw.get(k)
            if w is not None:
                deps.add(w)
            deps.update(self.readers.get(k, ()))
        return deps

    def _record(self, idx, reads, writes):
        for k in reads:
            self.readers.setdefault(k, []).append(idx)
        for k in writes:
            self.lastw[k] = idx
            self.readers[k] = []

    def op(self, engine, fn, reads=(), writes=(), tag=""):
        deps = self._deps(reads, writes)
        idx = len(self.ops)
        self.ops.append(dict(i=idx, eng=engine, kind="c", fn=fn, deps=deps,
                             tag=tag, inc=False))
        self._record(idx, reads, writes)
        return idx

    def barrier(self):
        last = {}
        for o in self.ops:
            last[o["eng"]] = o["i"]
        idxs = []
        for eng in ("sync", "gpsimd", "scalar", "tensor"):
            if eng in last or True:
                i = len(self.ops)
                deps = set(last.values())
                self.ops.append(dict(i=i, eng=eng, kind="c",
                                     fn=lambda e: e.nop(), deps=deps,
                                     tag=f"bar.{eng}", inc=False))
                idxs.append(i)
                last[eng] = i
        i = len(self.ops)
        self.ops.append(dict(i=i, eng="vector", kind="c",
                             fn=lambda e: e.nop(), deps=set(idxs),
                             tag="bar.vector", inc=False))
        self.lastw["__mem"] = i
        self.readers["__mem"] = []

    def dma(self, queue, transfers, reads=(), writes=(), tag=""):
        deps = self._deps(reads, writes)
        idx = len(self.ops)
        self.qcount[queue] += len(transfers)
        target = 16 * self.qcount[queue]
        self.ops.append(dict(i=idx, eng=queue, kind="dma", transfers=transfers,
                             deps=deps, tag=tag, inc=False))
        fidx = idx + 1
        self.ops.append(dict(i=fidx, eng=queue, kind="fence", deps=set(),
                             qtarget=target, tag=tag + ".fence", inc=False))
        for k in reads:
            self.readers.setdefault(k, []).append(fidx)
        for k in writes:
            self.lastw[k] = fidx
            self.readers[k] = []
        return fidx

    def emit(self):
        nc = self.nc
        ops = self.ops

        def waitable(d):
            return d + 1 if ops[d]["kind"] == "dma" else d

        for o in ops:
            o["deps"] = {waitable(d) for d in o["deps"] if waitable(d) != o["i"]}

        for o in ops:
            for d in o["deps"]:
                pd = ops[d]
                if pd["kind"] == "c" and pd["eng"] != o["eng"]:
                    pd["inc"] = True

        streams = {}
        for o in ops:
            streams.setdefault(o["eng"], []).append(o)
        for eng, lst in streams.items():
            c = 0
            for pos, o in enumerate(lst):
                if o.get("inc"):
                    c += 1
                o["cumval"] = c
                o["pos"] = pos

        with contextlib.ExitStack() as es:
            qsem = {q: es.enter_context(nc.semaphore(f"q_{q}"))
                    for q in self.QUEUES}
            csem = {e: es.enter_context(nc.semaphore(f"c_{e}"))
                    for e in self.COMPUTE_ENGINES + ("sync",)}
            block = es.enter_context(nc.Block())

            def emit_stream(eng_obj, eng_name):
                lst = streams.get(eng_name, [])
                high = {}
                drained = -1
                for o in lst:
                    need_drain = False
                    for d in sorted(o["deps"]):
                        pd = ops[d]
                        if pd["eng"] == eng_name:
                            if (pd["kind"] == "c" and eng_name != "tensor"
                                    and pd["pos"] > drained):
                                need_drain = True
                            continue
                        if pd["kind"] == "fence":
                            sem, val = qsem[pd["eng"]], pd["qtarget"]
                        else:
                            sem, val = csem[pd["eng"]], pd["cumval"]
                        if high.get(id(sem), 0) < val:
                            eng_obj.wait_ge(sem, val)
                            high[id(sem)] = val
                    if need_drain:
                        eng_obj.drain()
                        drained = o["pos"] - 1
                    if o["kind"] == "c":
                        inst = o["fn"](eng_obj)
                        if o.get("inc"):
                            if inst is None:
                                raise RuntimeError(f"op {o['tag']}: no inst")
                            inst.then_inc(csem[eng_name], 1)
                    elif o["kind"] == "dma":
                        for tr in o["transfers"]:
                            kw = tr[2] if len(tr) > 2 else {}
                            eng_obj.dma_start(out=tr[0], in_=tr[1], **kw) \
                                .then_inc(qsem[eng_name], 16)
                    else:
                        eng_obj.wait_ge(qsem[eng_name], o["qtarget"])

            for name in ("sync", "gpsimd", "vector", "scalar", "tensor"):
                getattr(block, name)(lambda e, n=name: emit_stream(e, n))


class Ctx:
    def __init__(self, nc, tok):
        self.nc = nc
        self.p = Prog(nc)
        self.es = contextlib.ExitStack()
        self.tok = tok
        self.tss = min(512, tok)
        self.nts = tok // self.tss
        self.tch = tok // CH
        self.uid = 0

    def sb(self, name, shape, dtype):
        es = getattr(self, "alloc_es", None) or self.es
        return es.enter_context(
            self.nc.sbuf_tensor(name, list(shape), dtype))

    def ps(self, name, shape, dtype=F32):
        return self.es.enter_context(
            self.nc.psum_tensor(name, list(shape), dtype))

    def name(self, base):
        self.uid += 1
        return f"{base}{self.uid}"

    def prealloc_ln(self):
        self.shared("lnxb", (128, self.tss), BF16)
        self.shared("lnsq", (128, self.tss), BF16)
        self.shared("lns1", (1, self.tok), F32)
        self.shared("lns2", (1, self.tok), F32)
        self.shared("lnt1", (128, self.tss), F32)
        self.shared("lngt", (128, self.tss), F32)

    def shared(self, name, shape, dtype):
        cache = getattr(self, "_shared", None)
        if cache is None:
            cache = self._shared = {}
        if name not in cache:
            if getattr(self, "scoped_shared", False):
                es = getattr(self, "alloc_es", None) or self.es
            else:
                es = self.es
            cache[name] = es.enter_context(
                self.nc.sbuf_tensor(name, list(shape), dtype))
        return cache[name]



def emit_ln_stats(ctx, src, src_key, rows, rows_key, pA, pB, bkA, bkB,
                  ones_col, eps=EPS_LN, tag="ln"):
    p = ctx.p
    tok, tss, nts = ctx.tok, ctx.tss, ctx.nts
    xb = ctx.shared("lnxb", (128, tss), BF16)
    sq = ctx.shared("lnsq", (128, tss), BF16)
    s1 = ctx.shared("lns1", (1, tok), F32)
    s2 = ctx.shared("lns2", (1, tok), F32)
    kxb, ksq, ks1, ks2 = "lnxb", "lnsq", "lns1", "lns2"
    for ts in range(nts):
        t0 = ts * tss
        for c in range(NCH):
            sl = src[:, c * tok + t0: c * tok + t0 + tss]
            p.op("vector", lambda v, sl=sl: v.tensor_copy(out=xb[:], in_=sl),
                 reads=[src_key], writes=[kxb], tag=f"{tag}.cast")
            p.op("scalar", lambda s, sl=sl: s.activation(
                out=sq[:], in_=sl, func=AF.Square),
                reads=[src_key], writes=[ksq], tag=f"{tag}.sq")
            p.op("tensor", lambda t, st=(c == 0), sp=(c == NCH - 1): t.matmul(
                pA[0:1, 0:tss], ones_col[:, 0:1], xb[:], start=st, stop=sp),
                reads=[kxb, "ones_col"], writes=[bkA], tag=f"{tag}.mm1")
            p.op("tensor", lambda t, st=(c == 0), sp=(c == NCH - 1): t.matmul(
                pB[0:1, 0:tss], ones_col[:, 0:1], sq[:], start=st, stop=sp),
                reads=[ksq, "ones_col"], writes=[bkB], tag=f"{tag}.mm2")
        p.op("vector", lambda v, t0=t0: v.tensor_copy(
            out=s1[:, t0:t0 + tss], in_=pA[0:1, 0:tss]),
            reads=[bkA], writes=[ks1], tag=f"{tag}.ev1")
        p.op("vector", lambda v, t0=t0: v.tensor_copy(
            out=s2[:, t0:t0 + tss], in_=pB[0:1, 0:tss]),
            reads=[bkB], writes=[ks2], tag=f"{tag}.ev2")
    a_row, b_row = rows["a"], rows["b"]
    ka, kb = rows_key + ".a", rows_key + ".b"
    p.op("vector", lambda v: v.tensor_scalar(
        out=b_row[:], in0=s1[:], scalar1=-1.0 / D, scalar2=None, op0=ALU.mult),
        reads=[ks1], writes=[kb], tag=f"{tag}.bneg")
    p.op("vector", lambda v: v.tensor_scalar(
        out=a_row[:], in0=s2[:], scalar1=1.0 / D, scalar2=None, op0=ALU.mult),
        reads=[ks2], writes=[ka], tag=f"{tag}.q")
    p.op("vector", lambda v: v.tensor_tensor(
        out=s1[:], in0=b_row[:], in1=b_row[:], op=ALU.mult),
        reads=[kb], writes=[ks1], tag=f"{tag}.msq")
    p.op("vector", lambda v: v.tensor_tensor(
        out=a_row[:], in0=a_row[:], in1=s1[:], op=ALU.subtract),
        reads=[ka, ks1], writes=[ka], tag=f"{tag}.var")
    csb, ckey = ctx.csb, ctx.ckey
    p.op("scalar", lambda s: s.activation(
        out=a_row[:], in_=a_row[:], func=AF.Ln, bias=csb[0:1, 130:131]),
        reads=[ka, ckey], writes=[ka], tag=f"{tag}.logv")
    p.op("scalar", lambda s: s.activation(
        out=a_row[:], in_=a_row[:], func=AF.Exp, scale=csb[0:1, 131:132]),
        reads=[ka, ckey], writes=[ka], tag=f"{tag}.rstd")


def emit_ln_apply(ctx, src, src_key, rows, rows_key, ones_row32,
                  pA, pB, bkA, bkB, outs, tag, or_key="consts"):
    p = ctx.p
    tok, tss, nts = ctx.tok, ctx.tss, ctx.nts
    t1 = ctx.shared("lnt1", (128, tss), F32)
    kt1 = "lnt1"
    for ts in range(nts):
        t0 = ts * tss
        p.op("tensor", lambda t, t0=t0: t.matmul(
            pA[:, 0:tss], ones_row32[0:1, :], rows["a"][:, t0:t0 + tss],
            start=True, stop=True),
            reads=[rows_key + ".a", or_key], writes=[bkA],
            tag=f"{tag}.bcA")
        p.op("tensor", lambda t, t0=t0: t.matmul(
            pB[:, 0:tss], ones_row32[0:1, :], rows["b"][:, t0:t0 + tss],
            start=True, stop=True),
            reads=[rows_key + ".b", or_key], writes=[bkB],
            tag=f"{tag}.bcB")
        for c in range(NCH):
            sl = src[:, c * tok + t0: c * tok + t0 + tss]
            p.op("vector", lambda v, sl=sl: v.tensor_tensor(
                out=t1[:], in0=sl, in1=pB[:, 0:tss], op=ALU.add),
                reads=[src_key, bkB], writes=[kt1], tag=f"{tag}.t1")
            for oi, o in enumerate(outs):
                osl = o["tile"][:, c * tok + t0: c * tok + t0 + tss]
                if o.get("gb") is None:
                    p.op("vector", lambda v, osl=osl: v.tensor_tensor(
                        out=osl, in0=t1[:], in1=pA[:, 0:tss], op=ALU.mult),
                        reads=[kt1, bkA], writes=[o["key"]],
                        tag=f"{tag}.ap{oi}")
                else:
                    g_col, b_col, gb_key = o["gb"]
                    tmp, ktmp = o["tmp"], o["tmpkey"]
                    p.op("vector", lambda v, tmp=tmp: v.tensor_tensor(
                        out=tmp[:], in0=t1[:], in1=pA[:, 0:tss], op=ALU.mult),
                        reads=[kt1, bkA], writes=[ktmp], tag=f"{tag}.apg{oi}")
                    p.op("vector", lambda v, osl=osl, c=c: v.tensor_scalar(
                        out=osl, in0=tmp[:], scalar1=g_col[:, c:c + 1],
                        scalar2=b_col[:, c:c + 1], op0=ALU.mult, op1=ALU.add),
                        reads=[ktmp, gb_key], writes=[o["key"]],
                        tag=f"{tag}.gb{oi}")



def build_launch1(tok):
    nc = bass.Bass("TRN2", target_bir_lowering=False)
    tch = tok // CH
    xin = nc.dram_tensor("xin", [D, tok], F32, kind="ExternalInput").ap()
    wq = nc.dram_tensor("wq", [D, D], F32, kind="ExternalInput").ap()
    wk = nc.dram_tensor("wk", [D, D], F32, kind="ExternalInput").ap()
    wv = nc.dram_tensor("wv", [D, D], F32, kind="ExternalInput").ap()
    ln1g = nc.dram_tensor("ln1g", [128, 8], F32, kind="ExternalInput").ap()
    ln1b = nc.dram_tensor("ln1b", [128, 8], F32, kind="ExternalInput").ap()
    consts = nc.dram_tensor("consts", [128, 136], F32, kind="ExternalInput").ap()
    o_abrows = nc.dram_tensor("abrows", [4, tok], F32, kind="ExternalOutput").ap()
    o_qfull = nc.dram_tensor("qout", [128, 8 * tok], BF16,
                             kind="ExternalOutput").ap()
    o_numden = nc.dram_tensor("numden", [tok, NHEAD * 65], F32,
                              kind="ExternalOutput").ap()
    o_stot = nc.dram_tensor("stot", [128, 8 * 65], F32,
                            kind="ExternalOutput").ap()
    o_dbg = nc.dram_tensor("dbg", [128, NHEAD * 65], F32,
                           kind="ExternalOutput").ap()
    o_dbg2 = nc.dram_tensor("dbg_mst", [128, NHEAD * CH], BF16,
                            kind="ExternalOutput").ap()
    o_dbg3 = nc.dram_tensor("dbg_vtm", [128, NHEAD * 65], BF16,
                            kind="ExternalOutput").ap()
    o_dbg4 = nc.dram_tensor("dbg_kfm", [64, NHEAD * CH], BF16,
                            kind="ExternalOutput").ap()
    o_dbg5 = nc.dram_tensor("dbg_ktm", [128, NHEAD * DH], BF16,
                            kind="ExternalOutput").ap()

    ctx = Ctx(nc, tok)
    p = ctx.p
    tss, nts = ctx.tss, ctx.nts

    with ctx.es:
        xt = ctx.sb("xt", (128, NCH * tok), F32)
        xab = ctx.sb("xab", (128, NCH * tok), BF16)
        qfull = ctx.sb("qfull", (128, 8 * tok), BF16)
        ones_col = ctx.sb("ones_col", (128, 1), BF16)
        mask = ctx.sb("mask", (128, CH), BF16)
        consts_sb = ctx.sb("consts_sb", (128, 136), F32)
        ctx.csb = consts_sb
        ctx.ckey = "consts"
        ctx.prealloc_ln()
        gtmp = ctx.shared("lngt", (128, tss), F32)
        g1 = ctx.sb("g1", (128, 8), F32)
        b1 = ctx.sb("b1", (128, 8), F32)
        s_run = ctx.sb("s_run", (128, 8 * 65), F32)
        s_pref = ctx.sb("s_pref", (128, 8 * 65), BF16)
        kfm = ctx.sb("kfm", (128, 8 * CH), BF16)
        ktm = ctx.sb("ktm", (128, NHEAD * DH), BF16)
        vtm = ctx.sb("vtm", (128, NHEAD * 65), BF16)
        mst = ctx.sb("mst", (128, NHEAD * CH), BF16)
        nst = ctx.sb("nst", (128, NHEAD * 65), F32)
        phit = ctx.sb("phit", (128, 512), F32)
        phie = ctx.sb("phie", (128, 512), F32)

        pA = ctx.ps("pA", (128, 512))
        pB = ctx.ps("pB", (128, 512))
        pC = ctx.ps("pC", (128, 512))
        pD = ctx.ps("pD", (128, 512))
        pE = ctx.ps("pE", (128, 512))
        pF = ctx.ps("pF", (128, 512))
        pG = ctx.ps("pG", (128, 512))
        pH = ctx.ps("pH", (128, 512))

        KX, KXAB, KQ = "xt", "xab", "qfull"
        KW, KWV = "wsb", "wvsb"
        KSRUN, KSPREF = "srun", "spref"
        KKFM, KKTM, KVTM, KMST, KNST = "kfm", "ktm", "vtm", "mst", "nst"
        KPH1, KPH2 = "phit", "phie"
        BA, BB, BC, BD, BE, BFb, BG, BH = ("b_A", "b_B", "b_C", "b_D", "b_E", "b_F", "b_G", "b_H")

        p.dma("sync", [(xt[:, c * tok:(c + 1) * tok],
                        xin[c * 128:(c + 1) * 128, :]) for c in range(NCH)],
              writes=[KX], tag="ldx")
        p.dma("sync", [(consts_sb[:], consts[:]),
                       (g1[:], ln1g[:]), (b1[:], ln1b[:])],
              writes=["consts", "g1b1"], tag="ldc")
        p.op("vector", lambda v: v.tensor_copy(
            out=ones_col[:], in_=consts_sb[:, 0:1]),
            reads=["consts"], writes=["ones_col"], tag="mkones")
        p.op("vector", lambda v: v.tensor_copy(
            out=mask[:], in_=consts_sb[:, 2:130]),
            reads=["consts"], writes=["mask"], tag="mkmask")
        ones_row32 = consts_sb[0:1, 2:130]

        with contextlib.ExitStack() as esL:
            ctx.alloc_es = esL
            rows0 = {"a": ctx.sb("r0a", (1, tok), F32),
                     "b": ctx.sb("r0b", (1, tok), F32)}
            rows1 = {"a": ctx.sb("r1a", (1, tok), F32),
                     "b": ctx.sb("r1b", (1, tok), F32)}
            emit_ln_stats(ctx, xt, KX, rows0, "r0", pA, pB, BA, BB, ones_col,
                          tag="ln0")
            emit_ln_apply(ctx, xt, KX, rows0, "r0", ones_row32, pA, pB, BA, BB,
                          [dict(tile=xt, key=KX)], tag="ln0ap")
            emit_ln_stats(ctx, xt, KX, rows1, "r1", pA, pB, BA, BB, ones_col,
                          tag="ln1")
            emit_ln_apply(ctx, xt, KX, rows1, "r1", ones_row32, pA, pB, BA, BB,
                          [dict(tile=xab, key=KXAB, gb=(g1, b1, "g1b1"),
                                tmp=gtmp, tmpkey="lngt")], tag="ln1ap")
            p.dma("sync", [(o_abrows[0:1, :], rows0["a"][:]),
                           (o_abrows[1:2, :], rows0["b"][:]),
                           (o_abrows[2:3, :], rows1["a"][:]),
                           (o_abrows[3:4, :], rows1["b"][:])],
                  reads=["r0.a", "r0.b", "r1.a", "r1.b"], tag="stab")
        ctx.alloc_es = None
        p.barrier()
        wsb = ctx.sb("wsb", (128, NCH * D), BF16)
        wvsb = ctx.sb("wvsb", (128, NCH * D), BF16)
        p.dma("gpsimd",
              [(wsb[:, c * D:(c + 1) * D], wq[c * 128:(c + 1) * 128, :])
               for c in range(NCH)], writes=[KW], tag="ldwq")

        def phi_to(bank, bkey, dsts, dkey, nsz, tag, full=False):
            p.op("vector", lambda v: v.tensor_scalar(
                out=phit[:, 0:nsz], in0=bank[:, 0:nsz], scalar1=0.0,
                scalar2=None, op0=ALU.min),
                reads=[bkey], writes=[KPH1], tag=f"{tag}.min")
            p.op("scalar", lambda s: s.activation(
                out=phie[:, 0:nsz], in_=phit[:, 0:nsz], func=AF.Exp),
                reads=[KPH1], writes=[KPH2], tag=f"{tag}.exp")
            if full:
                dst = dsts[0][0]
                p.op("vector", lambda v, dst=dst: v.scalar_tensor_tensor(
                    out=dst, in0=bank[:, 0:nsz], scalar=0.0,
                    in1=phie[:, 0:nsz], op0=ALU.max, op1=ALU.add),
                    reads=[bkey, KPH2], writes=[dkey], tag=f"{tag}.sttF")
                return
            for (dst, lo) in dsts:
                p.op("vector", lambda v, dst=dst, lo=lo: v.scalar_tensor_tensor(
                    out=dst, in0=bank[lo:lo + 64, 0:nsz], scalar=0.0,
                    in1=phie[lo:lo + 64, 0:nsz], op0=ALU.max, op1=ALU.add),
                    reads=[bkey, KPH2], writes=[dkey], tag=f"{tag}.stt{lo}")

        for m in range(NCH):
            for ts in range(nts):
                bank, bkey = ((pC, BC) if (m * nts + ts) % 2 == 0 else (pD, BD))
                t0 = ts * tss
                for k in range(NCH):
                    lhs = wsb[:, k * D + m * 128:(k * D + (m + 1) * 128)]
                    rhs = xab[:, k * tok + t0: k * tok + t0 + tss]
                    p.op("tensor", lambda t, lhs=lhs, rhs=rhs, k=k, bank=bank:
                         t.matmul(bank[:, 0:tss], lhs, rhs, start=(k == 0),
                                  stop=(k == NCH - 1)),
                         reads=[KW, KXAB], writes=[bkey], tag=f"qp{m}.{ts}")
                phi_to(bank, bkey,
                       [(qfull[:, m * tok + t0: m * tok + t0 + tss], 0)],
                       KQ, tss, f"qphi{m}.{ts}", full=True)
        p.dma("sync", [(o_qfull[:, :], qfull[:, :])], reads=[KQ], tag="stq")

        p.dma("gpsimd",
              [(wsb[:, c * D:(c + 1) * D], wk[c * 128:(c + 1) * 128, :])
               for c in range(NCH)], writes=[KW], tag="ldwk")
        p.dma("gpsimd",
              [(wvsb[:, c * D:(c + 1) * D], wv[c * 128:(c + 1) * 128, :])
               for c in range(NCH)], writes=[KWV], tag="ldwv")

        p.op("vector", lambda v: v.memset(s_run[:], 0.0), writes=[KSRUN],
             tag="s0")

        for c in range(tch):
            t0 = c * CH
            for m in range(NCH):
                bank, bkey = (pC, BC) if m % 2 == 0 else (pD, BD)
                for k in range(NCH):
                    lhs = wsb[:, k * D + m * 128: k * D + (m + 1) * 128]
                    rhs = xab[:, k * tok + t0: k * tok + t0 + CH]
                    p.op("tensor", lambda t, lhs=lhs, rhs=rhs, k=k, bank=bank:
                         t.matmul(bank[:, 0:CH], lhs, rhs, start=(k == 0),
                                  stop=(k == NCH - 1)),
                         reads=[KW, KXAB], writes=[bkey], tag=f"kp{c}.{m}")
                phi_to(bank, bkey, [(kfm[:, m * CH:(m + 1) * CH], 0)],
                       KKFM, CH, f"kphi{c}.{m}", full=True)
            for half in range(2):
                bank, bkey = (pE, BE) if half == 0 else (pF, BFb)
                for k in range(NCH):
                    lhs = xab[:, k * tok + t0: k * tok + t0 + CH]
                    rhs = wsb[:, k * D + half * 512: k * D + half * 512 + 512]
                    p.op("tensor", lambda t, lhs=lhs, rhs=rhs, k=k, bank=bank:
                         t.matmul(bank[:, 0:512], lhs, rhs, start=(k == 0),
                                  stop=(k == NCH - 1)),
                         reads=[KW, KXAB], writes=[bkey], tag=f"ktp{c}.{half}")
                p.op("vector", lambda v, bank=bank: v.tensor_scalar(
                    out=phit[:, 0:512], in0=bank[:, 0:512], scalar1=0.0,
                    scalar2=None, op0=ALU.min),
                    reads=[bkey], writes=[KPH1], tag=f"ktm{c}.{half}.min")
                p.op("scalar", lambda s: s.activation(
                    out=phie[:, 0:512], in_=phit[:, 0:512], func=AF.Exp),
                    reads=[KPH1], writes=[KPH2], tag=f"ktm{c}.{half}.exp")
                p.op("vector", lambda v, bank=bank, half=half:
                     v.scalar_tensor_tensor(
                         out=ktm[:, half * 512:(half + 1) * 512],
                         in0=bank[:, 0:512], scalar=0.0, in1=phie[:, 0:512],
                         op0=ALU.max, op1=ALU.add),
                     reads=[bkey, KPH2], writes=[KKTM], tag=f"ktm{c}.{half}.stt")
            for half in range(2):
                bank, bkey = (pE, BE) if half == 0 else (pF, BFb)
                for k in range(NCH):
                    lhs = xab[:, k * tok + t0: k * tok + t0 + CH]
                    rhs = wvsb[:, k * D + half * 512: k * D + half * 512 + 512]
                    p.op("tensor", lambda t, lhs=lhs, rhs=rhs, k=k, bank=bank:
                         t.matmul(bank[:, 0:512], lhs, rhs, start=(k == 0),
                                  stop=(k == NCH - 1)),
                         reads=[KWV, KXAB], writes=[bkey], tag=f"vp{c}.{half}")
                for hh in range(8):
                    h = half * 8 + hh
                    p.op("vector", lambda v, bank=bank, hh=hh, h=h:
                         v.tensor_copy(out=vtm[:, h * 65: h * 65 + 64],
                                       in_=bank[:, hh * 64:(hh + 1) * 64]),
                         reads=[bkey], writes=[KVTM], tag=f"vev{c}.{h}")
            for h in range(NHEAD):
                p.op("vector", lambda v, h=h: v.memset(
                    vtm[:, h * 65 + 64: h * 65 + 65], 1.0),
                    writes=[KVTM], tag=f"vone{c}.{h}")

            p.op("vector", lambda v: v.tensor_copy(out=s_pref[:], in_=s_run[:]),
                 reads=[KSRUN], writes=[KSPREF], tag=f"spref{c}")

            for h in range(NHEAD):
                hp, be = h // 2, 64 * (h % 2)
                sb_bank, sbk = (pC, BC) if h % 2 == 0 else (pD, BD)
                p.op("tensor", lambda t, hp=hp, be=be, sb_bank=sb_bank, t0=t0:
                     t.matmul(
                         sb_bank[:, 0:CH],
                         kfm[be:be + 64, hp * CH:(hp + 1) * CH],
                         qfull[be:be + 64, hp * tok + t0: hp * tok + t0 + CH],
                         start=True, stop=True),
                     reads=[KKFM, KQ], writes=[sbk], tag=f"sc{c}.{h}")
                p.op("vector", lambda v, h=h, sb_bank=sb_bank: v.tensor_tensor(
                    out=mst[:, h * CH:(h + 1) * CH], in0=sb_bank[:, 0:CH],
                    in1=mask[:], op=ALU.mult),
                    reads=[sbk, "mask"], writes=[KMST], tag=f"msk{c}.{h}")
                nb, nbk = (pE, BE) if h % 2 == 0 else (pF, BFb)
                ib, ibk = (pG, BG) if h % 2 == 0 else (pH, BH)
                p.op("tensor", lambda t, h=h, nb=nb: t.matmul(
                    nb[:, 0:65], mst[:, h * CH:(h + 1) * CH],
                    vtm[:, h * 65:(h + 1) * 65], start=True, stop=True),
                    reads=[KMST, KVTM], writes=[nbk], tag=f"numi{c}.{h}")
                p.op("tensor", lambda t, hp=hp, be=be, ib=ib, t0=t0: t.matmul(
                    ib[:, 0:65],
                    qfull[be:be + 64, hp * tok + t0: hp * tok + t0 + CH],
                    s_pref[be:be + 64, hp * 65:(hp + 1) * 65],
                    start=True, stop=True),
                    reads=[KQ, KSPREF], writes=[ibk], tag=f"numl{c}.{h}")
                p.op("vector", lambda v, h=h, nb=nb: v.tensor_copy(
                    out=nst[:, h * 65:(h + 1) * 65], in_=nb[:, 0:65]),
                    reads=[nbk], writes=[KNST], tag=f"numev{c}.{h}")
                p.op("vector", lambda v, h=h, ib=ib: v.tensor_tensor(
                    out=nst[:, h * 65:(h + 1) * 65],
                    in0=nst[:, h * 65:(h + 1) * 65],
                    in1=ib[:, 0:65], op=ALU.add),
                    reads=[ibk, KNST], writes=[KNST], tag=f"numad{c}.{h}")
                stb, stk = (pA, BA) if h % 2 == 0 else (pB, BB)
                p.op("tensor", lambda t, h=h, stb=stb: t.matmul(
                    stb[0:64, 0:65], ktm[:, h * DH:(h + 1) * DH],
                    vtm[:, h * 65:(h + 1) * 65], start=True, stop=True),
                    reads=[KKTM, KVTM], writes=[stk], tag=f"sup{c}.{h}")
                p.op("vector", lambda v, hp=hp, be=be, stb=stb: v.tensor_tensor(
                    out=s_run[be:be + 64, hp * 65:(hp + 1) * 65],
                    in0=s_run[be:be + 64, hp * 65:(hp + 1) * 65],
                    in1=stb[0:64, 0:65], op=ALU.add),
                    reads=[stk, KSRUN], writes=[KSRUN], tag=f"sacc{c}.{h}")
            p.dma("sync", [(o_numden[t0:t0 + CH, :], nst[:, :])],
                  reads=[KNST], tag=f"stnd{c}")
            if c == 0:
                p.dma("sync", [(o_dbg[:, :], nst[:, :]),
                               (o_dbg2[:, :], mst[:, :]),
                               (o_dbg3[:, :], vtm[:, :]),
                               (o_dbg4[:, :], kfm[:, :]),
                               (o_dbg5[:, :], ktm[:, :])],
                      reads=[KNST, KMST, KVTM, KKFM, KKTM], tag="stdbg")

        p.dma("sync", [(o_stot[:], s_run[:])], reads=[KSRUN], tag="ststot")

        p.emit()
    return nc




def build_launch2(tok):
    nc = bass.Bass("TRN2", target_bir_lowering=False)
    tch = tok // CH
    xin = nc.dram_tensor("xin", [D, tok], F32, kind="ExternalInput").ap()
    abrows = nc.dram_tensor("abrows", [4, tok], F32, kind="ExternalInput").ap()
    qin = nc.dram_tensor("qin", [128, 8 * tok], BF16, kind="ExternalInput").ap()
    ndin = nc.dram_tensor("ndin", [tok, NHEAD * 65], F32, kind="ExternalInput").ap()
    sall = nc.dram_tensor("sall", [128, NCORES * 8 * 65], F32,
                          kind="ExternalInput").ap()
    mcol = nc.dram_tensor("mcol", [128, NCORES], F32, kind="ExternalInput").ap()
    wo = nc.dram_tensor("wo", [D, D], F32, kind="ExternalInput").ap()
    wf1 = nc.dram_tensor("wf1", [D, 4 * D], F32, kind="ExternalInput").ap()
    wf2 = nc.dram_tensor("wf2", [4 * D, D], F32, kind="ExternalInput").ap()
    ln2g = nc.dram_tensor("ln2g", [128, 8], F32, kind="ExternalInput").ap()
    ln2b = nc.dram_tensor("ln2b", [128, 8], F32, kind="ExternalInput").ap()
    cbt = nc.dram_tensor("cbt", [D, D], F32, kind="ExternalInput").ap()
    scb = nc.dram_tensor("scb", [128, 8], F32, kind="ExternalInput").ap()
    fbias = nc.dram_tensor("fbias", [128, 8], F32, kind="ExternalInput").ap()
    ident = nc.dram_tensor("ident", [128, 128], F32, kind="ExternalInput").ap()
    consts = nc.dram_tensor("consts", [128, 136], F32, kind="ExternalInput").ap()
    o_x2 = nc.dram_tensor("x2", [D, tok], F32, kind="ExternalOutput").ap()

    ctx = Ctx(nc, tok)
    p = ctx.p
    tss, nts = ctx.tss, ctx.nts

    with ctx.es:
        consts_sb = ctx.sb("consts_sb", (128, 136), F32)
        ctx.csb = consts_sb
        ctx.ckey = "consts"
        ones_col = ctx.sb("ones_col", (128, 1), BF16)
        identb = ctx.sb("identb", (128, 128), BF16)
        xh = ctx.sb("xh", (128, NCH * tok), F32)
        ctx.prealloc_ln()
        gtmp = ctx.shared("lngt", (128, tss), F32)
        spre = ctx.sb("spre", (128, 8 * 65), BF16)
        mc = ctx.sb("mc", (128, NCORES), F32)
        g2 = ctx.sb("g2", (128, 8), F32)
        b2 = ctx.sb("b2", (128, 8), F32)
        t1b = ctx.sb("t1b", (128, tss), F32)
        xs = ctx.sb("xs", (128, tss), F32)

        pS1 = ctx.ps("pS1", (128, 512))
        pS2 = ctx.ps("pS2", (128, 512))
        pT1 = ctx.ps("pT1", (128, 1024), BF16)
        pT2 = ctx.ps("pT2", (128, 1024), BF16)
        pW1 = ctx.ps("pW1", (128, 512))
        pW2 = ctx.ps("pW2", (128, 512))

        KXH = "xh"
        B1, B2k, BT1, BT2, BW1, BW2 = "b_S1", "b_S2", "b_T1", "b_T2", "b_W1", "b_W2"

        p.dma("sync", [(consts_sb[:], consts[:])], writes=["consts"], tag="ldc")
        p.op("vector", lambda v: v.tensor_copy(
            out=ones_col[:], in_=consts_sb[:, 0:1]),
            reads=["consts"], writes=["ones_col"], tag="mkones")
        ones_row32 = consts_sb[0:1, 2:130]
        p.dma("gpsimd", [(identb[:], ident[:])], writes=["identb"], tag="ldid")
        p.dma("sync", [(xh[:, c * tok:(c + 1) * tok],
                        xin[c * 128:(c + 1) * 128, :]) for c in range(NCH)],
              writes=[KXH], tag="ldx")
        p.dma("sync", [(mc[:], mcol[:]), (g2[:], ln2g[:]), (b2[:], ln2b[:])],
              writes=["mc", "g2b2"], tag="ldr")

        with contextlib.ExitStack() as es2:
            ctx.alloc_es = es2
            sall_sb = ctx.sb("sall_sb", (128, NCORES * 8 * 65), F32)
            sacc = ctx.sb("sacc", (128, 8 * 65), F32)
            p.dma("sync", [(sall_sb[:], sall[:])], writes=["sall"], tag="ldsall")
            p.op("vector", lambda v: v.memset(sacc[:], 0.0), writes=["sacc"],
                 tag="sacc0")
            W = 8 * 65
            for r in range(NCORES):
                p.op("vector", lambda v, r=r: v.scalar_tensor_tensor(
                    out=sacc[:], in0=sall_sb[:, r * W:(r + 1) * W],
                    scalar=mc[:, r:r + 1], in1=sacc[:],
                    op0=ALU.mult, op1=ALU.add),
                    reads=["sall", "mc", "sacc"], writes=["sacc"], tag=f"sm{r}")
            p.op("vector", lambda v: v.tensor_copy(out=spre[:], in_=sacc[:]),
                 reads=["sacc"], writes=["spre"], tag="spreb")
        ctx.alloc_es = None
        p.barrier()

        with contextlib.ExitStack() as esB:
            ctx.alloc_es = esB
            rows = {"a": ctx.sb("rowa", (1, tok), F32),
                    "b": ctx.sb("rowb", (1, tok), F32)}
            p.dma("sync", [(rows["a"][:], abrows[0:1, :]),
                           (rows["b"][:], abrows[1:2, :])],
                  writes=["r.a", "r.b"], tag="ldab")
            emit_ln_apply(ctx, xh, KXH, rows, "r", ones_row32, pS1, pS2,
                          B1, B2k, [dict(tile=xh, key=KXH)], tag="l2ln0")
        ctx.alloc_es = None
        p.barrier()

        with contextlib.ExitStack() as esC:
            ctx.alloc_es = esC
            wosb = ctx.sb("wosb", (128, NCH * D), BF16)
            qch = ctx.sb("qch", (128, 8 * CH), BF16)
            ndch = ctx.sb("ndch", (128, NHEAD * 65), F32)
            nsum = ctx.sb("nsum", (128, 65), F32)
            dclip = ctx.sb("dclip", (128, 1), F32)
            rden = ctx.sb("rden", (128, 1), F32)
            attm = ctx.sb("attm", (128, NCH * CH), BF16)
            attf = ctx.sb("attf", (128, NCH * CH), BF16)
            p.dma("gpsimd",
                  [(wosb[:, c * D:(c + 1) * D], wo[c * 128:(c + 1) * 128, :])
                   for c in range(NCH)], writes=["wo"], tag="ldwo")
            for c in range(tch):
                t0 = c * CH
                p.dma("sync", [(qch[:], qin[:, :].rearrange(
                    "p (h t) -> p h t", h=8)[:, :, t0:t0 + CH])],
                      writes=["qch"], tag=f"ldq{c}")
                p.dma("sync", [(ndch[:], ndin[t0:t0 + CH, :])],
                      writes=["ndch"], tag=f"ldnd{c}")
                for h in range(NHEAD):
                    hp, be = h // 2, 64 * (h % 2)
                    bank, bk = (pS1, B1) if h % 2 == 0 else (pS2, B2k)
                    p.op("tensor", lambda t, hp=hp, be=be, bank=bank: t.matmul(
                        bank[:, 0:65], qch[be:be + 64, hp * CH:(hp + 1) * CH],
                        spre[be:be + 64, hp * 65:(hp + 1) * 65],
                        start=True, stop=True),
                        reads=["qch", "spre"], writes=[bk], tag=f"int{c}.{h}")
                    p.op("vector", lambda v, h=h, bank=bank: v.tensor_tensor(
                        out=nsum[:], in0=ndch[:, h * 65:(h + 1) * 65],
                        in1=bank[:, 0:65], op=ALU.add),
                        reads=["ndch", bk], writes=["nsum"], tag=f"nsum{c}.{h}")
                    p.op("vector", lambda v: v.tensor_scalar(
                        out=dclip[:], in0=nsum[:, 64:65], scalar1=EPS_DEN,
                        scalar2=None, op0=ALU.max),
                        reads=["nsum"], writes=["dclip"], tag=f"dcl{c}.{h}")
                    p.op("vector", lambda v: v.reciprocal(out=rden[:],
                                                          in_=dclip[:]),
                         reads=["dclip"], writes=["rden"], tag=f"rd{c}.{h}")
                    p.op("vector", lambda v, h=h: v.tensor_scalar(
                        out=attm[:, h * 64:(h + 1) * 64], in0=nsum[:, 0:64],
                        scalar1=rden[:, 0:1], scalar2=None, op0=ALU.mult),
                        reads=["nsum", "rden"], writes=["attm"],
                        tag=f"att{c}.{h}")
                for j in range(NCH):
                    bank, bk = (pT1, BT1) if j % 2 == 0 else (pT2, BT2)
                    p.op("tensor", lambda t, j=j, bank=bank: t.transpose(
                        bank[:, 0:CH], attm[:, j * CH:(j + 1) * CH],
                        identb[:]),
                        reads=["attm", "identb"], writes=[bk], tag=f"tr{c}.{j}")
                    p.op("vector", lambda v, j=j, bank=bank: v.tensor_copy(
                        out=attf[:, j * CH:(j + 1) * CH], in_=bank[:, 0:CH]),
                        reads=[bk], writes=["attf"], tag=f"trev{c}.{j}")
                for o in range(NCH):
                    bank, bk = (pW1, BW1) if o % 2 == 0 else (pW2, BW2)
                    for j in range(NCH):
                        p.op("tensor", lambda t, o=o, j=j, bank=bank: t.matmul(
                            bank[:, 0:CH],
                            wosb[:, j * D + o * 128: j * D + (o + 1) * 128],
                            attf[:, j * CH:(j + 1) * CH],
                            start=(j == 0), stop=(j == NCH - 1)),
                            reads=["wo", "attf"], writes=[bk],
                            tag=f"wo{c}.{o}.{j}")
                    sl = xh[:, o * tok + t0: o * tok + t0 + CH]
                    p.op("vector", lambda v, sl=sl, bank=bank: v.tensor_tensor(
                        out=sl, in0=sl, in1=bank[:, 0:CH], op=ALU.add),
                        reads=[bk, KXH], writes=[KXH], tag=f"xp{c}.{o}")
        ctx.alloc_es = None
        p.barrier()

        with contextlib.ExitStack() as esD:
            ctx.alloc_es = esD
            h2b = ctx.sb("h2b", (128, NCH * tok), BF16)
            with contextlib.ExitStack() as esD1:
                ctx.alloc_es = esD1
                rows1 = {"a": ctx.sb("r1a", (1, tok), F32),
                         "b": ctx.sb("r1b", (1, tok), F32)}
                emit_ln_stats(ctx, xh, KXH, rows1, "r1", pS1, pS2, B1, B2k,
                              ones_col, tag="ln2")
                emit_ln_apply(ctx, xh, KXH, rows1, "r1", ones_row32, pS1, pS2,
                              B1, B2k,
                              [dict(tile=h2b, key="h2b", gb=(g2, b2, "g2b2"),
                                    tmp=gtmp, tmpkey="lngt")], tag="ln2ap")
            ctx.alloc_es = esD
            p.barrier()
            wbigA = ctx.sb("wbigA", (128, NCH * 1024), BF16)
            wbigB = ctx.sb("wbigB", (128, 8 * 1024), BF16)
            gsb = ctx.sb("gsb", (128, 8 * tok), BF16)
            for q in range(4):
                p.dma("gpsimd",
                      [(wbigA[:, k * 1024:(k + 1) * 1024],
                        wf1[k * 128:(k + 1) * 128,
                            q * 1024:(q + 1) * 1024]) for k in range(NCH)],
                      writes=["wbigA"], tag=f"ldf1.{q}")
                for ts in range(nts):
                    t0 = ts * tss
                    for mm in range(8):
                        bank, bk = (pS1, B1) if mm % 2 == 0 else (pS2, B2k)
                        for k in range(NCH):
                            p.op("tensor",
                                 lambda t, mm=mm, k=k, bank=bank, t0=t0:
                                 t.matmul(bank[:, 0:tss],
                                          wbigA[:, k * 1024 + mm * 128:
                                                k * 1024 + (mm + 1) * 128],
                                          h2b[:, k * tok + t0:
                                              k * tok + t0 + tss],
                                          start=(k == 0), stop=(k == NCH - 1)),
                                 reads=["wbigA", "h2b"], writes=[bk],
                                 tag=f"f1.{q}.{ts}.{mm}")
                        p.op("scalar", lambda s, mm=mm, bank=bank, t0=t0:
                             s.activation(
                                 out=gsb[:, mm * tok + t0: mm * tok + t0 + tss],
                                 in_=bank[:, 0:tss], func=AF.Gelu),
                             reads=[bk], writes=["gsb"], tag=f"gel.{q}.{ts}.{mm}")
                p.dma("gpsimd",
                      [(wbigB[:, kk * 1024:(kk + 1) * 1024],
                        wf2[(q * 8 + kk) * 128:(q * 8 + kk + 1) * 128, :])
                       for kk in range(8)],
                      writes=["wbigB"], tag=f"ldf2.{q}")
                for ts in range(nts):
                    t0 = ts * tss
                    for o in range(NCH):
                        bank, bk = (pW1, BW1) if o % 2 == 0 else (pW2, BW2)
                        for kk in range(8):
                            p.op("tensor",
                                 lambda t, o=o, kk=kk, bank=bank, t0=t0:
                                 t.matmul(
                                     bank[:, 0:tss],
                                     wbigB[:, kk * 1024 + o * 128:
                                           kk * 1024 + (o + 1) * 128],
                                     gsb[:, kk * tok + t0: kk * tok + t0 + tss],
                                     start=(kk == 0), stop=(kk == 7)),
                                 reads=["wbigB", "gsb"], writes=[bk],
                                 tag=f"f2.{q}.{ts}.{o}")
                        sl = xh[:, o * tok + t0: o * tok + t0 + tss]
                        p.op("vector", lambda v, sl=sl, bank=bank:
                             v.tensor_tensor(
                                 out=sl, in0=sl, in1=bank[:, 0:tss],
                                 op=ALU.add),
                             reads=[bk, KXH], writes=[KXH],
                             tag=f"xf.{q}.{ts}.{o}")
        ctx.alloc_es = None
        p.barrier()

        with contextlib.ExitStack() as esF:
            ctx.alloc_es = esF
            rows2 = {"a": ctx.sb("r2a", (1, tok), F32),
                     "b": ctx.sb("r2b", (1, tok), F32)}
            emit_ln_stats(ctx, xh, KXH, rows2, "r2", pS1, pS2, B1, B2k,
                          ones_col, tag="pln0")
            for ts in range(nts):
                t0 = ts * tss
                p.op("tensor", lambda t, t0=t0: t.matmul(
                    pS1[:, 0:tss], ones_row32[0:1, :],
                    rows2["a"][:, t0:t0 + tss], start=True, stop=True),
                    reads=["r2.a", "consts"], writes=[B1], tag=f"pbA{ts}")
                p.op("tensor", lambda t, t0=t0: t.matmul(
                    pS2[:, 0:tss], ones_row32[0:1, :],
                    rows2["b"][:, t0:t0 + tss], start=True, stop=True),
                    reads=["r2.b", "consts"], writes=[B2k], tag=f"pbB{ts}")
                for c in range(NCH):
                    sl = xh[:, c * tok + t0: c * tok + t0 + tss]
                    p.dma("sync", [(xs[:, 0:tss],
                                    xin[c * 128:(c + 1) * 128, t0:t0 + tss])],
                          writes=["xs"], tag=f"ldxs{ts}.{c}")
                    p.op("vector", lambda v, sl=sl: v.tensor_tensor(
                        out=t1b[:, 0:tss], in0=sl, in1=pS2[:, 0:tss],
                        op=ALU.add),
                        reads=[KXH, B2k], writes=["t1b"], tag=f"pt1.{ts}.{c}")
                    p.op("vector", lambda v: v.tensor_tensor(
                        out=t1b[:, 0:tss], in0=t1b[:, 0:tss],
                        in1=pS1[:, 0:tss], op=ALU.mult),
                        reads=["t1b", B1], writes=["t1b"], tag=f"pt2.{ts}.{c}")
                    p.op("vector", lambda v, sl=sl: v.tensor_tensor(
                        out=sl, in0=t1b[:, 0:tss], in1=xs[:, 0:tss],
                        op=ALU.add),
                        reads=["t1b", "xs"], writes=[KXH], tag=f"pt3.{ts}.{c}")
        ctx.alloc_es = None
        p.barrier()

        zzb = ctx.sb("zzb", (128, NCH * tok), F32)
        with contextlib.ExitStack() as esG:
            ctx.alloc_es = esG
            xfb = ctx.sb("xfb", (128, NCH * tok), BF16)
            with contextlib.ExitStack() as esG1:
                ctx.alloc_es = esG1
                rows3 = {"a": ctx.sb("r3a", (1, tok), F32),
                         "b": ctx.sb("r3b", (1, tok), F32)}
                emit_ln_stats(ctx, xh, KXH, rows3, "r3", pS1, pS2, B1, B2k,
                              ones_col, tag="fln")
                emit_ln_apply(ctx, xh, KXH, rows3, "r3", ones_row32, pS1, pS2,
                              B1, B2k, [dict(tile=xfb, key="xfb")],
                              tag="flnap")
            ctx.alloc_es = esG
            p.barrier()
            cbt_sb = ctx.sb("cbt_sb", (128, NCH * D), BF16)
            scb_sb = ctx.sb("scb_sb", (128, 8), F32)
            fb_sb = ctx.sb("fb_sb", (128, 8), F32)
            p.dma("gpsimd",
                  [(cbt_sb[:, c * D:(c + 1) * D],
                    cbt[c * 128:(c + 1) * 128, :]) for c in range(NCH)],
                  writes=["cbt"], tag="ldcbt")
            p.dma("sync", [(scb_sb[:], scb[:]), (fb_sb[:], fbias[:])],
                  writes=["scbf"], tag="ldscb")
            p.op("vector", lambda v: v.tensor_scalar(
                out=scb_sb[:], in0=scb_sb[:], scalar1=1.0 / 127.0,
                scalar2=None, op0=ALU.mult),
                reads=["scbf"], writes=["scbf"], tag="scbdiv")
            for ts in range(nts):
                t0 = ts * tss
                for m in range(NCH):
                    bank, bk = (pW1, BW1) if m % 2 == 0 else (pW2, BW2)
                    for k in range(NCH):
                        p.op("tensor", lambda t, m=m, k=k, bank=bank, t0=t0:
                             t.matmul(bank[:, 0:tss],
                                      cbt_sb[:, k * D + m * 128:
                                             k * D + (m + 1) * 128],
                                      xfb[:, k * tok + t0: k * tok + t0 + tss],
                                      start=(k == 0), stop=(k == NCH - 1)),
                             reads=["cbt", "xfb"], writes=[bk],
                             tag=f"cb.{ts}.{m}")
                    p.op("vector", lambda v, m=m, bank=bank, t0=t0:
                         v.tensor_scalar(
                             out=zzb[:, m * tok + t0: m * tok + t0 + tss],
                             in0=bank[:, 0:tss], scalar1=scb_sb[:, m:m + 1],
                             scalar2=fb_sb[:, m:m + 1],
                             op0=ALU.mult, op1=ALU.add),
                         reads=[bk, "scbf"], writes=["zzb"],
                         tag=f"cbe.{ts}.{m}")
        ctx.alloc_es = None
        p.barrier()
        with contextlib.ExitStack() as esH:
            ctx.alloc_es = esH
            rows4 = {"a": ctx.sb("r4a", (1, tok), F32),
                     "b": ctx.sb("r4b", (1, tok), F32)}
            emit_ln_stats(ctx, zzb, "zzb", rows4, "r4", pS1, pS2, B1, B2k,
                          ones_col, tag="fln2")
            for ts in range(nts):
                t0 = ts * tss
                p.op("tensor", lambda t, t0=t0: t.matmul(
                    pS1[:, 0:tss], ones_row32[0:1, :],
                    rows4["a"][:, t0:t0 + tss], start=True, stop=True),
                    reads=["r4.a", "consts"], writes=[B1], tag=f"fbA{ts}")
                p.op("tensor", lambda t, t0=t0: t.matmul(
                    pS2[:, 0:tss], ones_row32[0:1, :],
                    rows4["b"][:, t0:t0 + tss], start=True, stop=True),
                    reads=["r4.b", "consts"], writes=[B2k], tag=f"fbB{ts}")
                for c in range(NCH):
                    zl = zzb[:, c * tok + t0: c * tok + t0 + tss]
                    sl = xh[:, c * tok + t0: c * tok + t0 + tss]
                    p.op("vector", lambda v, zl=zl: v.tensor_tensor(
                        out=t1b[:, 0:tss], in0=zl, in1=pS2[:, 0:tss],
                        op=ALU.add),
                        reads=["zzb", B2k], writes=["t1b"], tag=f"ft1.{ts}.{c}")
                    p.op("vector", lambda v: v.tensor_tensor(
                        out=t1b[:, 0:tss], in0=t1b[:, 0:tss],
                        in1=pS1[:, 0:tss], op=ALU.mult),
                        reads=["t1b", B1], writes=["t1b"], tag=f"ft2.{ts}.{c}")
                    p.op("vector", lambda v, sl=sl: v.tensor_tensor(
                        out=sl, in0=sl, in1=t1b[:, 0:tss], op=ALU.add),
                        reads=["t1b", KXH], writes=[KXH], tag=f"ft3.{ts}.{c}")
        ctx.alloc_es = None
        p.dma("sync", [(o_x2[c * 128:(c + 1) * 128, :],
                        xh[:, c * tok:(c + 1) * tok]) for c in range(NCH)],
              reads=[KXH], tag="stx2")

        p.emit()
    return nc



def build_launch3(tok):
    nc = bass.Bass("TRN2", target_bir_lowering=False)
    xin = nc.dram_tensor("xin", [D, tok], F32, kind="ExternalInput").ap()
    wbt = nc.dram_tensor("wbt", [D, 16], F32, kind="ExternalInput").ap()
    wct = nc.dram_tensor("wct", [D, 16], F32, kind="ExternalInput").ap()
    wdt = nc.dram_tensor("wdt", [D, D], F32, kind="ExternalInput").ap()
    alog = nc.dram_tensor("alog", [1, 16], F32, kind="ExternalInput").ap()
    dvec = nc.dram_tensor("dvec", [128, 8], F32, kind="ExternalInput").ap()
    sel16 = nc.dram_tensor("sel16", [16, 16 * 128], F32, kind="ExternalInput").ap()
    ident = nc.dram_tensor("ident", [128, 128], F32, kind="ExternalInput").ap()
    consts = nc.dram_tensor("consts", [128, 136], F32, kind="ExternalInput").ap()
    o_y0 = nc.dram_tensor("y0", [D, tok], BF16, kind="ExternalOutput").ap()
    o_d = nc.dram_tensor("dout", [D, tok], BF16, kind="ExternalOutput").ap()
    o_c = nc.dram_tensor("crow", [16, tok], BF16, kind="ExternalOutput").ap()
    o_acc = nc.dram_tensor("acc", [128, 128], F32, kind="ExternalOutput").ap()
    o_dec = nc.dram_tensor("dec", [128, 128], F32, kind="ExternalOutput").ap()

    ctx = Ctx(nc, tok)
    p = ctx.p
    HW = tok // 2
    hss = min(512, HW)
    hts = HW // hss

    with ctx.es:
        ctx.scoped_shared = True
        consts_sb = ctx.sb("consts_sb", (128, 136), F32)
        ctx.csb = consts_sb
        ctx.ckey = "consts"
        ones_col = ctx.sb("ones_col", (128, 1), BF16)
        identb = ctx.sb("identb", (128, 128), BF16)
        bsb = ctx.sb("bsb", (16, tok), BF16)
        csbB = ctx.sb("csbB", (16, tok), BF16)
        sel = ctx.sb("sel", (16, 16 * 128), BF16)
        asb = ctx.sb("asb", (128, 16), F32)
        dv = ctx.sb("dv", (128, 8), F32)
        acc_sb = ctx.sb("acc_sb", (128, 128), F32)
        dec_sb = ctx.sb("dec_sb", (128, 128), F32)
        dtp = ctx.sb("dtp", (128, 16), F32)
        etile = ctx.sb("etile", (128, 16), F32)
        sp2 = ctx.sb("sp2", (128, 512), F32)
        sp1b = ctx.sb("sp1b", (128, 512), BF16)
        zcol = ctx.sb("zcol", (128, 1), F32)
        arow_sb = ctx.sb("arow_sb", (1, 16), F32)
        wsmall = ctx.sb("wsmall", (128, NCH * 32), BF16)
        hmid = ctx.sb("hmid", (128, 128), F32)
        xhb = ctx.sb("xhb", (128, NCH * tok), BF16)

        pS1 = ctx.ps("pS1", (128, 512))
        pS2 = ctx.ps("pS2", (128, 512))
        pBC = ctx.ps("pBC", (128, 512))
        pB40 = ctx.ps("pB40", (128, 512))
        pY = [ctx.ps(f"pY{i}", (128, 512)) for i in range(4)]
        KY = [f"b_Y{i}" for i in range(4)]
        B1, B2k = "b_S1", "b_S2"
        bcastB = [pBC, pB40, pY[2], pY[3]]
        bcastK = ["b_BC", "b_B40", KY[2], KY[3]]

        p.dma("sync", [(consts_sb[:], consts[:]), (dv[:], dvec[:]),
                       (arow_sb[:], alog[:])],
              writes=["consts", "dv", "arow"], tag="ldc")
        p.dma("gpsimd", [(sel[:], sel16[:]), (identb[:], ident[:])],
              writes=["sel", "identb"], tag="ldsel")
        p.op("vector", lambda v: v.tensor_copy(out=ones_col[:],
                                               in_=consts_sb[:, 0:1]),
             reads=["consts"], writes=["ones_col"], tag="mkones")
        p.op("vector", lambda v: v.memset(zcol[:], 0.0), writes=["zcol"],
             tag="zc")
        ones_row32 = consts_sb[0:1, 2:130]

        with contextlib.ExitStack() as esA:
            ctx.alloc_es = esA
            xh = ctx.sb("xh", (128, NCH * tok), F32)
            KXH = "xh"
            p.dma("sync", [(xh[:, c * tok:(c + 1) * tok],
                            xin[c * 128:(c + 1) * 128, :]) for c in range(NCH)],
                  writes=[KXH], tag="ldx")
            rows = {"a": ctx.sb("ra", (1, tok), F32),
                    "b": ctx.sb("rb", (1, tok), F32)}
            emit_ln_stats(ctx, xh, KXH, rows, "r", pS1, pS2, B1, B2k, ones_col,
                          tag="l3ln")
            emit_ln_apply(ctx, xh, KXH, rows, "r", ones_row32, pS1, pS2,
                          B1, B2k, [dict(tile=xhb, key="xhb")], tag="l3lnap")
        ctx.alloc_es = None
        ctx._shared = {}
        p.barrier()

        p.op("tensor", lambda t: t.matmul(
            pS1[:, 0:16], ones_row32[0:1, :], arow_sb[:],
            start=True, stop=True),
            reads=["arow", "consts"], writes=[B1], tag="abc")
        p.op("scalar", lambda s: s.activation(
            out=asb[:], in_=pS1[:, 0:16], func=AF.Exp),
            reads=[B1], writes=["asb"], tag="aexp")
        p.op("vector", lambda v: v.tensor_scalar(
            out=asb[:], in0=asb[:], scalar1=-1.0, scalar2=None,
            op0=ALU.mult), reads=["asb"], writes=["asb"], tag="aneg")

        p.dma("gpsimd",
              [(wsmall[:, c * 32: c * 32 + 16],
                wbt[c * 128:(c + 1) * 128, :]) for c in range(NCH)] +
              [(wsmall[:, c * 32 + 16: c * 32 + 32],
                wct[c * 128:(c + 1) * 128, :]) for c in range(NCH)],
              writes=["wsmall"], tag="ldwbc")
        nts4 = tok // 512 if tok >= 512 else 1
        tss4 = min(512, tok)
        for which, dst, kdst in ((0, bsb, "bsb"), (1, csbB, "csbB")):
            for ts in range(nts4):
                t0 = ts * tss4
                for k in range(NCH):
                    p.op("tensor", lambda t, which=which, k=k, t0=t0:
                         t.matmul(pBC[0:16, 0:tss4],
                                  wsmall[:, k * 32 + which * 16:
                                         k * 32 + which * 16 + 16],
                                  xhb[:, k * tok + t0: k * tok + t0 + tss4],
                                  start=(k == 0), stop=(k == NCH - 1)),
                         reads=["wsmall", "xhb"], writes=["b_BC"],
                         tag=f"bc{which}.{ts}.{k}")
                p.op("vector", lambda v, dst=dst, t0=t0: v.tensor_copy(
                    out=dst[:, t0:t0 + tss4], in_=pBC[0:16, 0:tss4]),
                    reads=["b_BC"], writes=[kdst], tag=f"bce{which}.{ts}")
        p.dma("gpsimd", [(o_c[:], csbB[:])], reads=["csbB"], tag="stc")

        wdsbB = [ctx.sb(f"wdsb{i}", (128, NCH * 128), BF16) for i in range(2)]
        dall = ctx.sb("dall", (128, NCH * HW), F32)
        dxall = ctx.sb("dxall", (128, NCH * HW), F32)
        BbS = ctx.sb("BbS", (128, 16 * HW), BF16)
        CbS = ctx.sb("CbS", (128, 16 * HW), BF16)
        daB = [ctx.sb(f"da{i}", (128, HW), F32) for i in range(2)]
        uB = [ctx.sb(f"u{i}", (128, HW), BF16) for i in range(2)]
        hhB = [ctx.sb(f"hh{i}", (128, HW), F32) for i in range(2)]
        hbxB = [ctx.sb(f"hbx{i}", (128, HW), BF16) for i in range(2)]

        for H in range(2):
            h0 = H * HW
            for hc in range(NCH):
                wdsb, kwd = wdsbB[hc % 2], f"wdsb{hc % 2}"
                KDA, KDX = f"dal{hc}", f"dxl{hc}"
                p.dma("gpsimd",
                      [(wdsb[:, k * 128:(k + 1) * 128],
                        wdt[k * 128:(k + 1) * 128,
                            hc * 128:(hc + 1) * 128]) for k in range(NCH)],
                      writes=[kwd], tag=f"ldwd{H}.{hc}")
                for sl in range(hts):
                    s0 = sl * hss
                    bank, bk = (pS1, B1) if sl % 2 == 0 else (pS2, B2k)
                    for k in range(NCH):
                        p.op("tensor", lambda t, k=k, s0=s0, bank=bank,
                             wdsb=wdsb, h0=h0: t.matmul(
                                 bank[:, 0:hss],
                                 wdsb[:, k * 128:(k + 1) * 128],
                                 xhb[:, k * tok + h0 + s0:
                                     k * tok + h0 + s0 + hss],
                                 start=(k == 0), stop=(k == NCH - 1)),
                             reads=[kwd, "xhb"], writes=[bk],
                             tag=f"dp{H}.{hc}.{sl}.{k}")
                    p.op("scalar", lambda s, bank=bank: s.activation(
                        out=sp2[:, 0:hss], in_=bank[:, 0:hss], func=AF.Abs),
                        reads=[bk], writes=["sp2"], tag=f"spa{H}.{hc}.{sl}")
                    p.op("scalar", lambda s: s.activation(
                        out=sp2[:, 0:hss], in_=sp2[:, 0:hss], func=AF.Exp,
                        scale=consts_sb[:, 133:134]),
                        reads=["sp2", "consts"], writes=["sp2"],
                        tag=f"spe{H}.{hc}.{sl}")
                    p.op("scalar", lambda s: s.activation(
                        out=sp2[:, 0:hss], in_=sp2[:, 0:hss], func=AF.Ln,
                        bias=consts_sb[:, 0:1]),
                        reads=["sp2", "consts"], writes=["sp2"],
                        tag=f"spl{H}.{hc}.{sl}")
                    p.op("vector", lambda v, bank=bank, hc=hc, s0=s0:
                         v.scalar_tensor_tensor(
                             out=dall[:, hc * HW + s0: hc * HW + s0 + hss],
                             in0=bank[:, 0:hss],
                             scalar=0.0, in1=sp2[:, 0:hss],
                             op0=ALU.max, op1=ALU.add),
                         reads=[bk, "sp2"], writes=[KDA],
                         tag=f"spf{H}.{hc}.{sl}")
                    p.op("gpsimd", lambda g, hc=hc, s0=s0, h0=h0:
                         g.tensor_tensor(
                             out=dxall[:, hc * HW + s0: hc * HW + s0 + hss],
                             in0=dall[:, hc * HW + s0: hc * HW + s0 + hss],
                             in1=xhb[:, hc * tok + h0 + s0:
                                     hc * tok + h0 + s0 + hss],
                             op=ALU.mult),
                         reads=[KDA, "xhb"], writes=[KDX],
                         tag=f"dx{H}.{hc}.{sl}")
                p.dma("gpsimd", [(o_d[hc * 128:(hc + 1) * 128, h0:h0 + HW],
                                  dall[:, hc * HW:(hc + 1) * HW])],
                      reads=[KDA], tag=f"std{H}.{hc}")
                if H == 0:
                    p.op("vector", lambda v, hc=hc: v.reduce_sum(
                        out=dtp[:, hc:hc + 1],
                        in_=dall[:, hc * HW:(hc + 1) * HW], axis=AX.X),
                        reads=[KDA], writes=["dtp"], tag=f"dt0.{hc}")
                else:
                    p.op("vector", lambda v, hc=hc: v.reduce_sum(
                        out=dtp[:, 8 + hc:9 + hc],
                        in_=dall[:, hc * HW:(hc + 1) * HW], axis=AX.X),
                        reads=[KDA], writes=["dtp"], tag=f"dt1.{hc}")
                    p.op("vector", lambda v, hc=hc: v.tensor_tensor(
                        out=dtp[:, 8 + hc:9 + hc], in0=dtp[:, hc:hc + 1],
                        in1=dtp[:, 8 + hc:9 + hc], op=ALU.add),
                        reads=["dtp"], writes=["dtp"], tag=f"dts.{hc}")
                    p.op("vector", lambda v, hc=hc: v.tensor_scalar(
                        out=etile[:], in0=asb[:],
                        scalar1=dtp[:, 8 + hc:9 + hc],
                        scalar2=None, op0=ALU.mult),
                        reads=["dtp", "asb"], writes=["etile"], tag=f"de{hc}")
                    p.op("scalar", lambda s, hc=hc: s.activation(
                        out=dec_sb[:, hc * 16:(hc + 1) * 16], in_=etile[:],
                        func=AF.Exp),
                        reads=["etile"], writes=["dec"], tag=f"dex{hc}")
            bi = 0
            for n in range(16):
                for which, src, ksrc, dst, kdst in (
                        (0, bsb, "bsb", BbS, "BbS"),
                        (1, csbB, "csbB", CbS, "CbS")):
                    for sl in range(hts):
                        s0 = sl * hss
                        bank, bk = bcastB[bi % 4], bcastK[bi % 4]
                        bi += 1
                        p.op("tensor", lambda t, n=n, s0=s0, bank=bank,
                             src=src, h0=h0: t.matmul(
                                 bank[:, 0:hss],
                                 sel[:, n * 128:(n + 1) * 128],
                                 src[:, h0 + s0: h0 + s0 + hss],
                                 start=True, stop=True),
                             reads=["sel", ksrc], writes=[bk],
                             tag=f"bc{H}.{n}.{which}.{sl}")
                        p.op("scalar", lambda s, n=n, s0=s0, bank=bank,
                             dst=dst: s.activation(
                                 out=dst[:, n * HW + s0: n * HW + s0 + hss],
                                 in_=bank[:, 0:hss], func=AF.Copy),
                             reads=[bk], writes=[kdst],
                             tag=f"be{H}.{n}.{which}.{sl}")
            for hc in range(NCH):
                KDA, KDX = f"dal{hc}", f"dxl{hc}"
                for n in range(16):
                    ib = n % 2
                    da, uu, hh, hb = daB[ib], uB[ib], hhB[ib], hbxB[ib]
                    kda, ku, kh, khb = (f"da{ib}", f"u{ib}",
                                        f"hh{ib}", f"hbx{ib}")
                    p.op("scalar", lambda s, n=n, hc=hc, da=da: s.activation(
                        out=da[:, :], in_=dall[:, hc * HW:(hc + 1) * HW],
                        func=AF.Exp, scale=asb[:, n:n + 1]),
                        reads=[KDA, "asb"], writes=[kda],
                        tag=f"da{H}.{hc}.{n}")
                    p.op("gpsimd", lambda g, n=n, hc=hc, uu=uu:
                         g.tensor_tensor(
                             out=uu[:, :],
                             in0=dxall[:, hc * HW:(hc + 1) * HW],
                             in1=BbS[:, n * HW:(n + 1) * HW], op=ALU.mult),
                         reads=[KDX, "BbS"], writes=[ku],
                         tag=f"u{H}.{hc}.{n}")
                    idx = hc * 16 + n
                    p.op("vector", lambda v, da=da, uu=uu, hh=hh, idx=idx,
                         H=H: v.tensor_tensor_scan(
                             out=hh[:, :], data0=da[:, :], data1=uu[:, :],
                             initial=(zcol[:, 0:1] if H == 0
                                      else hmid[:, idx:idx + 1]),
                             op0=ALU.mult, op1=ALU.add),
                         reads=[kda, ku, "zcol"] + ([] if H == 0
                                                    else ["hmid"]),
                         writes=[kh], tag=f"scan{H}.{hc}.{n}")
                    if H == 0:
                        p.op("gpsimd", lambda g, idx=idx, hh=hh:
                             g.tensor_copy(
                                 out=hmid[:, idx:idx + 1],
                                 in_=hh[:, HW - 1:HW]),
                             reads=[kh], writes=["hmid"], tag=f"hm{idx}")
                    else:
                        p.op("gpsimd", lambda g, idx=idx, hh=hh:
                             g.tensor_copy(
                                 out=acc_sb[:, idx:idx + 1],
                                 in_=hh[:, HW - 1:HW]),
                             reads=[kh], writes=["acc"], tag=f"ac{idx}")
                    p.op("vector", lambda v, n=n, hh=hh, hb=hb:
                         v.tensor_tensor(
                             out=hb[:, :], in0=hh[:, :],
                             in1=CbS[:, n * HW:(n + 1) * HW], op=ALU.mult),
                         reads=[kh, "CbS"], writes=[khb],
                         tag=f"ch{H}.{hc}.{n}")
                    if n > 0:
                        hbp, khbp = hbxB[(n - 1) % 2], f"hbx{(n - 1) % 2}"
                        for sl in range(hts):
                            s0 = sl * hss
                            p.op("tensor", lambda t, n=n, s0=s0, sl=sl,
                                 hb=hbp: t.matmul(
                                     pY[sl][:, 0:hss], identb[:],
                                     hb[:, s0:s0 + hss],
                                     start=(n == 1), stop=False),
                                 reads=[khbp, "identb"], writes=[KY[sl]],
                                 tag=f"y{H}.{hc}.{n - 1}.{sl}")
                for sl in range(hts):
                    s0 = sl * hss
                    p.op("tensor", lambda t, s0=s0, sl=sl: t.matmul(
                        pY[sl][:, 0:hss], identb[:],
                        hbxB[1][:, s0:s0 + hss],
                        start=False, stop=True),
                        reads=["hbx1", "identb"], writes=[KY[sl]],
                        tag=f"y{H}.{hc}.15.{sl}")
                for sl in range(hts):
                    s0 = sl * hss
                    p.op("vector", lambda v, hc=hc, s0=s0, sl=sl, h0=h0:
                         v.scalar_tensor_tensor(
                             out=sp1b[:, 0:hss],
                             in0=xhb[:, hc * tok + h0 + s0:
                                     hc * tok + h0 + s0 + hss],
                             scalar=dv[:, hc:hc + 1], in1=pY[sl][:, 0:hss],
                             op0=ALU.mult, op1=ALU.add),
                         reads=["xhb", "dv", KY[sl]], writes=["sp1b"],
                         tag=f"ye{H}.{hc}.{sl}")
                    p.dma("sync", [(o_y0[hc * 128:(hc + 1) * 128,
                                         h0 + s0:h0 + s0 + hss],
                                    sp1b[:, 0:hss])],
                          reads=["sp1b"], tag=f"sty{H}.{hc}.{sl}")
        p.dma("sync", [(o_acc[:], acc_sb[:]), (o_dec[:], dec_sb[:])],
              reads=["acc", "dec"], tag="stad")

        p.emit()
    return nc



def build_launch4(tok):
    nc = bass.Bass("TRN2", target_bir_lowering=False)
    tch = tok // CH
    xin = nc.dram_tensor("xin", [D, tok], F32, kind="ExternalInput").ap()
    y0in = nc.dram_tensor("y0in", [D, tok], BF16, kind="ExternalInput").ap()
    din = nc.dram_tensor("din", [D, tok], BF16, kind="ExternalInput").ap()
    cin = nc.dram_tensor("cin", [16, tok], BF16, kind="ExternalInput").ap()
    accall = nc.dram_tensor("accall", [128, NCORES * 128], F32,
                            kind="ExternalInput").ap()
    decall = nc.dram_tensor("decall", [128, NCORES * 128], F32,
                            kind="ExternalInput").ap()
    mcol8 = nc.dram_tensor("mcol8", [128, NCORES], F32,
                           kind="ExternalInput").ap()
    alog = nc.dram_tensor("alog", [1, 16], F32, kind="ExternalInput").ap()
    sel16 = nc.dram_tensor("sel16", [16, 16 * 128], F32,
                           kind="ExternalInput").ap()
    woutT = nc.dram_tensor("woutT", [D, D], F32, kind="ExternalInput").ap()
    wgate = nc.dram_tensor("wgate", [D, 6], F32, kind="ExternalInput").ap()
    atmat = nc.dram_tensor("atmat", [96, 100], F32, kind="ExternalInput").ap()
    bmat = nc.dram_tensor("bmat", [96, D], F32, kind="ExternalInput").ap()
    wtwT = nc.dram_tensor("wtwT", [D, D], F32, kind="ExternalInput").ap()
    wtbrow = nc.dram_tensor("wtbrow", [1, D], F32, kind="ExternalInput").ap()
    wdwT = nc.dram_tensor("wdwT", [D, D], F32, kind="ExternalInput").ap()
    wdb = nc.dram_tensor("wdb", [128, 8], F32, kind="ExternalInput").ap()
    scalein = nc.dram_tensor("scalein", [1, 1], F32, kind="ExternalInput").ap()
    ident = nc.dram_tensor("ident", [128, 128], F32, kind="ExternalInput").ap()
    consts = nc.dram_tensor("consts", [128, 136], F32, kind="ExternalInput").ap()
    o_x = nc.dram_tensor("xout", [D, tok], F32, kind="ExternalOutput").ap()

    ctx = Ctx(nc, tok)
    p = ctx.p
    tss, nts = ctx.tss, ctx.nts

    with ctx.es:
        consts_sb = ctx.sb("consts_sb", (128, 136), F32)
        ctx.csb = consts_sb
        ctx.ckey = "consts"
        ones_col = ctx.sb("ones_col", (128, 1), BF16)
        identb = ctx.sb("identb", (128, 128), BF16)
        xh = ctx.sb("xh", (128, NCH * tok), F32)
        ctx.prealloc_ln()
        ctx.shared("rowA", (1, tok), F32)
        ctx.shared("rowB", (1, tok), F32)
        pS1 = ctx.ps("pS1", (128, 512))
        pS2 = ctx.ps("pS2", (128, 512))
        B1, B2k = "b_S1", "b_S2"
        KXH = "xh"

        p.dma("sync", [(consts_sb[:], consts[:])], writes=["consts"], tag="ldc")
        p.op("vector", lambda v: v.tensor_copy(out=ones_col[:],
                                               in_=consts_sb[:, 0:1]),
             reads=["consts"], writes=["ones_col"], tag="mkones")
        ones_row32 = consts_sb[0:1, 2:130]
        p.dma("gpsimd", [(identb[:], ident[:])], writes=["identb"], tag="ldid")
        p.dma("sync", [(xh[:, c * tok:(c + 1) * tok],
                        xin[c * 128:(c + 1) * 128, :]) for c in range(NCH)],
              writes=[KXH], tag="ldx")

        with contextlib.ExitStack() as esA:
            ctx.alloc_es = esA
            def sbA(name, shape, dtype):
                return esA.enter_context(nc.sbuf_tensor(name, list(shape), dtype))
            asb = sbA("asb", (128, 16), F32)
            arow_sb = sbA("arow_sb", (1, 16), F32)
            sel = sbA("sel", (16, 16 * 128), BF16)
            hin = sbA("hin", (128, 128), F32)
            csb2 = sbA("csb2", (16, tok), BF16)
            ybig = sbA("ybig", (128, NCH * tok), BF16)
            wosb = sbA("wosb", (128, NCH * D), BF16)
            pY = [ctx.ps(f"pY{i}", (128, 512)) for i in range(max(4, nts))]
            KY = [f"b_Y{i}" for i in range(max(4, nts))]

            p.dma("sync", [(arow_sb[:], alog[:])],
                  writes=["arow"], tag="lds")
            p.dma("gpsimd", [(sel[:], sel16[:]), (csb2[:], cin[:])],
                  writes=["sel", "csb2"], tag="ldsc2")
            p.op("tensor", lambda t: t.matmul(
                pS1[:, 0:16], ones_row32[0:1, :], arow_sb[:],
                start=True, stop=True),
                reads=["arow", "consts"], writes=[B1], tag="abc")
            p.op("scalar", lambda s: s.activation(
                out=asb[:], in_=pS1[:, 0:16], func=AF.Exp),
                reads=[B1], writes=["asb"], tag="aexp")
            p.op("vector", lambda v: v.tensor_scalar(
                out=asb[:], in0=asb[:], scalar1=-1.0, scalar2=None,
                op0=ALU.mult), reads=["asb"], writes=["asb"], tag="aneg")
            with contextlib.ExitStack() as esC0:
                ctx.alloc_es = esC0
                accs = ctx.sb("accs", (128, 128), F32)
                decs = ctx.sb("decs", (128, 128), F32)
                mc = ctx.sb("mc", (128, NCORES), F32)
                m1 = ctx.sb("m1", (128, NCORES), F32)
                dtl = ctx.sb("dtl", (128, 128), F32)
                tmpa = ctx.sb("tmpa", (128, 128), F32)
                p.dma("sync", [(mc[:], mcol8[:])], writes=["mc"], tag="ldsc")
                p.op("vector", lambda v: v.tensor_scalar(
                    out=m1[:], in0=mc[:], scalar1=-1.0, scalar2=1.0,
                    op0=ALU.mult, op1=ALU.add),
                    reads=["mc"], writes=["m1"], tag="m1")
                p.op("vector", lambda v: v.memset(hin[:], 0.0), writes=["hin"],
                     tag="h0")
                for r in range(NCORES):
                    p.dma("sync", [(accs[:], accall[:, r * 128:(r + 1) * 128]),
                                   (decs[:], decall[:, r * 128:(r + 1) * 128])],
                          writes=["accs", "decs"], tag=f"ldad{r}")
                    p.op("vector", lambda v, r=r: v.tensor_scalar(
                        out=dtl[:], in0=decs[:, :],
                        scalar1=mc[:, r:r + 1], scalar2=m1[:, r:r + 1],
                        op0=ALU.mult, op1=ALU.add),
                        reads=["decs", "mc", "m1"], writes=["dtl"],
                        tag=f"dt{r}")
                    p.op("vector", lambda v, r=r: v.tensor_scalar(
                        out=tmpa[:], in0=accs[:, :],
                        scalar1=mc[:, r:r + 1], scalar2=None, op0=ALU.mult),
                        reads=["accs", "mc"], writes=["tmpa"], tag=f"ta{r}")
                    p.op("vector", lambda v: v.tensor_tensor(
                        out=hin[:], in0=hin[:], in1=dtl[:], op=ALU.mult),
                        reads=["hin", "dtl"], writes=["hin"], tag=f"hm{r}")
                    p.op("vector", lambda v: v.tensor_tensor(
                        out=hin[:], in0=hin[:], in1=tmpa[:], op=ALU.add),
                        reads=["hin", "tmpa"], writes=["hin"], tag=f"ha{r}")
            ctx.alloc_es = esA
            p.barrier()
            p.dma("gpsimd",
                  [(wosb[:, c * D:(c + 1) * D],
                    woutT[c * 128:(c + 1) * 128, :]) for c in range(NCH)],
                  writes=["wo"], tag="ldwo")
            esA1 = contextlib.ExitStack()
            ctx.alloc_es = esA1
            dhc = ctx.sb("dhc", (128, tok), BF16)
            cumdB = [ctx.sb(f"cumd{i}", (128, tok), F32) for i in range(2)]
            mhcB = [ctx.sb(f"mhc{i}", (128, tok), F32) for i in range(2)]
            hbxB = [ctx.sb(f"hbx{i}", (128, tok), BF16) for i in range(2)]
            zt5 = ctx.sb("zt5", (128, tss), F32)
            p.op("vector", lambda v: v.memset(zt5[:], 0.0), writes=["zt5"],
                 tag="z4")
            p.dma("scalar",
                  [(ybig[:, c * tok:(c + 1) * tok],
                    y0in[c * 128:(c + 1) * 128, :]) for c in range(NCH)],
                  writes=["ybig"], tag="ldy0")
            pB40 = ctx.ps("pB40", (128, 512))
            cbB = [pB40, pS1, pS2]
            cbK = ["b_B40", B1, B2k]
            for hc in range(NCH):
                hp = hc % 2
                cumd = cumdB[hp]
                KDH, KCU = "dhc", f"cumd{hp}"
                p.dma("sync", [(dhc[:], din[hc * 128:(hc + 1) * 128, :])],
                      writes=[KDH], tag=f"ldd{hc}")
                for sg in range(nts):
                    s0 = sg * tss
                    rd = [KDH, "zt5"] + ([KCU] if sg > 0 else [])
                    p.op("vector", lambda v, s0=s0, sg=sg, cumd=cumd:
                         v.tensor_tensor_scan(
                             out=cumd[:, s0:s0 + tss],
                             data0=dhc[:, s0:s0 + tss], data1=zt5[:, :],
                             initial=(zt5[:, 0:1] if sg == 0
                                      else cumd[:, s0 - 1:s0]),
                             op0=ALU.add, op1=ALU.add),
                         reads=rd, writes=[KCU], tag=f"cs{hc}.{sg}")
                for n in range(16):
                    ib = n % 2
                    mh, hb = mhcB[ib], hbxB[ib]
                    kmh, khb = f"mhc{ib}", f"hbx{ib}"
                    p.op("scalar", lambda s, n=n, mh=mh, cumd=cumd:
                         s.activation(
                             out=mh[:, :], in_=cumd[:, :], func=AF.Exp,
                             scale=asb[:, n:n + 1]),
                         reads=[KCU, "asb"], writes=[kmh],
                         tag=f"M{hc}.{n}")
                    def emit_cb(ts, n=n):
                        t0 = ts * tss
                        bi = (n * nts + ts) % 3
                        bank, bk = cbB[bi], cbK[bi]
                        p.op("tensor", lambda t, n=n, t0=t0, bank=bank:
                             t.matmul(bank[:, 0:tss],
                                      sel[:, n * 128:(n + 1) * 128],
                                      csb2[:, t0:t0 + tss],
                                      start=True, stop=True),
                             reads=["sel", "csb2"], writes=[bk],
                             tag=f"cb{hc}.{n}.{ts}")

                    def emit_chx(ts, n=n, mh=mh, hb=hb, kmh=kmh, khb=khb):
                        t0 = ts * tss
                        bi = (n * nts + ts) % 3
                        bank, bk = cbB[bi], cbK[bi]
                        p.op("vector", lambda v, hc=hc, n=n, t0=t0, bank=bank,
                             mh=mh, hb=hb: v.scalar_tensor_tensor(
                                 out=hb[:, t0:t0 + tss],
                                 in0=mh[:, t0:t0 + tss],
                                 scalar=hin[:, hc * 16 + n: hc * 16 + n + 1],
                                 in1=bank[:, 0:tss],
                                 op0=ALU.mult, op1=ALU.mult),
                             reads=[kmh, "hin", bk], writes=[khb],
                             tag=f"chx{hc}.{n}.{ts}")

                    for ts in range(nts):
                        emit_cb(ts)
                        if ts >= 2:
                            emit_chx(ts - 2)
                    for ts in range(max(0, nts - 2), nts):
                        emit_chx(ts)
                    if n > 0:
                        hbp, khbp = hbxB[(n - 1) % 2], f"hbx{(n - 1) % 2}"
                        for ts in range(nts):
                            t0 = ts * tss
                            p.op("tensor", lambda t, n=n, t0=t0, ts=ts,
                                 hb=hbp: t.matmul(
                                     pY[ts][:, 0:tss], identb[:],
                                     hb[:, t0:t0 + tss],
                                     start=(n == 1), stop=False),
                                 reads=[khbp, "identb"], writes=[KY[ts]],
                                 tag=f"y{hc}.{n - 1}.{ts}")
                for ts in range(nts):
                    t0 = ts * tss
                    p.op("tensor", lambda t, t0=t0, ts=ts: t.matmul(
                        pY[ts][:, 0:tss], identb[:],
                        hbxB[1][:, t0:t0 + tss],
                        start=False, stop=True),
                        reads=["hbx1", "identb"], writes=[KY[ts]],
                        tag=f"y{hc}.15.{ts}")
                for ts in range(nts):
                    t0 = ts * tss
                    p.op("vector", lambda v, hc=hc, t0=t0, ts=ts:
                         v.tensor_tensor(
                             out=ybig[:, hc * tok + t0: hc * tok + t0 + tss],
                             in0=ybig[:, hc * tok + t0: hc * tok + t0 + tss],
                             in1=pY[ts][:, 0:tss],
                             op=ALU.add),
                         reads=["ybig", KY[ts]], writes=["ybig"],
                         tag=f"yb{hc}.{ts}")
            esA1.close()
            ctx.alloc_es = esA
            p.barrier()
            zts = sbA("zts", (128, NCH * 512), F32)
            zb16 = ctx.shared("lnxb", (128, tss), BF16)
            zsq = ctx.shared("lnsq", (128, tss), BF16)
            s1r = ctx.shared("lns1", (1, tok), F32)
            s2r = ctx.shared("lns2", (1, tok), F32)
            arow5 = ctx.shared("rowA", (1, tok), F32)
            brow5 = ctx.shared("rowB", (1, tok), F32)
            t1c = sbA("t1c", (128, tss), F32)
            for ts in range(nts):
                t0 = ts * tss
                for o in range(NCH):
                    bank, bk = (pS1, B1) if o % 2 == 0 else (pS2, B2k)
                    for k in range(NCH):
                        p.op("tensor", lambda t, o=o, k=k, t0=t0, bank=bank:
                             t.matmul(bank[:, 0:tss],
                                      wosb[:, k * D + o * 128:
                                           k * D + (o + 1) * 128],
                                      ybig[:, k * tok + t0: k * tok + t0 + tss],
                                      start=(k == 0), stop=(k == NCH - 1)),
                             reads=["wo", "ybig"], writes=[bk],
                             tag=f"wo{o}.{ts}.{k}")
                    p.op("vector", lambda v, o=o, bank=bank: v.tensor_copy(
                        out=zts[:, o * tss:(o + 1) * tss], in_=bank[:, 0:tss]),
                        reads=[bk], writes=["zts"], tag=f"woe{o}.{ts}")
                for o in range(NCH):
                    zl = zts[:, o * tss:(o + 1) * tss]
                    p.op("vector", lambda v, zl=zl: v.tensor_copy(
                        out=zb16[:, 0:tss], in_=zl),
                        reads=["zts"], writes=["lnxb"], tag=f"zc{ts}.{o}")
                    p.op("scalar", lambda s, zl=zl: s.activation(
                        out=zsq[:, 0:tss], in_=zl, func=AF.Square),
                        reads=["zts"], writes=["lnsq"], tag=f"zs{ts}.{o}")
                    p.op("tensor", lambda t, o=o, t0=t0: t.matmul(
                        pS1[0:1, 0:tss], ones_col[:, 0:1], zb16[:, 0:tss],
                        start=(o == 0), stop=(o == NCH - 1)),
                        reads=["lnxb", "ones_col"], writes=[B1],
                        tag=f"zm1{ts}.{o}")
                    p.op("tensor", lambda t, o=o, t0=t0: t.matmul(
                        pS2[0:1, 0:tss], ones_col[:, 0:1], zsq[:, 0:tss],
                        start=(o == 0), stop=(o == NCH - 1)),
                        reads=["lnsq", "ones_col"], writes=[B2k],
                        tag=f"zm2{ts}.{o}")
                p.op("vector", lambda v, t0=t0: v.tensor_copy(
                    out=s1r[:, t0:t0 + tss], in_=pS1[0:1, 0:tss]),
                    reads=[B1], writes=["lns1"], tag=f"ze1{ts}")
                p.op("vector", lambda v, t0=t0: v.tensor_copy(
                    out=s2r[:, t0:t0 + tss], in_=pS2[0:1, 0:tss]),
                    reads=[B2k], writes=["lns2"], tag=f"ze2{ts}")
                p.op("vector", lambda v, t0=t0: v.tensor_scalar(
                    out=brow5[:, t0:t0 + tss], in0=s1r[:, t0:t0 + tss],
                    scalar1=-1.0 / D, scalar2=None, op0=ALU.mult),
                    reads=["lns1"], writes=["rr.b"], tag=f"zb{ts}")
                p.op("vector", lambda v, t0=t0: v.tensor_scalar(
                    out=arow5[:, t0:t0 + tss], in0=s2r[:, t0:t0 + tss],
                    scalar1=1.0 / D, scalar2=None, op0=ALU.mult),
                    reads=["lns2"], writes=["rr.a"], tag=f"za{ts}")
                p.op("vector", lambda v, t0=t0: v.tensor_tensor(
                    out=s1r[:, t0:t0 + tss], in0=brow5[:, t0:t0 + tss],
                    in1=brow5[:, t0:t0 + tss], op=ALU.mult),
                    reads=["rr.b"], writes=["lns1"], tag=f"zq{ts}")
                p.op("vector", lambda v, t0=t0: v.tensor_tensor(
                    out=arow5[:, t0:t0 + tss], in0=arow5[:, t0:t0 + tss],
                    in1=s1r[:, t0:t0 + tss], op=ALU.subtract),
                    reads=["rr.a", "lns1"], writes=["rr.a"], tag=f"zv{ts}")
                p.op("scalar", lambda s, t0=t0: s.activation(
                    out=arow5[:, t0:t0 + tss], in_=arow5[:, t0:t0 + tss],
                    func=AF.Ln, bias=consts_sb[0:1, 130:131]),
                    reads=["rr.a", "consts"], writes=["rr.a"], tag=f"zl{ts}")
                p.op("scalar", lambda s, t0=t0: s.activation(
                    out=arow5[:, t0:t0 + tss], in_=arow5[:, t0:t0 + tss],
                    func=AF.Exp, scale=consts_sb[0:1, 131:132]),
                    reads=["rr.a", "consts"], writes=["rr.a"], tag=f"zr{ts}")
                p.op("tensor", lambda t, t0=t0: t.matmul(
                    pS1[:, 0:tss], ones_row32[0:1, :],
                    arow5[:, t0:t0 + tss], start=True, stop=True),
                    reads=["rr.a", "consts"], writes=[B1], tag=f"5bA{ts}")
                p.op("tensor", lambda t, t0=t0: t.matmul(
                    pS2[:, 0:tss], ones_row32[0:1, :],
                    brow5[:, t0:t0 + tss], start=True, stop=True),
                    reads=["rr.b", "consts"], writes=[B2k], tag=f"5bB{ts}")
                for cc in range(NCH):
                    zl = zts[:, cc * tss:(cc + 1) * tss]
                    sl = xh[:, cc * tok + t0: cc * tok + t0 + tss]
                    p.op("vector", lambda v, zl=zl: v.tensor_tensor(
                        out=t1c[:], in0=zl, in1=pS2[:, 0:tss], op=ALU.add),
                        reads=["zts", B2k], writes=["t1c"],
                        tag=f"5t1.{ts}.{cc}")
                    p.op("vector", lambda v: v.tensor_tensor(
                        out=t1c[:], in0=t1c[:], in1=pS1[:, 0:tss],
                        op=ALU.mult),
                        reads=["t1c", B1], writes=["t1c"], tag=f"5t2.{ts}.{cc}")
                    p.op("vector", lambda v, sl=sl: v.tensor_tensor(
                        out=sl, in0=sl, in1=t1c[:], op=ALU.add),
                        reads=["t1c", KXH], writes=[KXH], tag=f"5t3.{ts}.{cc}")
        ctx.alloc_es = None
        p.barrier()

        ffb = ctx.sb("ffb", (128, NCH * tok), BF16)
        rows6 = {"a": ctx.shared("rowA", (1, tok), F32),
                 "b": ctx.shared("rowB", (1, tok), F32)}
        emit_ln_stats(ctx, xh, KXH, rows6, "rr", pS1, pS2, B1, B2k, ones_col,
                      tag="l4f")
        emit_ln_apply(ctx, xh, KXH, rows6, "rr", ones_row32, pS1, pS2, B1, B2k,
                      [dict(tile=ffb, key="ffb")], tag="l4fap")
        arow6 = ctx.sb("arow6", (1, tok), BF16)
        brow6 = ctx.sb("brow6", (1, tok), BF16)
        onesb = ctx.sb("onesb", (1, 128), BF16)
        p.op("gpsimd", lambda g: g.tensor_copy(out=onesb[:],
                                               in_=consts_sb[0:1, 2:130]),
             reads=["consts"], writes=["onesb"], tag="mkonesb")
        p.op("vector", lambda v: v.tensor_copy(out=arow6[:], in_=rows6["a"][:]),
             reads=["rr.a"], writes=["r6.a"], tag="cp6a")
        p.op("vector", lambda v: v.tensor_copy(out=brow6[:], in_=rows6["b"][:]),
             reads=["rr.b"], writes=["r6.b"], tag="cp6b")

        tksb = ctx.sb("tksb", (128, NCH * 600), BF16)
        valsb = ctx.sb("valsb", (100, 6 * D), BF16)
        wgsb = ctx.sb("wgsb", (128, NCH * 6), BF16)
        wbsb = ctx.sb("wbsb", (128, NCH * D), BF16)
        wdbs = ctx.sb("wdbs", (128, 8), F32)
        scol = ctx.sb("scol", (128, 1), F32)
        scsb = ctx.sb("scsb", (1, 1), F32)
        p.dma("gpsimd",
              [(wgsb[:, c * 6:(c + 1) * 6], wgate[c * 128:(c + 1) * 128, :])
               for c in range(NCH)] +
              [(wbsb[:, c * D:(c + 1) * D], wtwT[c * 128:(c + 1) * 128, :])
               for c in range(NCH)],
              writes=["wgsb", "wbsb"], tag="ldffn")
        p.dma("sync", [(wdbs[:], wdb[:]),
                       (scsb[:], scalein[:])],
              writes=["wdbs", "scsb"], tag="ldffb")
        p.op("tensor", lambda t: t.matmul(
            pS1[:, 0:1], ones_row32[0:1, :], scsb[:], start=True, stop=True),
            reads=["scsb", "consts"], writes=[B1], tag="scb")
        p.op("vector", lambda v: v.tensor_copy(out=scol[:], in_=pS1[:, 0:1]),
             reads=[B1], writes=["scol"], tag="sce")
        with contextlib.ExitStack() as esT:
            ctx.alloc_es = esT
            atsb = ctx.sb("atsb", (16, 6 * 100), BF16)
            bmsb = ctx.sb("bmsb", (16, 6 * D), BF16)
            p.dma("gpsimd",
                  [(atsb[:, e * 100:(e + 1) * 100],
                    atmat[e * 16:(e + 1) * 16, :]) for e in range(6)] +
                  [(bmsb[:, e * D:(e + 1) * D],
                    bmat[e * 16:(e + 1) * 16, :]) for e in range(6)],
                  writes=["atsb", "bmsb"], tag="ldab2")
            for e in range(6):
                for cc in range(NCH):
                    bank, bk = ((pS1, B1) if (e * NCH + cc) % 2 == 0
                                else (pS2, B2k))
                    p.op("tensor", lambda t, e=e, cc=cc, bank=bank: t.matmul(
                        bank[:, 0:100], bmsb[:, e * D + cc * 128:
                                             e * D + (cc + 1) * 128],
                        atsb[:, e * 100:(e + 1) * 100], start=True, stop=True),
                        reads=["bmsb", "atsb"], writes=[bk], tag=f"tk{e}.{cc}")
                    p.op("vector", lambda v, e=e, cc=cc, bank=bank:
                         v.tensor_copy(
                             out=tksb[:, cc * 600 + e * 100:
                                      cc * 600 + (e + 1) * 100],
                             in_=bank[:, 0:100]),
                         reads=[bk], writes=["tksb"], tag=f"tke{e}.{cc}")
        ctx.alloc_es = None
        p.barrier()
        esV = contextlib.ExitStack()
        ctx.alloc_es = esV
        vtmp = ctx.sb("vtmp", (100, 512), F32)
        wtbs = ctx.sb("wtbs", (1, D), F32)
        p.dma("sync", [(wtbs[:], wtbrow[:])], writes=["wtbs"], tag="ldwtb")
        for e in range(6):
            for half in range(2):
                bank, bk = (pS1, B1) if half == 0 else (pS2, B2k)
                for cc in range(NCH):
                    p.op("tensor", lambda t, e=e, cc=cc, half=half, bank=bank:
                         t.matmul(bank[0:100, 0:512],
                                  tksb[:, cc * 600 + e * 100:
                                       cc * 600 + (e + 1) * 100],
                                  wbsb[:, cc * D + half * 512:
                                       cc * D + half * 512 + 512],
                                  start=(cc == 0), stop=(cc == NCH - 1)),
                         reads=["tksb", "wbsb"], writes=[bk],
                         tag=f"vl{e}.{half}.{cc}")
                p.op("vector", lambda v, half=half, bank=bank: v.tensor_copy(
                    out=vtmp[:, 0:512], in_=bank[0:100, 0:512]),
                    reads=[bk], writes=["vtmp"], tag=f"vle{e}.{half}")
                bank2, bk2 = (pS2, B2k) if half == 0 else (pS1, B1)
                p.op("tensor", lambda t, half=half, bank2=bank2: t.matmul(
                    bank2[:, 0:512], ones_row32[0:1, :],
                    wtbs[0:1, half * 512: half * 512 + 512],
                    start=True, stop=True),
                    reads=["wtbs", "consts"], writes=[bk2], tag=f"vb{e}.{half}")
                p.op("vector", lambda v, e=e, half=half, bank2=bank2:
                     v.tensor_tensor(
                         out=valsb[:, e * D + half * 512:
                                   e * D + half * 512 + 512],
                         in0=vtmp[:, 0:512], in1=bank2[0:100, 0:512],
                         op=ALU.add),
                     reads=["vtmp", bk2], writes=["valsb"], tag=f"vf{e}.{half}")
            p.op("vector", lambda v, e=e: v.memset(
                valsb[0:1, e * D:(e + 1) * D], 0.0),
                writes=["valsb"], tag=f"vz{e}")
        esV.close()
        ctx.alloc_es = None
        p.barrier()
        p.dma("gpsimd",
              [(wbsb[:, c * D:(c + 1) * D], wdwT[c * 128:(c + 1) * 128, :])
               for c in range(NCH)], writes=["wbsb"], tag="ldwdw")

        PARQ = []
        for par in range(2):
            PARQ.append(dict(
                fch=ctx.sb(f"fch{par}", (128, NCH * CH), F32),
                lgt=ctx.sb(f"lgt{par}", (128, 6), F32),
                mxg=ctx.sb(f"mxg{par}", (128, 1), F32),
                g01=ctx.sb(f"g01{par}", (128, 6), F32),
                scx=ctx.sb(f"scx{par}", (128, 600), F32),
                mxe=ctx.sb(f"mxe{par}", (128, 6), F32),
                sms=ctx.sb(f"sms{par}", (128, 6), F32),
                smr=ctx.sb(f"smr{par}", (128, 6), F32),
                gnc=ctx.sb(f"gnc{par}", (128, 6), F32),
                asel=ctx.sb(f"asel{par}", (128, 600), BF16),
                aselT=ctx.sb(f"aselT{par}", (100, 6 * CH), BF16),
                dfx=ctx.sb(f"dfx{par}", (128, NCH * CH), BF16),
                cmb=[ctx.sb(f"cmb{par}_{j}", (128, CH), F32) for j in range(3)],
            ))
        lnxbS = ctx.shared("lnxb", (128, tss), BF16)
        lnsqS = ctx.shared("lnsq", (128, tss), BF16)
        lnt1S = ctx.shared("lnt1", (128, tss), F32)
        lngtS = ctx.shared("lngt", (128, tss), F32)
        lns1S = ctx.shared("lns1", (1, tok), F32)
        for par in range(2):
            PARQ[par]["zb"] = (lnxbS[:, par * CH:(par + 1) * CH]
                               if tss >= 2 * CH
                               else ctx.sb(f"zbl{par}", (128, CH), BF16))
            PARQ[par]["zq"] = (lnsqS[:, par * CH:(par + 1) * CH]
                               if tss >= 2 * CH
                               else ctx.sb(f"zsl{par}", (128, CH), BF16))
            PARQ[par]["t1"] = [
                (lnt1S[:, (2 * par + j) * CH:(2 * par + j + 1) * CH]
                 if tss >= (2 * par + j + 1) * CH
                 else ctx.sb(f"t1l{par}_{j}", (128, CH), F32))
                for j in range(2)]
            PARQ[par]["msk"] = [
                (lngtS[:, (2 * par + j) * CH:(2 * par + j + 1) * CH]
                 if tss >= (2 * par + j + 1) * CH
                 else ctx.sb(f"msk{par}_{j}", (128, CH), F32))
                for j in range(2)]
            PARQ[par]["lr"] = (lns1S[0:1, par * 3 * CH:(par + 1) * 3 * CH]
                               if tok >= 6 * CH
                               else ctx.sb(f"lrw{par}", (1, 3 * CH), F32))
        bksP = [[(pS1, B1), (pS2, B2k), (pY[2], KY[2])],
                [(pY[0], KY[0]), (pY[1], KY[1]), (pY[3], KY[3])]]
        pT1 = ctx.ps("pT1", (128, 1024), BF16)
        for c in range(tch):
            t0 = c * CH
            par = c % 2
            Q = PARQ[par]
            (bA, kA), (bB, kB), (bC, kC) = bksP[par]
            fch, scx, asel, aselT, dfx = (Q["fch"], Q["scx"], Q["asel"],
                                          Q["aselT"], Q["dfx"])
            lgt, mxg, g01, mxe, sms, smr, gnc = (Q["lgt"], Q["mxg"], Q["g01"],
                                                 Q["mxe"], Q["sms"], Q["smr"],
                                                 Q["gnc"])
            KF, KXC = f"fch{par}", f"xh.c{c}"
            KSC = [f"scx{par}.{e}" for e in range(6)]
            KAS = [f"asel{par}.{e}" for e in range(6)]
            p.op("tensor", lambda t, t0=t0, bA=bA: t.matmul(
                bA[:, 0:CH], onesb[0:1, :], arow6[:, t0:t0 + CH],
                start=True, stop=True),
                reads=["r6.a", "onesb"], writes=[kA], tag=f"6bA{c}")
            p.op("tensor", lambda t, t0=t0, bB=bB: t.matmul(
                bB[:, 0:CH], onesb[0:1, :], brow6[:, t0:t0 + CH],
                start=True, stop=True),
                reads=["r6.b", "onesb"], writes=[kB], tag=f"6bB{c}")
            for cc in range(NCH):
                sl = xh[:, cc * tok + t0: cc * tok + t0 + CH]
                p.op("vector", lambda v, sl=sl, cc=cc, fch=fch, bB=bB:
                     v.tensor_tensor(
                         out=fch[:, cc * CH:(cc + 1) * CH], in0=sl,
                         in1=bB[:, 0:CH], op=ALU.add),
                     reads=[KXH, KXC, kB], writes=[KF], tag=f"fc1{c}.{cc}")
                p.op("vector", lambda v, cc=cc, fch=fch, bA=bA:
                     v.tensor_tensor(
                         out=fch[:, cc * CH:(cc + 1) * CH],
                         in0=fch[:, cc * CH:(cc + 1) * CH],
                         in1=bA[:, 0:CH], op=ALU.mult),
                     reads=[KF, kA], writes=[KF], tag=f"fc2{c}.{cc}")
            for k in range(NCH):
                p.op("tensor", lambda t, k=k, t0=t0, bC=bC: t.matmul(
                    bC[:, 0:6], ffb[:, k * tok + t0: k * tok + t0 + CH],
                    wgsb[:, k * 6:(k + 1) * 6],
                    start=(k == 0), stop=(k == NCH - 1)),
                    reads=["ffb", "wgsb"], writes=[kC], tag=f"lg{c}.{k}")
            p.op("vector", lambda v, lgt=lgt, bC=bC: v.tensor_copy(
                out=lgt[:], in_=bC[:, 0:6]),
                reads=[kC], writes=[f"lgt{par}"], tag=f"lge{c}")
            p.op("vector", lambda v, lgt=lgt, mxg=mxg: v.reduce_max(
                out=mxg[:], in_=lgt[:], axis=AX.X),
                reads=[f"lgt{par}"], writes=[f"mxg{par}"], tag=f"mx{c}")
            p.op("vector", lambda v, lgt=lgt, mxg=mxg, g01=g01: v.tensor_scalar(
                out=g01[:], in0=lgt[:], scalar1=mxg[:, 0:1], scalar2=None,
                op0=ALU.is_equal),
                reads=[f"lgt{par}", f"mxg{par}"], writes=[f"g01{par}"],
                tag=f"oh{c}")
            for half, (lo, wdt_) in enumerate(((0, 512), (512, 88))):
                bank, bk = (bA, kA) if half == 0 else (bB, kB)
                for k in range(NCH):
                    p.op("tensor", lambda t, k=k, t0=t0, lo=lo, wdt_=wdt_,
                         bank=bank: t.matmul(
                             bank[:, 0:wdt_],
                             ffb[:, k * tok + t0: k * tok + t0 + CH],
                             tksb[:, k * 600 + lo: k * 600 + lo + wdt_],
                             start=(k == 0), stop=(k == NCH - 1)),
                         reads=["ffb", "tksb"], writes=[bk],
                         tag=f"sc{c}.{half}.{k}")
                p.op("vector", lambda v, lo=lo, wdt_=wdt_, bank=bank, scx=scx:
                     v.tensor_copy(out=scx[:, lo:lo + wdt_],
                                   in_=bank[:, 0:wdt_]),
                     reads=[bk], writes=KSC, tag=f"sce{c}.{half}")
            for e in range(6):
                p.op("vector", lambda v, e=e, scx=scx, mxe=mxe: v.reduce_max(
                    out=mxe[:, e:e + 1], in_=scx[:, e * 100:(e + 1) * 100],
                    axis=AX.X),
                    reads=[KSC[e]], writes=[f"mxe{par}"], tag=f"sm{c}.{e}")
                p.op("gpsimd", lambda g, e=e, scx=scx, mxe=mxe: g.tensor_scalar(
                    out=scx[:, e * 100:(e + 1) * 100],
                    in0=scx[:, e * 100:(e + 1) * 100],
                    scalar1=mxe[:, e:e + 1], scalar2=None, op0=ALU.subtract),
                    reads=[KSC[e], f"mxe{par}"], writes=[KSC[e]],
                    tag=f"ss{c}.{e}")
            for e in range(6):
                p.op("scalar", lambda s, e=e, scx=scx, sms=sms: s.activation(
                    out=scx[:, e * 100:(e + 1) * 100],
                    in_=scx[:, e * 100:(e + 1) * 100], func=AF.Exp,
                    scale=consts_sb[:, 135:136], accum_out=sms[:, e:e + 1]),
                    reads=[KSC[e], "consts"], writes=[KSC[e], f"sms{par}"],
                    tag=f"se{c}.{e}")
            for e in range(6):
                p.op("vector", lambda v, e=e, sms=sms, smr=smr: v.reciprocal(
                    out=smr[:, e:e + 1], in_=sms[:, e:e + 1]),
                    reads=[f"sms{par}"], writes=[f"smr{par}"], tag=f"sr{c}.{e}")
            p.op("gpsimd", lambda g, smr=smr, g01=g01, gnc=gnc:
                 g.tensor_tensor(
                     out=gnc[:, 0:6], in0=smr[:, 0:6],
                     in1=g01[:, 0:6], op=ALU.mult),
                 reads=[f"smr{par}", f"g01{par}"], writes=[f"gnc{par}"],
                 tag=f"gn{c}")
            for e in range(6):
                p.op("gpsimd", lambda g, e=e, scx=scx, asel=asel, gnc=gnc:
                     g.tensor_scalar(
                         out=asel[:, e * 100:(e + 1) * 100],
                         in0=scx[:, e * 100:(e + 1) * 100],
                         scalar1=gnc[:, e:e + 1], scalar2=None, op0=ALU.mult),
                     reads=[KSC[e], f"gnc{par}"], writes=[KAS[e]],
                     tag=f"as{c}.{e}")
            for e in range(6):
                p.op("tensor", lambda t, e=e, asel=asel: t.transpose(
                    pT1[0:100, e * CH:(e + 1) * CH],
                    asel[:, e * 100:(e + 1) * 100], identb[:]),
                    reads=[KAS[e], "identb"], writes=[f"b_T1.{e}"],
                    tag=f"tr{c}.{e}")
            for e in range(6):
                p.op("vector", lambda v, e=e, aselT=aselT: v.tensor_copy(
                    out=aselT[:, e * CH:(e + 1) * CH],
                    in_=pT1[0:100, e * CH:(e + 1) * CH]),
                    reads=[f"b_T1.{e}"], writes=[f"aselT{par}"],
                    tag=f"tre{c}.{e}")
            for dc in range(NCH):
                bank, bk = (bA, kA) if dc % 2 == 0 else (bB, kB)
                for e in range(6):
                    p.op("tensor", lambda t, dc=dc, e=e, bank=bank,
                         aselT=aselT: t.matmul(
                             bank[:, 0:CH],
                             valsb[:, e * D + dc * 128: e * D + (dc + 1) * 128],
                             aselT[:, e * CH:(e + 1) * CH],
                             start=(e == 0), stop=(e == 5)),
                         reads=["valsb", f"aselT{par}"], writes=[bk],
                         tag=f"df{c}.{dc}.{e}")
                p.op("vector", lambda v, dc=dc, bank=bank, fch=fch, dfx=dfx:
                     v.tensor_tensor(
                         out=dfx[:, dc * CH:(dc + 1) * CH],
                         in0=fch[:, dc * CH:(dc + 1) * CH],
                         in1=bank[:, 0:CH], op=ALU.add),
                     reads=[KF, bk], writes=[f"dfx{par}"], tag=f"dfe{c}.{dc}")
            def emit_mzgz(cp):
                cmb = Q["cmb"][cp % 3]
                msk = Q["msk"][cp % 2]
                kcm, kms = f"cmb{par}_{cp % 3}", f"msk{par}_{cp % 2}"
                p.op("vector", lambda v, cmb=cmb, msk=msk: v.tensor_scalar(
                    out=msk[:, 0:CH], in0=cmb[:], scalar1=0.0, scalar2=None,
                    op0=ALU.is_equal),
                    reads=[kcm], writes=[kms], tag=f"mz{c}.{cp}")
                p.op("vector", lambda v, cmb=cmb, msk=msk:
                     v.scalar_tensor_tensor(
                         out=cmb[:], in0=msk[:, 0:CH], scalar=consts_sb[:, 134:135],
                         in1=cmb[:], op0=ALU.mult, op1=ALU.add),
                     reads=[kms, kcm, "consts"], writes=[kcm],
                     tag=f"gz{c}.{cp}")

            def emit_lnbo(cp):
                cmb = Q["cmb"][cp % 3]
                kcm = f"cmb{par}_{cp % 3}"
                p.op("scalar", lambda s, cmb=cmb: s.activation(
                    out=cmb[:], in_=cmb[:], func=AF.Ln),
                    reads=[kcm], writes=[kcm], tag=f"ln{c}.{cp}")
                p.op("vector", lambda v, cp=cp, cmb=cmb, fch=fch:
                     v.tensor_tensor(
                         out=fch[:, cp * CH:(cp + 1) * CH],
                         in0=fch[:, cp * CH:(cp + 1) * CH], in1=cmb[:],
                         op=ALU.add),
                     reads=[kcm, KF], writes=[KF], tag=f"bo{c}.{cp}")

            for cp in range(NCH):
                bank, bk = (bA, kA) if cp % 2 == 0 else (bB, kB)
                cmb = Q["cmb"][cp % 3]
                kcm = f"cmb{par}_{cp % 3}"
                for dc in range(NCH):
                    p.op("tensor", lambda t, cp=cp, dc=dc, bank=bank, dfx=dfx:
                         t.matmul(
                             bank[:, 0:CH],
                             wbsb[:, dc * D + cp * 128: dc * D + (cp + 1) * 128],
                             dfx[:, dc * CH:(dc + 1) * CH],
                             start=(dc == 0), stop=(dc == NCH - 1)),
                         reads=["wbsb", f"dfx{par}"], writes=[bk],
                         tag=f"dl{c}.{cp}.{dc}")
                p.op("vector", lambda v, cp=cp, bank=bank, cmb=cmb:
                     v.tensor_scalar(
                         out=cmb[:], in0=bank[:, 0:CH],
                         scalar1=wdbs[:, cp:cp + 1], scalar2=scol[:, 0:1],
                         op0=ALU.add, op1=ALU.mult),
                     reads=[bk, "wdbs", "scol"], writes=[kcm],
                     tag=f"dle{c}.{cp}")
                p.op("scalar", lambda s, cmb=cmb: s.activation(
                    out=cmb[:], in_=cmb[:], func=AF.Exp),
                    reads=[kcm], writes=[kcm], tag=f"ex{c}.{cp}")
                if cp >= 1:
                    emit_mzgz(cp - 1)
                if cp >= 2:
                    emit_lnbo(cp - 2)
            emit_mzgz(NCH - 1)
            emit_lnbo(NCH - 2)
            emit_lnbo(NCH - 1)
            zbP, zqP = Q["zb"], Q["zq"]
            for cc in range(NCH):
                fl = fch[:, cc * CH:(cc + 1) * CH]
                p.op("vector", lambda v, fl=fl, zbP=zbP: v.tensor_copy(
                    out=zbP[:, 0:CH], in_=fl),
                    reads=[KF], writes=[f"zbl{par}"], tag=f"pc{c}.{cc}")
                p.op("scalar", lambda s, fl=fl, zqP=zqP: s.activation(
                    out=zqP[:, 0:CH], in_=fl, func=AF.Square),
                    reads=[KF], writes=[f"zsl{par}"], tag=f"ps{c}.{cc}")
                p.op("tensor", lambda t, cc=cc, zbP=zbP, bC=bC: t.matmul(
                    bC[0:1, 0:CH], ones_col[:, 0:1], zbP[:, 0:CH],
                    start=(cc == 0), stop=(cc == NCH - 1)),
                    reads=[f"zbl{par}", "ones_col"], writes=[kC],
                    tag=f"pm1{c}.{cc}")
                p.op("tensor", lambda t, cc=cc, zqP=zqP, bA=bA: t.matmul(
                    bA[0:1, 0:CH], ones_col[:, 0:1], zqP[:, 0:CH],
                    start=(cc == 0), stop=(cc == NCH - 1)),
                    reads=[f"zsl{par}", "ones_col"], writes=[kA],
                    tag=f"pm2{c}.{cc}")
            lr = Q["lr"]
            klr = f"lrw{par}"
            ar = lr[0:1, 0:CH]
            br = lr[0:1, CH:2 * CH]
            sq = lr[0:1, 2 * CH:3 * CH]
            p.op("vector", lambda v, br=br, bC=bC: v.tensor_scalar(
                out=br, in0=bC[0:1, 0:CH],
                scalar1=-1.0 / D, scalar2=None, op0=ALU.mult),
                reads=[kC], writes=[klr], tag=f"pb{c}")
            p.op("vector", lambda v, ar=ar, bA=bA: v.tensor_scalar(
                out=ar, in0=bA[0:1, 0:CH],
                scalar1=1.0 / D, scalar2=None, op0=ALU.mult),
                reads=[kA], writes=[klr], tag=f"pa{c}")
            p.op("vector", lambda v, sq=sq, br=br: v.tensor_tensor(
                out=sq, in0=br, in1=br, op=ALU.mult),
                reads=[klr], writes=[klr], tag=f"pq{c}")
            p.op("vector", lambda v, ar=ar, sq=sq: v.tensor_tensor(
                out=ar, in0=ar, in1=sq, op=ALU.subtract),
                reads=[klr], writes=[klr], tag=f"pv{c}")
            p.op("scalar", lambda s, ar=ar: s.activation(
                out=ar, in_=ar, func=AF.Ln, bias=consts_sb[0:1, 130:131]),
                reads=[klr, "consts"], writes=[klr], tag=f"pl{c}")
            p.op("scalar", lambda s, ar=ar: s.activation(
                out=ar, in_=ar, func=AF.Exp, scale=consts_sb[0:1, 131:132]),
                reads=[klr, "consts"], writes=[klr], tag=f"pr{c}")
            p.op("tensor", lambda t, ar=ar, bA=bA: t.matmul(
                bA[:, 0:CH], ones_row32[0:1, :], ar,
                start=True, stop=True),
                reads=[klr, "consts"], writes=[kA], tag=f"pA{c}")
            p.op("tensor", lambda t, br=br, bB=bB: t.matmul(
                bB[:, 0:CH], ones_row32[0:1, :], br,
                start=True, stop=True),
                reads=[klr, "consts"], writes=[kB], tag=f"pB{c}")
            for cc in range(NCH):
                fl = fch[:, cc * CH:(cc + 1) * CH]
                sl = xh[:, cc * tok + t0: cc * tok + t0 + CH]
                t1 = Q["t1"][cc % 2]
                kt1 = f"t1l{par}_{cc % 2}"
                p.op("vector", lambda v, fl=fl, t1=t1, bB=bB: v.tensor_tensor(
                    out=t1[:, 0:CH], in0=fl, in1=bB[:, 0:CH], op=ALU.add),
                    reads=[KF, kB], writes=[kt1], tag=f"q1{c}.{cc}")
                p.op("vector", lambda v, t1=t1, bA=bA: v.tensor_tensor(
                    out=t1[:, 0:CH], in0=t1[:, 0:CH], in1=bA[:, 0:CH],
                    op=ALU.mult),
                    reads=[kt1, kA], writes=[kt1], tag=f"q2{c}.{cc}")
                p.op("vector", lambda v, sl=sl, t1=t1: v.tensor_tensor(
                    out=sl, in0=sl, in1=t1[:, 0:CH], op=ALU.add),
                    reads=[kt1, KXC], writes=[KXC], tag=f"q3{c}.{cc}")

        p.dma("sync", [(o_x[c * 128:(c + 1) * 128, :],
                        xh[:, c * tok:(c + 1) * tok]) for c in range(NCH)],
              reads=[KXH] + [f"xh.c{c}" for c in range(tch)], tag="stx")

        p.emit()
    return nc



def make_consts():
    c = np.zeros((128, 136), np.float32)
    c[:, 0] = 1.0
    s = np.arange(128)[:, None]
    t = np.arange(128)[None, :]
    c[:, 2:130] = (s <= t).astype(np.float32)
    c[:, 130] = EPS_LN
    c[:, 131] = -0.5
    c[:, 132] = EPS_DEN
    c[:, 133] = -1.0
    c[:, 134] = float(np.finfo(np.float32).eps)
    c[:, 135] = 1.0 / 32.0
    return c


def col_tile(vec):
    return np.ascontiguousarray(vec.reshape(8, 128).T).astype(np.float32)


def make_sel16():
    s = np.zeros((16, 16 * 128), np.float32)
    for n in range(16):
        s[n, n * 128:(n + 1) * 128] = 1.0
    return s


_NC_CACHE = {}


def _get_nc(name, builder, tok):
    key = (name, tok)
    if key not in _NC_CACHE:
        _NC_CACHE[key] = builder(tok)
    return _NC_CACHE[key]


def kernel(**inputs):
    TOKPC = 2048
    x = np.asarray(inputs["L0_attn_x"], np.float32)
    xT = np.ascontiguousarray(np.transpose(x, (0, 2, 1)))
    consts = make_consts()
    sel16 = make_sel16()
    identm = np.eye(128, dtype=np.float32)

    def shard(i):
        b, s0 = i // 4, (i % 4) * TOKPC
        return np.ascontiguousarray(xT[b][:, s0:s0 + TOKPC])

    nc1 = _get_nc("l1", build_launch1, TOKPC)
    wq = np.ascontiguousarray(inputs["L0_attn_Wq"], np.float32)
    wk = np.ascontiguousarray(inputs["L0_attn_Wk"], np.float32)
    wv = np.ascontiguousarray(inputs["L0_attn_Wv"], np.float32)
    in1 = [{"xin": shard(i), "wq": wq, "wk": wk, "wv": wv,
            "ln1g": col_tile(np.asarray(inputs["L0_attn_ln1_g"], np.float32)),
            "ln1b": col_tile(np.asarray(inputs["L0_attn_ln1_b"], np.float32)),
            "consts": consts} for i in range(NCORES)]
    r1 = run_bass_kernel_spmd(nc1, in1, core_ids=list(range(NCORES))).results

    nc2 = _get_nc("l2", build_launch2, TOKPC)
    sall = np.ascontiguousarray(
        np.concatenate([r1[i]["stot"] for i in range(NCORES)], axis=1))
    def mk_mcol(i, rows):
        m = np.zeros((rows, NCORES), np.float32)
        g0 = (i // 4) * 4
        m[:, g0:i] = 1.0
        return m
    wo = np.ascontiguousarray(inputs["L0_attn_Wo"], np.float32)
    wf1 = np.ascontiguousarray(inputs["L0_attn_Wf1"], np.float32)
    wf2 = np.ascontiguousarray(inputs["L0_attn_Wf2"], np.float32)
    cbt = np.ascontiguousarray(np.asarray(inputs["L0_ffn_CB"], np.float32).T)
    in2 = [{"xin": shard(i), "abrows": r1[i]["abrows"], "qin": r1[i]["qout"],
            "ndin": r1[i]["numden"], "sall": sall, "mcol": mk_mcol(i, 128),
            "wo": wo, "wf1": wf1, "wf2": wf2,
            "ln2g": col_tile(np.asarray(inputs["L0_attn_ln2_g"], np.float32)),
            "ln2b": col_tile(np.asarray(inputs["L0_attn_ln2_b"], np.float32)),
            "cbt": cbt,
            "scb": col_tile(np.asarray(inputs["L0_ffn_SCB"], np.float32)),
            "fbias": col_tile(np.asarray(inputs["L0_ffn_bias"], np.float32)),
            "ident": identm, "consts": consts} for i in range(NCORES)]
    r2 = run_bass_kernel_spmd(nc2, in2, core_ids=list(range(NCORES))).results

    nc3 = _get_nc("l3", build_launch3, TOKPC)
    wxm = np.asarray(inputs["L1_attn_Wx"], np.float32)
    wbt = np.ascontiguousarray(wxm[:16].T)
    wct = np.ascontiguousarray(wxm[16:32].T)
    wdt = np.ascontiguousarray(wxm[32:].T)
    alog = np.ascontiguousarray(
        np.asarray(inputs["L1_attn_A_log"], np.float32)[None, :])
    dvec = col_tile(np.asarray(inputs["L1_attn_D"], np.float32))
    in3 = [{"xin": r2[i]["x2"], "wbt": wbt, "wct": wct, "wdt": wdt,
            "alog": alog, "dvec": dvec, "sel16": sel16, "ident": identm,
            "consts": consts} for i in range(NCORES)]
    r3 = run_bass_kernel_spmd(nc3, in3, core_ids=list(range(NCORES))).results

    nc4 = _get_nc("l4", build_launch4, TOKPC)
    accall = np.ascontiguousarray(
        np.concatenate([r3[i]["acc"] for i in range(NCORES)], axis=1))
    decall = np.ascontiguousarray(
        np.concatenate([r3[i]["dec"] for i in range(NCORES)], axis=1))
    woutT = np.ascontiguousarray(np.asarray(inputs["L1_attn_Wout"],
                                            np.float32).T)
    Am = np.asarray(inputs["L1_ffn_A"], np.float32)
    Bm = np.asarray(inputs["L1_ffn_B"], np.float32)
    atmat = np.ascontiguousarray(Am.transpose(0, 2, 1)).reshape(96, 100)
    bmat = np.ascontiguousarray(Bm.reshape(96, 1024))
    wgate = np.ascontiguousarray(np.asarray(inputs["L1_ffn_w_gate"],
                                            np.float32))
    wtwT = np.ascontiguousarray(np.asarray(inputs["L1_ffn_wt_w"],
                                           np.float32).T)
    wtbrow = np.ascontiguousarray(
        np.asarray(inputs["L1_ffn_wt_b"], np.float32)[None, :])
    wdwT = np.ascontiguousarray(np.asarray(inputs["L1_ffn_wd_w"],
                                           np.float32).T)
    wdb = col_tile(np.asarray(inputs["L1_ffn_wd_b"], np.float32))
    scalein = np.ascontiguousarray(
        np.asarray(inputs["L1_ffn_scale"], np.float32).reshape(1, 1))
    in4 = [{"xin": r2[i]["x2"], "y0in": r3[i]["y0"], "din": r3[i]["dout"],
            "cin": r3[i]["crow"], "accall": accall, "decall": decall,
            "mcol8": mk_mcol(i, 128), "alog": alog, "sel16": sel16,
            "woutT": woutT, "wgate": wgate, "atmat": atmat, "bmat": bmat,
            "wtwT": wtwT, "wtbrow": wtbrow, "wdwT": wdwT, "wdb": wdb,
            "scalein": scalein, "ident": identm, "consts": consts}
           for i in range(NCORES)]
    r4 = run_bass_kernel_spmd(nc4, in4, core_ids=list(range(NCORES))).results

    outT = np.zeros((2, D, 8192), np.float32)
    for i in range(NCORES):
        b, s0 = i // 4, (i % 4) * TOKPC
        outT[b][:, s0:s0 + TOKPC] = r4[i]["xout"]
    return np.ascontiguousarray(np.transpose(outT, (0, 2, 1)))


# version 22
# speedup vs baseline: 1.0275x; 1.0275x over previous
import contextlib
import numpy as np
import ml_dtypes

import concourse.bass as bass
import concourse.mybir as mybir
from concourse.bass_utils import run_bass_kernel_spmd

F32 = mybir.dt.float32
BF16 = mybir.dt.bfloat16
AF = mybir.ActivationFunctionType
ALU = mybir.AluOpType
AX = mybir.AxisListType

NCORES = 8
D = 1024
NCH = 8
NHEAD = 16
DH = 64
CH = 128
EPS_LN = 1e-5
EPS_DEN = 1e-6

BF = ml_dtypes.bfloat16



class Prog:
    COMPUTE_ENGINES = ("tensor", "vector", "scalar", "gpsimd")
    QUEUES = ("sync", "gpsimd", "scalar")

    def __init__(self, nc):
        self.nc = nc
        self.ops = []
        self.lastw = {}
        self.readers = {}
        self.qcount = {q: 0 for q in self.QUEUES}

    def _deps(self, reads, writes):
        deps = set()
        gb = self.lastw.get("__mem")
        if gb is not None:
            deps.add(gb)
        for k in reads:
            w = self.lastw.get(k)
            if w is not None:
                deps.add(w)
        for k in writes:
            w = self.lastw.get(k)
            if w is not None:
                deps.add(w)
            deps.update(self.readers.get(k, ()))
        return deps

    def _record(self, idx, reads, writes):
        for k in reads:
            self.readers.setdefault(k, []).append(idx)
        for k in writes:
            self.lastw[k] = idx
            self.readers[k] = []

    def op(self, engine, fn, reads=(), writes=(), tag=""):
        deps = self._deps(reads, writes)
        idx = len(self.ops)
        self.ops.append(dict(i=idx, eng=engine, kind="c", fn=fn, deps=deps,
                             tag=tag, inc=False))
        self._record(idx, reads, writes)
        return idx

    def barrier(self):
        last = {}
        for o in self.ops:
            last[o["eng"]] = o["i"]
        idxs = []
        for eng in ("sync", "gpsimd", "scalar", "tensor"):
            if eng in last or True:
                i = len(self.ops)
                deps = set(last.values())
                self.ops.append(dict(i=i, eng=eng, kind="c",
                                     fn=lambda e: e.nop(), deps=deps,
                                     tag=f"bar.{eng}", inc=False))
                idxs.append(i)
                last[eng] = i
        i = len(self.ops)
        self.ops.append(dict(i=i, eng="vector", kind="c",
                             fn=lambda e: e.nop(), deps=set(idxs),
                             tag="bar.vector", inc=False))
        self.lastw["__mem"] = i
        self.readers["__mem"] = []

    def dma(self, queue, transfers, reads=(), writes=(), tag=""):
        deps = self._deps(reads, writes)
        idx = len(self.ops)
        self.qcount[queue] += len(transfers)
        target = 16 * self.qcount[queue]
        self.ops.append(dict(i=idx, eng=queue, kind="dma", transfers=transfers,
                             deps=deps, tag=tag, inc=False))
        fidx = idx + 1
        self.ops.append(dict(i=fidx, eng=queue, kind="fence", deps=set(),
                             qtarget=target, tag=tag + ".fence", inc=False))
        for k in reads:
            self.readers.setdefault(k, []).append(fidx)
        for k in writes:
            self.lastw[k] = fidx
            self.readers[k] = []
        return fidx

    def emit(self):
        nc = self.nc
        ops = self.ops

        def waitable(d):
            return d + 1 if ops[d]["kind"] == "dma" else d

        for o in ops:
            o["deps"] = {waitable(d) for d in o["deps"] if waitable(d) != o["i"]}

        for o in ops:
            for d in o["deps"]:
                pd = ops[d]
                if pd["kind"] == "c" and pd["eng"] != o["eng"]:
                    pd["inc"] = True

        streams = {}
        for o in ops:
            streams.setdefault(o["eng"], []).append(o)
        for eng, lst in streams.items():
            c = 0
            for pos, o in enumerate(lst):
                if o.get("inc"):
                    c += 1
                o["cumval"] = c
                o["pos"] = pos

        with contextlib.ExitStack() as es:
            qsem = {q: es.enter_context(nc.semaphore(f"q_{q}"))
                    for q in self.QUEUES}
            csem = {e: es.enter_context(nc.semaphore(f"c_{e}"))
                    for e in self.COMPUTE_ENGINES + ("sync",)}
            block = es.enter_context(nc.Block())

            def emit_stream(eng_obj, eng_name):
                lst = streams.get(eng_name, [])
                high = {}
                drained = -1
                for o in lst:
                    need_drain = False
                    for d in sorted(o["deps"]):
                        pd = ops[d]
                        if pd["eng"] == eng_name:
                            if (pd["kind"] == "c" and eng_name != "tensor"
                                    and pd["pos"] > drained):
                                need_drain = True
                            continue
                        if pd["kind"] == "fence":
                            sem, val = qsem[pd["eng"]], pd["qtarget"]
                        else:
                            sem, val = csem[pd["eng"]], pd["cumval"]
                        if high.get(id(sem), 0) < val:
                            eng_obj.wait_ge(sem, val)
                            high[id(sem)] = val
                    if need_drain:
                        eng_obj.drain()
                        drained = o["pos"] - 1
                    if o["kind"] == "c":
                        inst = o["fn"](eng_obj)
                        if o.get("inc"):
                            if inst is None:
                                raise RuntimeError(f"op {o['tag']}: no inst")
                            inst.then_inc(csem[eng_name], 1)
                    elif o["kind"] == "dma":
                        for tr in o["transfers"]:
                            kw = tr[2] if len(tr) > 2 else {}
                            eng_obj.dma_start(out=tr[0], in_=tr[1], **kw) \
                                .then_inc(qsem[eng_name], 16)
                    else:
                        eng_obj.wait_ge(qsem[eng_name], o["qtarget"])

            for name in ("sync", "gpsimd", "vector", "scalar", "tensor"):
                getattr(block, name)(lambda e, n=name: emit_stream(e, n))


class Ctx:
    def __init__(self, nc, tok):
        self.nc = nc
        self.p = Prog(nc)
        self.es = contextlib.ExitStack()
        self.tok = tok
        self.tss = min(512, tok)
        self.nts = tok // self.tss
        self.tch = tok // CH
        self.uid = 0

    def sb(self, name, shape, dtype):
        es = getattr(self, "alloc_es", None) or self.es
        return es.enter_context(
            self.nc.sbuf_tensor(name, list(shape), dtype))

    def ps(self, name, shape, dtype=F32):
        return self.es.enter_context(
            self.nc.psum_tensor(name, list(shape), dtype))

    def name(self, base):
        self.uid += 1
        return f"{base}{self.uid}"

    def prealloc_ln(self):
        self.shared("lnxb", (128, self.tss), BF16)
        self.shared("lnsq", (128, self.tss), BF16)
        self.shared("lns1", (1, self.tok), F32)
        self.shared("lns2", (1, self.tok), F32)
        self.shared("lnt1", (128, self.tss), F32)
        self.shared("lngt", (128, self.tss), F32)

    def shared(self, name, shape, dtype):
        cache = getattr(self, "_shared", None)
        if cache is None:
            cache = self._shared = {}
        if name not in cache:
            if getattr(self, "scoped_shared", False):
                es = getattr(self, "alloc_es", None) or self.es
            else:
                es = self.es
            cache[name] = es.enter_context(
                self.nc.sbuf_tensor(name, list(shape), dtype))
        return cache[name]



def emit_ln_stats(ctx, src, src_key, rows, rows_key, pA, pB, bkA, bkB,
                  ones_col, eps=EPS_LN, tag="ln"):
    p = ctx.p
    tok, tss, nts = ctx.tok, ctx.tss, ctx.nts
    xb = ctx.shared("lnxb", (128, tss), BF16)
    sq = ctx.shared("lnsq", (128, tss), BF16)
    s1 = ctx.shared("lns1", (1, tok), F32)
    s2 = ctx.shared("lns2", (1, tok), F32)
    kxb, ksq, ks1, ks2 = "lnxb", "lnsq", "lns1", "lns2"
    for ts in range(nts):
        t0 = ts * tss
        for c in range(NCH):
            sl = src[:, c * tok + t0: c * tok + t0 + tss]
            p.op("vector", lambda v, sl=sl: v.tensor_copy(out=xb[:], in_=sl),
                 reads=[src_key], writes=[kxb], tag=f"{tag}.cast")
            p.op("scalar", lambda s, sl=sl: s.activation(
                out=sq[:], in_=sl, func=AF.Square),
                reads=[src_key], writes=[ksq], tag=f"{tag}.sq")
            p.op("tensor", lambda t, st=(c == 0), sp=(c == NCH - 1): t.matmul(
                pA[0:1, 0:tss], ones_col[:, 0:1], xb[:], start=st, stop=sp),
                reads=[kxb, "ones_col"], writes=[bkA], tag=f"{tag}.mm1")
            p.op("tensor", lambda t, st=(c == 0), sp=(c == NCH - 1): t.matmul(
                pB[0:1, 0:tss], ones_col[:, 0:1], sq[:], start=st, stop=sp),
                reads=[ksq, "ones_col"], writes=[bkB], tag=f"{tag}.mm2")
        p.op("vector", lambda v, t0=t0: v.tensor_copy(
            out=s1[:, t0:t0 + tss], in_=pA[0:1, 0:tss]),
            reads=[bkA], writes=[ks1], tag=f"{tag}.ev1")
        p.op("vector", lambda v, t0=t0: v.tensor_copy(
            out=s2[:, t0:t0 + tss], in_=pB[0:1, 0:tss]),
            reads=[bkB], writes=[ks2], tag=f"{tag}.ev2")
    a_row, b_row = rows["a"], rows["b"]
    ka, kb = rows_key + ".a", rows_key + ".b"
    p.op("vector", lambda v: v.tensor_scalar(
        out=b_row[:], in0=s1[:], scalar1=-1.0 / D, scalar2=None, op0=ALU.mult),
        reads=[ks1], writes=[kb], tag=f"{tag}.bneg")
    p.op("vector", lambda v: v.tensor_scalar(
        out=a_row[:], in0=s2[:], scalar1=1.0 / D, scalar2=None, op0=ALU.mult),
        reads=[ks2], writes=[ka], tag=f"{tag}.q")
    p.op("vector", lambda v: v.tensor_tensor(
        out=s1[:], in0=b_row[:], in1=b_row[:], op=ALU.mult),
        reads=[kb], writes=[ks1], tag=f"{tag}.msq")
    p.op("vector", lambda v: v.tensor_tensor(
        out=a_row[:], in0=a_row[:], in1=s1[:], op=ALU.subtract),
        reads=[ka, ks1], writes=[ka], tag=f"{tag}.var")
    csb, ckey = ctx.csb, ctx.ckey
    p.op("scalar", lambda s: s.activation(
        out=a_row[:], in_=a_row[:], func=AF.Ln, bias=csb[0:1, 130:131]),
        reads=[ka, ckey], writes=[ka], tag=f"{tag}.logv")
    p.op("scalar", lambda s: s.activation(
        out=a_row[:], in_=a_row[:], func=AF.Exp, scale=csb[0:1, 131:132]),
        reads=[ka, ckey], writes=[ka], tag=f"{tag}.rstd")


def emit_ln_apply(ctx, src, src_key, rows, rows_key, ones_row32,
                  pA, pB, bkA, bkB, outs, tag, or_key="consts"):
    p = ctx.p
    tok, tss, nts = ctx.tok, ctx.tss, ctx.nts
    t1 = ctx.shared("lnt1", (128, tss), F32)
    kt1 = "lnt1"
    for ts in range(nts):
        t0 = ts * tss
        p.op("tensor", lambda t, t0=t0: t.matmul(
            pA[:, 0:tss], ones_row32[0:1, :], rows["a"][:, t0:t0 + tss],
            start=True, stop=True),
            reads=[rows_key + ".a", or_key], writes=[bkA],
            tag=f"{tag}.bcA")
        p.op("tensor", lambda t, t0=t0: t.matmul(
            pB[:, 0:tss], ones_row32[0:1, :], rows["b"][:, t0:t0 + tss],
            start=True, stop=True),
            reads=[rows_key + ".b", or_key], writes=[bkB],
            tag=f"{tag}.bcB")
        for c in range(NCH):
            sl = src[:, c * tok + t0: c * tok + t0 + tss]
            p.op("vector", lambda v, sl=sl: v.tensor_tensor(
                out=t1[:], in0=sl, in1=pB[:, 0:tss], op=ALU.add),
                reads=[src_key, bkB], writes=[kt1], tag=f"{tag}.t1")
            for oi, o in enumerate(outs):
                osl = o["tile"][:, c * tok + t0: c * tok + t0 + tss]
                if o.get("gb") is None:
                    p.op("vector", lambda v, osl=osl: v.tensor_tensor(
                        out=osl, in0=t1[:], in1=pA[:, 0:tss], op=ALU.mult),
                        reads=[kt1, bkA], writes=[o["key"]],
                        tag=f"{tag}.ap{oi}")
                else:
                    g_col, b_col, gb_key = o["gb"]
                    tmp, ktmp = o["tmp"], o["tmpkey"]
                    p.op("vector", lambda v, tmp=tmp: v.tensor_tensor(
                        out=tmp[:], in0=t1[:], in1=pA[:, 0:tss], op=ALU.mult),
                        reads=[kt1, bkA], writes=[ktmp], tag=f"{tag}.apg{oi}")
                    p.op("vector", lambda v, osl=osl, c=c: v.tensor_scalar(
                        out=osl, in0=tmp[:], scalar1=g_col[:, c:c + 1],
                        scalar2=b_col[:, c:c + 1], op0=ALU.mult, op1=ALU.add),
                        reads=[ktmp, gb_key], writes=[o["key"]],
                        tag=f"{tag}.gb{oi}")



def build_launch1(tok):
    nc = bass.Bass("TRN2", target_bir_lowering=False)
    tch = tok // CH
    xin = nc.dram_tensor("xin", [D, tok], F32, kind="ExternalInput").ap()
    wq = nc.dram_tensor("wq", [D, D], F32, kind="ExternalInput").ap()
    wk = nc.dram_tensor("wk", [D, D], F32, kind="ExternalInput").ap()
    wv = nc.dram_tensor("wv", [D, D], F32, kind="ExternalInput").ap()
    ln1g = nc.dram_tensor("ln1g", [128, 8], F32, kind="ExternalInput").ap()
    ln1b = nc.dram_tensor("ln1b", [128, 8], F32, kind="ExternalInput").ap()
    consts = nc.dram_tensor("consts", [128, 136], F32, kind="ExternalInput").ap()
    o_abrows = nc.dram_tensor("abrows", [4, tok], F32, kind="ExternalOutput").ap()
    o_qfull = nc.dram_tensor("qout", [128, 8 * tok], BF16,
                             kind="ExternalOutput").ap()
    o_numden = nc.dram_tensor("numden", [tok, NHEAD * 65], F32,
                              kind="ExternalOutput").ap()
    o_stot = nc.dram_tensor("stot", [128, 8 * 65], F32,
                            kind="ExternalOutput").ap()
    o_dbg = nc.dram_tensor("dbg", [128, NHEAD * 65], F32,
                           kind="ExternalOutput").ap()
    o_dbg2 = nc.dram_tensor("dbg_mst", [128, NHEAD * CH], BF16,
                            kind="ExternalOutput").ap()
    o_dbg3 = nc.dram_tensor("dbg_vtm", [128, NHEAD * 65], BF16,
                            kind="ExternalOutput").ap()
    o_dbg4 = nc.dram_tensor("dbg_kfm", [64, NHEAD * CH], BF16,
                            kind="ExternalOutput").ap()
    o_dbg5 = nc.dram_tensor("dbg_ktm", [128, NHEAD * DH], BF16,
                            kind="ExternalOutput").ap()

    ctx = Ctx(nc, tok)
    p = ctx.p
    tss, nts = ctx.tss, ctx.nts

    with ctx.es:
        xt = ctx.sb("xt", (128, NCH * tok), F32)
        xab = ctx.sb("xab", (128, NCH * tok), BF16)
        qfull = ctx.sb("qfull", (128, 8 * tok), BF16)
        ones_col = ctx.sb("ones_col", (128, 1), BF16)
        mask = ctx.sb("mask", (128, CH), BF16)
        consts_sb = ctx.sb("consts_sb", (128, 136), F32)
        ctx.csb = consts_sb
        ctx.ckey = "consts"
        ctx.prealloc_ln()
        gtmp = ctx.shared("lngt", (128, tss), F32)
        g1 = ctx.sb("g1", (128, 8), F32)
        b1 = ctx.sb("b1", (128, 8), F32)
        s_run = ctx.sb("s_run", (128, 8 * 65), F32)
        s_pref = ctx.sb("s_pref", (128, 8 * 65), BF16)
        kfm = ctx.sb("kfm", (128, 8 * CH), BF16)
        ktm = ctx.sb("ktm", (128, NHEAD * DH), BF16)
        vtm = ctx.sb("vtm", (128, NHEAD * 65), BF16)
        mst = ctx.sb("mst", (128, NHEAD * CH), BF16)
        nst = ctx.sb("nst", (128, NHEAD * 65), F32)
        phit = ctx.sb("phit", (128, 512), F32)
        phie = ctx.sb("phie", (128, 512), F32)

        pA = ctx.ps("pA", (128, 512))
        pB = ctx.ps("pB", (128, 512))
        pC = ctx.ps("pC", (128, 512))
        pD = ctx.ps("pD", (128, 512))
        pE = ctx.ps("pE", (128, 512))
        pF = ctx.ps("pF", (128, 512))
        pG = ctx.ps("pG", (128, 512))
        pH = ctx.ps("pH", (128, 512))

        KX, KXAB, KQ = "xt", "xab", "qfull"
        KW, KWV = "wsb", "wvsb"
        KSRUN, KSPREF = "srun", "spref"
        KKFM, KKTM, KVTM, KMST, KNST = "kfm", "ktm", "vtm", "mst", "nst"
        KPH1, KPH2 = "phit", "phie"
        BA, BB, BC, BD, BE, BFb, BG, BH = ("b_A", "b_B", "b_C", "b_D", "b_E", "b_F", "b_G", "b_H")

        p.dma("sync", [(xt[:, c * tok:(c + 1) * tok],
                        xin[c * 128:(c + 1) * 128, :]) for c in range(NCH)],
              writes=[KX], tag="ldx")
        p.dma("sync", [(consts_sb[:], consts[:]),
                       (g1[:], ln1g[:]), (b1[:], ln1b[:])],
              writes=["consts", "g1b1"], tag="ldc")
        p.op("vector", lambda v: v.tensor_copy(
            out=ones_col[:], in_=consts_sb[:, 0:1]),
            reads=["consts"], writes=["ones_col"], tag="mkones")
        p.op("vector", lambda v: v.tensor_copy(
            out=mask[:], in_=consts_sb[:, 2:130]),
            reads=["consts"], writes=["mask"], tag="mkmask")
        ones_row32 = consts_sb[0:1, 2:130]

        with contextlib.ExitStack() as esL:
            ctx.alloc_es = esL
            rows0 = {"a": ctx.sb("r0a", (1, tok), F32),
                     "b": ctx.sb("r0b", (1, tok), F32)}
            rows1 = {"a": ctx.sb("r1a", (1, tok), F32),
                     "b": ctx.sb("r1b", (1, tok), F32)}
            emit_ln_stats(ctx, xt, KX, rows0, "r0", pA, pB, BA, BB, ones_col,
                          tag="ln0")
            emit_ln_apply(ctx, xt, KX, rows0, "r0", ones_row32, pA, pB, BA, BB,
                          [dict(tile=xt, key=KX)], tag="ln0ap")
            emit_ln_stats(ctx, xt, KX, rows1, "r1", pA, pB, BA, BB, ones_col,
                          tag="ln1")
            emit_ln_apply(ctx, xt, KX, rows1, "r1", ones_row32, pA, pB, BA, BB,
                          [dict(tile=xab, key=KXAB, gb=(g1, b1, "g1b1"),
                                tmp=gtmp, tmpkey="lngt")], tag="ln1ap")
            p.dma("sync", [(o_abrows[0:1, :], rows0["a"][:]),
                           (o_abrows[1:2, :], rows0["b"][:]),
                           (o_abrows[2:3, :], rows1["a"][:]),
                           (o_abrows[3:4, :], rows1["b"][:])],
                  reads=["r0.a", "r0.b", "r1.a", "r1.b"], tag="stab")
        ctx.alloc_es = None
        p.barrier()
        wsb = ctx.sb("wsb", (128, NCH * D), BF16)
        wvsb = ctx.sb("wvsb", (128, NCH * D), BF16)
        p.dma("gpsimd",
              [(wsb[:, c * D:(c + 1) * D], wq[c * 128:(c + 1) * 128, :])
               for c in range(NCH)], writes=[KW], tag="ldwq")

        def phi_to(bank, bkey, dsts, dkey, nsz, tag, full=False):
            p.op("vector", lambda v: v.tensor_scalar(
                out=phit[:, 0:nsz], in0=bank[:, 0:nsz], scalar1=0.0,
                scalar2=None, op0=ALU.min),
                reads=[bkey], writes=[KPH1], tag=f"{tag}.min")
            p.op("scalar", lambda s: s.activation(
                out=phie[:, 0:nsz], in_=phit[:, 0:nsz], func=AF.Exp),
                reads=[KPH1], writes=[KPH2], tag=f"{tag}.exp")
            if full:
                dst = dsts[0][0]
                p.op("vector", lambda v, dst=dst: v.scalar_tensor_tensor(
                    out=dst, in0=bank[:, 0:nsz], scalar=0.0,
                    in1=phie[:, 0:nsz], op0=ALU.max, op1=ALU.add),
                    reads=[bkey, KPH2], writes=[dkey], tag=f"{tag}.sttF")
                return
            for (dst, lo) in dsts:
                p.op("vector", lambda v, dst=dst, lo=lo: v.scalar_tensor_tensor(
                    out=dst, in0=bank[lo:lo + 64, 0:nsz], scalar=0.0,
                    in1=phie[lo:lo + 64, 0:nsz], op0=ALU.max, op1=ALU.add),
                    reads=[bkey, KPH2], writes=[dkey], tag=f"{tag}.stt{lo}")

        for m in range(NCH):
            for ts in range(nts):
                bank, bkey = ((pC, BC) if (m * nts + ts) % 2 == 0 else (pD, BD))
                t0 = ts * tss
                for k in range(NCH):
                    lhs = wsb[:, k * D + m * 128:(k * D + (m + 1) * 128)]
                    rhs = xab[:, k * tok + t0: k * tok + t0 + tss]
                    p.op("tensor", lambda t, lhs=lhs, rhs=rhs, k=k, bank=bank:
                         t.matmul(bank[:, 0:tss], lhs, rhs, start=(k == 0),
                                  stop=(k == NCH - 1)),
                         reads=[KW, KXAB], writes=[bkey], tag=f"qp{m}.{ts}")
                phi_to(bank, bkey,
                       [(qfull[:, m * tok + t0: m * tok + t0 + tss], 0)],
                       KQ, tss, f"qphi{m}.{ts}", full=True)
        p.dma("sync", [(o_qfull[:, :], qfull[:, :])], reads=[KQ], tag="stq")

        p.dma("gpsimd",
              [(wsb[:, c * D:(c + 1) * D], wk[c * 128:(c + 1) * 128, :])
               for c in range(NCH)], writes=[KW], tag="ldwk")
        p.dma("gpsimd",
              [(wvsb[:, c * D:(c + 1) * D], wv[c * 128:(c + 1) * 128, :])
               for c in range(NCH)], writes=[KWV], tag="ldwv")

        p.op("vector", lambda v: v.memset(s_run[:], 0.0), writes=[KSRUN],
             tag="s0")

        for c in range(tch):
            t0 = c * CH
            for m in range(NCH):
                bank, bkey = (pC, BC) if m % 2 == 0 else (pD, BD)
                for k in range(NCH):
                    lhs = wsb[:, k * D + m * 128: k * D + (m + 1) * 128]
                    rhs = xab[:, k * tok + t0: k * tok + t0 + CH]
                    p.op("tensor", lambda t, lhs=lhs, rhs=rhs, k=k, bank=bank:
                         t.matmul(bank[:, 0:CH], lhs, rhs, start=(k == 0),
                                  stop=(k == NCH - 1)),
                         reads=[KW, KXAB], writes=[bkey], tag=f"kp{c}.{m}")
                phi_to(bank, bkey, [(kfm[:, m * CH:(m + 1) * CH], 0)],
                       KKFM, CH, f"kphi{c}.{m}", full=True)
            for half in range(2):
                bank, bkey = (pE, BE) if half == 0 else (pF, BFb)
                for k in range(NCH):
                    lhs = xab[:, k * tok + t0: k * tok + t0 + CH]
                    rhs = wsb[:, k * D + half * 512: k * D + half * 512 + 512]
                    p.op("tensor", lambda t, lhs=lhs, rhs=rhs, k=k, bank=bank:
                         t.matmul(bank[:, 0:512], lhs, rhs, start=(k == 0),
                                  stop=(k == NCH - 1)),
                         reads=[KW, KXAB], writes=[bkey], tag=f"ktp{c}.{half}")
                p.op("vector", lambda v, bank=bank: v.tensor_scalar(
                    out=phit[:, 0:512], in0=bank[:, 0:512], scalar1=0.0,
                    scalar2=None, op0=ALU.min),
                    reads=[bkey], writes=[KPH1], tag=f"ktm{c}.{half}.min")
                p.op("scalar", lambda s: s.activation(
                    out=phie[:, 0:512], in_=phit[:, 0:512], func=AF.Exp),
                    reads=[KPH1], writes=[KPH2], tag=f"ktm{c}.{half}.exp")
                p.op("vector", lambda v, bank=bank, half=half:
                     v.scalar_tensor_tensor(
                         out=ktm[:, half * 512:(half + 1) * 512],
                         in0=bank[:, 0:512], scalar=0.0, in1=phie[:, 0:512],
                         op0=ALU.max, op1=ALU.add),
                     reads=[bkey, KPH2], writes=[KKTM], tag=f"ktm{c}.{half}.stt")
            for half in range(2):
                bank, bkey = (pE, BE) if half == 0 else (pF, BFb)
                for k in range(NCH):
                    lhs = xab[:, k * tok + t0: k * tok + t0 + CH]
                    rhs = wvsb[:, k * D + half * 512: k * D + half * 512 + 512]
                    p.op("tensor", lambda t, lhs=lhs, rhs=rhs, k=k, bank=bank:
                         t.matmul(bank[:, 0:512], lhs, rhs, start=(k == 0),
                                  stop=(k == NCH - 1)),
                         reads=[KWV, KXAB], writes=[bkey], tag=f"vp{c}.{half}")
                for hh in range(8):
                    h = half * 8 + hh
                    p.op("vector", lambda v, bank=bank, hh=hh, h=h:
                         v.tensor_copy(out=vtm[:, h * 65: h * 65 + 64],
                                       in_=bank[:, hh * 64:(hh + 1) * 64]),
                         reads=[bkey], writes=[KVTM], tag=f"vev{c}.{h}")
            for h in range(NHEAD):
                p.op("vector", lambda v, h=h: v.memset(
                    vtm[:, h * 65 + 64: h * 65 + 65], 1.0),
                    writes=[KVTM], tag=f"vone{c}.{h}")

            p.op("vector", lambda v: v.tensor_copy(out=s_pref[:], in_=s_run[:]),
                 reads=[KSRUN], writes=[KSPREF], tag=f"spref{c}")

            for h in range(NHEAD):
                hp, be = h // 2, 64 * (h % 2)
                sb_bank, sbk = (pC, BC) if h % 2 == 0 else (pD, BD)
                p.op("tensor", lambda t, hp=hp, be=be, sb_bank=sb_bank, t0=t0:
                     t.matmul(
                         sb_bank[:, 0:CH],
                         kfm[be:be + 64, hp * CH:(hp + 1) * CH],
                         qfull[be:be + 64, hp * tok + t0: hp * tok + t0 + CH],
                         start=True, stop=True),
                     reads=[KKFM, KQ], writes=[sbk], tag=f"sc{c}.{h}")
                p.op("vector", lambda v, h=h, sb_bank=sb_bank: v.tensor_tensor(
                    out=mst[:, h * CH:(h + 1) * CH], in0=sb_bank[:, 0:CH],
                    in1=mask[:], op=ALU.mult),
                    reads=[sbk, "mask"], writes=[KMST], tag=f"msk{c}.{h}")
                nb, nbk = (pE, BE) if h % 2 == 0 else (pF, BFb)
                ib, ibk = (pG, BG) if h % 2 == 0 else (pH, BH)
                p.op("tensor", lambda t, h=h, nb=nb: t.matmul(
                    nb[:, 0:65], mst[:, h * CH:(h + 1) * CH],
                    vtm[:, h * 65:(h + 1) * 65], start=True, stop=True),
                    reads=[KMST, KVTM], writes=[nbk], tag=f"numi{c}.{h}")
                p.op("tensor", lambda t, hp=hp, be=be, ib=ib, t0=t0: t.matmul(
                    ib[:, 0:65],
                    qfull[be:be + 64, hp * tok + t0: hp * tok + t0 + CH],
                    s_pref[be:be + 64, hp * 65:(hp + 1) * 65],
                    start=True, stop=True),
                    reads=[KQ, KSPREF], writes=[ibk], tag=f"numl{c}.{h}")
                p.op("vector", lambda v, h=h, nb=nb: v.tensor_copy(
                    out=nst[:, h * 65:(h + 1) * 65], in_=nb[:, 0:65]),
                    reads=[nbk], writes=[KNST], tag=f"numev{c}.{h}")
                p.op("vector", lambda v, h=h, ib=ib: v.tensor_tensor(
                    out=nst[:, h * 65:(h + 1) * 65],
                    in0=nst[:, h * 65:(h + 1) * 65],
                    in1=ib[:, 0:65], op=ALU.add),
                    reads=[ibk, KNST], writes=[KNST], tag=f"numad{c}.{h}")
                stb, stk = (pA, BA) if h % 2 == 0 else (pB, BB)
                p.op("tensor", lambda t, h=h, stb=stb: t.matmul(
                    stb[0:64, 0:65], ktm[:, h * DH:(h + 1) * DH],
                    vtm[:, h * 65:(h + 1) * 65], start=True, stop=True),
                    reads=[KKTM, KVTM], writes=[stk], tag=f"sup{c}.{h}")
                p.op("vector", lambda v, hp=hp, be=be, stb=stb: v.tensor_tensor(
                    out=s_run[be:be + 64, hp * 65:(hp + 1) * 65],
                    in0=s_run[be:be + 64, hp * 65:(hp + 1) * 65],
                    in1=stb[0:64, 0:65], op=ALU.add),
                    reads=[stk, KSRUN], writes=[KSRUN], tag=f"sacc{c}.{h}")
            p.dma("sync", [(o_numden[t0:t0 + CH, :], nst[:, :])],
                  reads=[KNST], tag=f"stnd{c}")
            if c == 0:
                p.dma("sync", [(o_dbg[:, :], nst[:, :]),
                               (o_dbg2[:, :], mst[:, :]),
                               (o_dbg3[:, :], vtm[:, :]),
                               (o_dbg4[:, :], kfm[:, :]),
                               (o_dbg5[:, :], ktm[:, :])],
                      reads=[KNST, KMST, KVTM, KKFM, KKTM], tag="stdbg")

        p.dma("sync", [(o_stot[:], s_run[:])], reads=[KSRUN], tag="ststot")

        p.emit()
    return nc




def build_launch2(tok):
    nc = bass.Bass("TRN2", target_bir_lowering=False)
    tch = tok // CH
    xin = nc.dram_tensor("xin", [D, tok], F32, kind="ExternalInput").ap()
    abrows = nc.dram_tensor("abrows", [4, tok], F32, kind="ExternalInput").ap()
    qin = nc.dram_tensor("qin", [128, 8 * tok], BF16, kind="ExternalInput").ap()
    ndin = nc.dram_tensor("ndin", [tok, NHEAD * 65], F32, kind="ExternalInput").ap()
    sall = nc.dram_tensor("sall", [128, NCORES * 8 * 65], F32,
                          kind="ExternalInput").ap()
    mcol = nc.dram_tensor("mcol", [128, NCORES], F32, kind="ExternalInput").ap()
    wo = nc.dram_tensor("wo", [D, D], F32, kind="ExternalInput").ap()
    wf1 = nc.dram_tensor("wf1", [D, 4 * D], F32, kind="ExternalInput").ap()
    wf2 = nc.dram_tensor("wf2", [4 * D, D], F32, kind="ExternalInput").ap()
    ln2g = nc.dram_tensor("ln2g", [128, 8], F32, kind="ExternalInput").ap()
    ln2b = nc.dram_tensor("ln2b", [128, 8], F32, kind="ExternalInput").ap()
    cbt = nc.dram_tensor("cbt", [D, D], F32, kind="ExternalInput").ap()
    scb = nc.dram_tensor("scb", [128, 8], F32, kind="ExternalInput").ap()
    fbias = nc.dram_tensor("fbias", [128, 8], F32, kind="ExternalInput").ap()
    ident = nc.dram_tensor("ident", [128, 128], F32, kind="ExternalInput").ap()
    consts = nc.dram_tensor("consts", [128, 136], F32, kind="ExternalInput").ap()
    o_x2 = nc.dram_tensor("x2", [D, tok], F32, kind="ExternalOutput").ap()

    ctx = Ctx(nc, tok)
    p = ctx.p
    tss, nts = ctx.tss, ctx.nts

    with ctx.es:
        consts_sb = ctx.sb("consts_sb", (128, 136), F32)
        ctx.csb = consts_sb
        ctx.ckey = "consts"
        ones_col = ctx.sb("ones_col", (128, 1), BF16)
        identb = ctx.sb("identb", (128, 128), BF16)
        xh = ctx.sb("xh", (128, NCH * tok), F32)
        ctx.prealloc_ln()
        gtmp = ctx.shared("lngt", (128, tss), F32)
        spre = ctx.sb("spre", (128, 8 * 65), BF16)
        mc = ctx.sb("mc", (128, NCORES), F32)
        g2 = ctx.sb("g2", (128, 8), F32)
        b2 = ctx.sb("b2", (128, 8), F32)
        t1b = ctx.sb("t1b", (128, tss), F32)
        xs = ctx.sb("xs", (128, tss), F32)

        pS1 = ctx.ps("pS1", (128, 512))
        pS2 = ctx.ps("pS2", (128, 512))
        pT1 = ctx.ps("pT1", (128, 1024), BF16)
        pT2 = ctx.ps("pT2", (128, 1024), BF16)
        pW1 = ctx.ps("pW1", (128, 512))
        pW2 = ctx.ps("pW2", (128, 512))

        KXH = "xh"
        B1, B2k, BT1, BT2, BW1, BW2 = "b_S1", "b_S2", "b_T1", "b_T2", "b_W1", "b_W2"

        p.dma("sync", [(consts_sb[:], consts[:])], writes=["consts"], tag="ldc")
        p.op("vector", lambda v: v.tensor_copy(
            out=ones_col[:], in_=consts_sb[:, 0:1]),
            reads=["consts"], writes=["ones_col"], tag="mkones")
        ones_row32 = consts_sb[0:1, 2:130]
        p.dma("gpsimd", [(identb[:], ident[:])], writes=["identb"], tag="ldid")
        p.dma("sync", [(xh[:, c * tok:(c + 1) * tok],
                        xin[c * 128:(c + 1) * 128, :]) for c in range(NCH)],
              writes=[KXH], tag="ldx")
        p.dma("sync", [(mc[:], mcol[:]), (g2[:], ln2g[:]), (b2[:], ln2b[:])],
              writes=["mc", "g2b2"], tag="ldr")

        with contextlib.ExitStack() as es2:
            ctx.alloc_es = es2
            sall_sb = ctx.sb("sall_sb", (128, NCORES * 8 * 65), F32)
            sacc = ctx.sb("sacc", (128, 8 * 65), F32)
            p.dma("sync", [(sall_sb[:], sall[:])], writes=["sall"], tag="ldsall")
            p.op("vector", lambda v: v.memset(sacc[:], 0.0), writes=["sacc"],
                 tag="sacc0")
            W = 8 * 65
            for r in range(NCORES):
                p.op("vector", lambda v, r=r: v.scalar_tensor_tensor(
                    out=sacc[:], in0=sall_sb[:, r * W:(r + 1) * W],
                    scalar=mc[:, r:r + 1], in1=sacc[:],
                    op0=ALU.mult, op1=ALU.add),
                    reads=["sall", "mc", "sacc"], writes=["sacc"], tag=f"sm{r}")
            p.op("vector", lambda v: v.tensor_copy(out=spre[:], in_=sacc[:]),
                 reads=["sacc"], writes=["spre"], tag="spreb")
        ctx.alloc_es = None
        p.barrier()

        with contextlib.ExitStack() as esB:
            ctx.alloc_es = esB
            rows = {"a": ctx.sb("rowa", (1, tok), F32),
                    "b": ctx.sb("rowb", (1, tok), F32)}
            p.dma("sync", [(rows["a"][:], abrows[0:1, :]),
                           (rows["b"][:], abrows[1:2, :])],
                  writes=["r.a", "r.b"], tag="ldab")
            emit_ln_apply(ctx, xh, KXH, rows, "r", ones_row32, pS1, pS2,
                          B1, B2k, [dict(tile=xh, key=KXH)], tag="l2ln0")
        ctx.alloc_es = None
        p.barrier()

        with contextlib.ExitStack() as esC:
            ctx.alloc_es = esC
            wosb = ctx.sb("wosb", (128, NCH * D), BF16)
            qch = ctx.sb("qch", (128, 8 * CH), BF16)
            ndch = ctx.sb("ndch", (128, NHEAD * 65), F32)
            nsum = ctx.sb("nsum", (128, 65), F32)
            dclip = ctx.sb("dclip", (128, 1), F32)
            rden = ctx.sb("rden", (128, 1), F32)
            attm = ctx.sb("attm", (128, NCH * CH), BF16)
            attf = ctx.sb("attf", (128, NCH * CH), BF16)
            p.dma("gpsimd",
                  [(wosb[:, c * D:(c + 1) * D], wo[c * 128:(c + 1) * 128, :])
                   for c in range(NCH)], writes=["wo"], tag="ldwo")
            for c in range(tch):
                t0 = c * CH
                p.dma("sync", [(qch[:], qin[:, :].rearrange(
                    "p (h t) -> p h t", h=8)[:, :, t0:t0 + CH])],
                      writes=["qch"], tag=f"ldq{c}")
                p.dma("sync", [(ndch[:], ndin[t0:t0 + CH, :])],
                      writes=["ndch"], tag=f"ldnd{c}")
                for h in range(NHEAD):
                    hp, be = h // 2, 64 * (h % 2)
                    bank, bk = (pS1, B1) if h % 2 == 0 else (pS2, B2k)
                    p.op("tensor", lambda t, hp=hp, be=be, bank=bank: t.matmul(
                        bank[:, 0:65], qch[be:be + 64, hp * CH:(hp + 1) * CH],
                        spre[be:be + 64, hp * 65:(hp + 1) * 65],
                        start=True, stop=True),
                        reads=["qch", "spre"], writes=[bk], tag=f"int{c}.{h}")
                    p.op("vector", lambda v, h=h, bank=bank: v.tensor_tensor(
                        out=nsum[:], in0=ndch[:, h * 65:(h + 1) * 65],
                        in1=bank[:, 0:65], op=ALU.add),
                        reads=["ndch", bk], writes=["nsum"], tag=f"nsum{c}.{h}")
                    p.op("vector", lambda v: v.tensor_scalar(
                        out=dclip[:], in0=nsum[:, 64:65], scalar1=EPS_DEN,
                        scalar2=None, op0=ALU.max),
                        reads=["nsum"], writes=["dclip"], tag=f"dcl{c}.{h}")
                    p.op("vector", lambda v: v.reciprocal(out=rden[:],
                                                          in_=dclip[:]),
                         reads=["dclip"], writes=["rden"], tag=f"rd{c}.{h}")
                    p.op("vector", lambda v, h=h: v.tensor_scalar(
                        out=attm[:, h * 64:(h + 1) * 64], in0=nsum[:, 0:64],
                        scalar1=rden[:, 0:1], scalar2=None, op0=ALU.mult),
                        reads=["nsum", "rden"], writes=["attm"],
                        tag=f"att{c}.{h}")
                for j in range(NCH):
                    bank, bk = (pT1, BT1) if j % 2 == 0 else (pT2, BT2)
                    p.op("tensor", lambda t, j=j, bank=bank: t.transpose(
                        bank[:, 0:CH], attm[:, j * CH:(j + 1) * CH],
                        identb[:]),
                        reads=["attm", "identb"], writes=[bk], tag=f"tr{c}.{j}")
                    p.op("vector", lambda v, j=j, bank=bank: v.tensor_copy(
                        out=attf[:, j * CH:(j + 1) * CH], in_=bank[:, 0:CH]),
                        reads=[bk], writes=["attf"], tag=f"trev{c}.{j}")
                for o in range(NCH):
                    bank, bk = (pW1, BW1) if o % 2 == 0 else (pW2, BW2)
                    for j in range(NCH):
                        p.op("tensor", lambda t, o=o, j=j, bank=bank: t.matmul(
                            bank[:, 0:CH],
                            wosb[:, j * D + o * 128: j * D + (o + 1) * 128],
                            attf[:, j * CH:(j + 1) * CH],
                            start=(j == 0), stop=(j == NCH - 1)),
                            reads=["wo", "attf"], writes=[bk],
                            tag=f"wo{c}.{o}.{j}")
                    sl = xh[:, o * tok + t0: o * tok + t0 + CH]
                    p.op("vector", lambda v, sl=sl, bank=bank: v.tensor_tensor(
                        out=sl, in0=sl, in1=bank[:, 0:CH], op=ALU.add),
                        reads=[bk, KXH], writes=[KXH], tag=f"xp{c}.{o}")
        ctx.alloc_es = None
        p.barrier()

        with contextlib.ExitStack() as esD:
            ctx.alloc_es = esD
            h2b = ctx.sb("h2b", (128, NCH * tok), BF16)
            with contextlib.ExitStack() as esD1:
                ctx.alloc_es = esD1
                rows1 = {"a": ctx.sb("r1a", (1, tok), F32),
                         "b": ctx.sb("r1b", (1, tok), F32)}
                emit_ln_stats(ctx, xh, KXH, rows1, "r1", pS1, pS2, B1, B2k,
                              ones_col, tag="ln2")
                emit_ln_apply(ctx, xh, KXH, rows1, "r1", ones_row32, pS1, pS2,
                              B1, B2k,
                              [dict(tile=h2b, key="h2b", gb=(g2, b2, "g2b2"),
                                    tmp=gtmp, tmpkey="lngt")], tag="ln2ap")
            ctx.alloc_es = esD
            p.barrier()
            wbigA = ctx.sb("wbigA", (128, NCH * 1024), BF16)
            wbigB = ctx.sb("wbigB", (128, 8 * 1024), BF16)
            gsb = ctx.sb("gsb", (128, 8 * tok), BF16)
            for q in range(4):
                p.dma("gpsimd",
                      [(wbigA[:, k * 1024:(k + 1) * 1024],
                        wf1[k * 128:(k + 1) * 128,
                            q * 1024:(q + 1) * 1024]) for k in range(NCH)],
                      writes=["wbigA"], tag=f"ldf1.{q}")
                for ts in range(nts):
                    t0 = ts * tss
                    for mm in range(8):
                        bank, bk = (pS1, B1) if mm % 2 == 0 else (pS2, B2k)
                        for k in range(NCH):
                            p.op("tensor",
                                 lambda t, mm=mm, k=k, bank=bank, t0=t0:
                                 t.matmul(bank[:, 0:tss],
                                          wbigA[:, k * 1024 + mm * 128:
                                                k * 1024 + (mm + 1) * 128],
                                          h2b[:, k * tok + t0:
                                              k * tok + t0 + tss],
                                          start=(k == 0), stop=(k == NCH - 1)),
                                 reads=["wbigA", "h2b"], writes=[bk],
                                 tag=f"f1.{q}.{ts}.{mm}")
                        p.op("scalar", lambda s, mm=mm, bank=bank, t0=t0:
                             s.activation(
                                 out=gsb[:, mm * tok + t0: mm * tok + t0 + tss],
                                 in_=bank[:, 0:tss], func=AF.Gelu),
                             reads=[bk], writes=["gsb"], tag=f"gel.{q}.{ts}.{mm}")
                p.dma("gpsimd",
                      [(wbigB[:, kk * 1024:(kk + 1) * 1024],
                        wf2[(q * 8 + kk) * 128:(q * 8 + kk + 1) * 128, :])
                       for kk in range(8)],
                      writes=["wbigB"], tag=f"ldf2.{q}")
                for ts in range(nts):
                    t0 = ts * tss
                    for o in range(NCH):
                        bank, bk = (pW1, BW1) if o % 2 == 0 else (pW2, BW2)
                        for kk in range(8):
                            p.op("tensor",
                                 lambda t, o=o, kk=kk, bank=bank, t0=t0:
                                 t.matmul(
                                     bank[:, 0:tss],
                                     wbigB[:, kk * 1024 + o * 128:
                                           kk * 1024 + (o + 1) * 128],
                                     gsb[:, kk * tok + t0: kk * tok + t0 + tss],
                                     start=(kk == 0), stop=(kk == 7)),
                                 reads=["wbigB", "gsb"], writes=[bk],
                                 tag=f"f2.{q}.{ts}.{o}")
                        sl = xh[:, o * tok + t0: o * tok + t0 + tss]
                        p.op("vector", lambda v, sl=sl, bank=bank:
                             v.tensor_tensor(
                                 out=sl, in0=sl, in1=bank[:, 0:tss],
                                 op=ALU.add),
                             reads=[bk, KXH], writes=[KXH],
                             tag=f"xf.{q}.{ts}.{o}")
        ctx.alloc_es = None
        p.barrier()

        with contextlib.ExitStack() as esF:
            ctx.alloc_es = esF
            rows2 = {"a": ctx.sb("r2a", (1, tok), F32),
                     "b": ctx.sb("r2b", (1, tok), F32)}
            emit_ln_stats(ctx, xh, KXH, rows2, "r2", pS1, pS2, B1, B2k,
                          ones_col, tag="pln0")
            for ts in range(nts):
                t0 = ts * tss
                p.op("tensor", lambda t, t0=t0: t.matmul(
                    pS1[:, 0:tss], ones_row32[0:1, :],
                    rows2["a"][:, t0:t0 + tss], start=True, stop=True),
                    reads=["r2.a", "consts"], writes=[B1], tag=f"pbA{ts}")
                p.op("tensor", lambda t, t0=t0: t.matmul(
                    pS2[:, 0:tss], ones_row32[0:1, :],
                    rows2["b"][:, t0:t0 + tss], start=True, stop=True),
                    reads=["r2.b", "consts"], writes=[B2k], tag=f"pbB{ts}")
                for c in range(NCH):
                    sl = xh[:, c * tok + t0: c * tok + t0 + tss]
                    p.dma("sync", [(xs[:, 0:tss],
                                    xin[c * 128:(c + 1) * 128, t0:t0 + tss])],
                          writes=["xs"], tag=f"ldxs{ts}.{c}")
                    p.op("vector", lambda v, sl=sl: v.tensor_tensor(
                        out=t1b[:, 0:tss], in0=sl, in1=pS2[:, 0:tss],
                        op=ALU.add),
                        reads=[KXH, B2k], writes=["t1b"], tag=f"pt1.{ts}.{c}")
                    p.op("vector", lambda v: v.tensor_tensor(
                        out=t1b[:, 0:tss], in0=t1b[:, 0:tss],
                        in1=pS1[:, 0:tss], op=ALU.mult),
                        reads=["t1b", B1], writes=["t1b"], tag=f"pt2.{ts}.{c}")
                    p.op("vector", lambda v, sl=sl: v.tensor_tensor(
                        out=sl, in0=t1b[:, 0:tss], in1=xs[:, 0:tss],
                        op=ALU.add),
                        reads=["t1b", "xs"], writes=[KXH], tag=f"pt3.{ts}.{c}")
        ctx.alloc_es = None
        p.barrier()

        zzb = ctx.sb("zzb", (128, NCH * tok), F32)
        with contextlib.ExitStack() as esG:
            ctx.alloc_es = esG
            xfb = ctx.sb("xfb", (128, NCH * tok), BF16)
            with contextlib.ExitStack() as esG1:
                ctx.alloc_es = esG1
                rows3 = {"a": ctx.sb("r3a", (1, tok), F32),
                         "b": ctx.sb("r3b", (1, tok), F32)}
                emit_ln_stats(ctx, xh, KXH, rows3, "r3", pS1, pS2, B1, B2k,
                              ones_col, tag="fln")
                emit_ln_apply(ctx, xh, KXH, rows3, "r3", ones_row32, pS1, pS2,
                              B1, B2k, [dict(tile=xfb, key="xfb")],
                              tag="flnap")
            ctx.alloc_es = esG
            p.barrier()
            cbt_sb = ctx.sb("cbt_sb", (128, NCH * D), BF16)
            scb_sb = ctx.sb("scb_sb", (128, 8), F32)
            fb_sb = ctx.sb("fb_sb", (128, 8), F32)
            p.dma("gpsimd",
                  [(cbt_sb[:, c * D:(c + 1) * D],
                    cbt[c * 128:(c + 1) * 128, :]) for c in range(NCH)],
                  writes=["cbt"], tag="ldcbt")
            p.dma("sync", [(scb_sb[:], scb[:]), (fb_sb[:], fbias[:])],
                  writes=["scbf"], tag="ldscb")
            p.op("vector", lambda v: v.tensor_scalar(
                out=scb_sb[:], in0=scb_sb[:], scalar1=1.0 / 127.0,
                scalar2=None, op0=ALU.mult),
                reads=["scbf"], writes=["scbf"], tag="scbdiv")
            for ts in range(nts):
                t0 = ts * tss
                for m in range(NCH):
                    bank, bk = (pW1, BW1) if m % 2 == 0 else (pW2, BW2)
                    for k in range(NCH):
                        p.op("tensor", lambda t, m=m, k=k, bank=bank, t0=t0:
                             t.matmul(bank[:, 0:tss],
                                      cbt_sb[:, k * D + m * 128:
                                             k * D + (m + 1) * 128],
                                      xfb[:, k * tok + t0: k * tok + t0 + tss],
                                      start=(k == 0), stop=(k == NCH - 1)),
                             reads=["cbt", "xfb"], writes=[bk],
                             tag=f"cb.{ts}.{m}")
                    p.op("vector", lambda v, m=m, bank=bank, t0=t0:
                         v.tensor_scalar(
                             out=zzb[:, m * tok + t0: m * tok + t0 + tss],
                             in0=bank[:, 0:tss], scalar1=scb_sb[:, m:m + 1],
                             scalar2=fb_sb[:, m:m + 1],
                             op0=ALU.mult, op1=ALU.add),
                         reads=[bk, "scbf"], writes=["zzb"],
                         tag=f"cbe.{ts}.{m}")
        ctx.alloc_es = None
        p.barrier()
        with contextlib.ExitStack() as esH:
            ctx.alloc_es = esH
            rows4 = {"a": ctx.sb("r4a", (1, tok), F32),
                     "b": ctx.sb("r4b", (1, tok), F32)}
            emit_ln_stats(ctx, zzb, "zzb", rows4, "r4", pS1, pS2, B1, B2k,
                          ones_col, tag="fln2")
            for ts in range(nts):
                t0 = ts * tss
                p.op("tensor", lambda t, t0=t0: t.matmul(
                    pS1[:, 0:tss], ones_row32[0:1, :],
                    rows4["a"][:, t0:t0 + tss], start=True, stop=True),
                    reads=["r4.a", "consts"], writes=[B1], tag=f"fbA{ts}")
                p.op("tensor", lambda t, t0=t0: t.matmul(
                    pS2[:, 0:tss], ones_row32[0:1, :],
                    rows4["b"][:, t0:t0 + tss], start=True, stop=True),
                    reads=["r4.b", "consts"], writes=[B2k], tag=f"fbB{ts}")
                for c in range(NCH):
                    zl = zzb[:, c * tok + t0: c * tok + t0 + tss]
                    sl = xh[:, c * tok + t0: c * tok + t0 + tss]
                    p.op("vector", lambda v, zl=zl: v.tensor_tensor(
                        out=t1b[:, 0:tss], in0=zl, in1=pS2[:, 0:tss],
                        op=ALU.add),
                        reads=["zzb", B2k], writes=["t1b"], tag=f"ft1.{ts}.{c}")
                    p.op("vector", lambda v: v.tensor_tensor(
                        out=t1b[:, 0:tss], in0=t1b[:, 0:tss],
                        in1=pS1[:, 0:tss], op=ALU.mult),
                        reads=["t1b", B1], writes=["t1b"], tag=f"ft2.{ts}.{c}")
                    p.op("vector", lambda v, sl=sl: v.tensor_tensor(
                        out=sl, in0=sl, in1=t1b[:, 0:tss], op=ALU.add),
                        reads=["t1b", KXH], writes=[KXH], tag=f"ft3.{ts}.{c}")
        ctx.alloc_es = None
        p.dma("sync", [(o_x2[c * 128:(c + 1) * 128, :],
                        xh[:, c * tok:(c + 1) * tok]) for c in range(NCH)],
              reads=[KXH], tag="stx2")

        p.emit()
    return nc



def build_launch3(tok):
    nc = bass.Bass("TRN2", target_bir_lowering=False)
    xin = nc.dram_tensor("xin", [D, tok], F32, kind="ExternalInput").ap()
    wbt = nc.dram_tensor("wbt", [D, 16], F32, kind="ExternalInput").ap()
    wct = nc.dram_tensor("wct", [D, 16], F32, kind="ExternalInput").ap()
    wdt = nc.dram_tensor("wdt", [D, D], F32, kind="ExternalInput").ap()
    alog = nc.dram_tensor("alog", [1, 16], F32, kind="ExternalInput").ap()
    dvec = nc.dram_tensor("dvec", [128, 8], F32, kind="ExternalInput").ap()
    sel16 = nc.dram_tensor("sel16", [16, 16 * 128], F32, kind="ExternalInput").ap()
    ident = nc.dram_tensor("ident", [128, 128], F32, kind="ExternalInput").ap()
    consts = nc.dram_tensor("consts", [128, 136], F32, kind="ExternalInput").ap()
    o_y0 = nc.dram_tensor("y0", [D, tok], BF16, kind="ExternalOutput").ap()
    o_d = nc.dram_tensor("dout", [D, tok], BF16, kind="ExternalOutput").ap()
    o_c = nc.dram_tensor("crow", [16, tok], BF16, kind="ExternalOutput").ap()
    o_acc = nc.dram_tensor("acc", [128, 128], F32, kind="ExternalOutput").ap()
    o_dec = nc.dram_tensor("dec", [128, 128], F32, kind="ExternalOutput").ap()

    ctx = Ctx(nc, tok)
    p = ctx.p
    HW = tok // 2
    hss = min(512, HW)
    hts = HW // hss

    with ctx.es:
        ctx.scoped_shared = True
        consts_sb = ctx.sb("consts_sb", (128, 136), F32)
        ctx.csb = consts_sb
        ctx.ckey = "consts"
        ones_col = ctx.sb("ones_col", (128, 1), BF16)
        identb = ctx.sb("identb", (128, 128), BF16)
        bsb = ctx.sb("bsb", (16, tok), BF16)
        csbB = ctx.sb("csbB", (16, tok), BF16)
        sel = ctx.sb("sel", (16, 16 * 128), BF16)
        asb = ctx.sb("asb", (128, 16), F32)
        dv = ctx.sb("dv", (128, 8), F32)
        acc_sb = ctx.sb("acc_sb", (128, 128), F32)
        dec_sb = ctx.sb("dec_sb", (128, 128), F32)
        dtp = ctx.sb("dtp", (128, 16), F32)
        etile = ctx.sb("etile", (128, 16), F32)
        sp2 = ctx.sb("sp2", (128, 512), F32)
        sp1b = ctx.sb("sp1b", (128, 512), BF16)
        zcol = ctx.sb("zcol", (128, 1), F32)
        arow_sb = ctx.sb("arow_sb", (1, 16), F32)
        wsmall = ctx.sb("wsmall", (128, NCH * 32), BF16)
        hmid = ctx.sb("hmid", (128, 128), F32)
        xhb = ctx.sb("xhb", (128, NCH * tok), BF16)

        pS1 = ctx.ps("pS1", (128, 512))
        pS2 = ctx.ps("pS2", (128, 512))
        pBC = ctx.ps("pBC", (128, 512))
        pB40 = ctx.ps("pB40", (128, 512))
        pY = [ctx.ps(f"pY{i}", (128, 512)) for i in range(4)]
        KY = [f"b_Y{i}" for i in range(4)]
        B1, B2k = "b_S1", "b_S2"
        bcastB = [pBC, pB40, pY[2], pY[3]]
        bcastK = ["b_BC", "b_B40", KY[2], KY[3]]

        p.dma("sync", [(consts_sb[:], consts[:]), (dv[:], dvec[:]),
                       (arow_sb[:], alog[:])],
              writes=["consts", "dv", "arow"], tag="ldc")
        p.dma("gpsimd", [(sel[:], sel16[:]), (identb[:], ident[:])],
              writes=["sel", "identb"], tag="ldsel")
        p.op("vector", lambda v: v.tensor_copy(out=ones_col[:],
                                               in_=consts_sb[:, 0:1]),
             reads=["consts"], writes=["ones_col"], tag="mkones")
        p.op("vector", lambda v: v.memset(zcol[:], 0.0), writes=["zcol"],
             tag="zc")
        ones_row32 = consts_sb[0:1, 2:130]

        with contextlib.ExitStack() as esA:
            ctx.alloc_es = esA
            xh = ctx.sb("xh", (128, NCH * tok), F32)
            KXH = "xh"
            p.dma("sync", [(xh[:, c * tok:(c + 1) * tok],
                            xin[c * 128:(c + 1) * 128, :]) for c in range(NCH)],
                  writes=[KXH], tag="ldx")
            rows = {"a": ctx.sb("ra", (1, tok), F32),
                    "b": ctx.sb("rb", (1, tok), F32)}
            emit_ln_stats(ctx, xh, KXH, rows, "r", pS1, pS2, B1, B2k, ones_col,
                          tag="l3ln")
            emit_ln_apply(ctx, xh, KXH, rows, "r", ones_row32, pS1, pS2,
                          B1, B2k, [dict(tile=xhb, key="xhb")], tag="l3lnap")
        ctx.alloc_es = None
        ctx._shared = {}
        p.barrier()

        p.op("tensor", lambda t: t.matmul(
            pS1[:, 0:16], ones_row32[0:1, :], arow_sb[:],
            start=True, stop=True),
            reads=["arow", "consts"], writes=[B1], tag="abc")
        p.op("scalar", lambda s: s.activation(
            out=asb[:], in_=pS1[:, 0:16], func=AF.Exp),
            reads=[B1], writes=["asb"], tag="aexp")
        p.op("vector", lambda v: v.tensor_scalar(
            out=asb[:], in0=asb[:], scalar1=-1.0, scalar2=None,
            op0=ALU.mult), reads=["asb"], writes=["asb"], tag="aneg")

        p.dma("gpsimd",
              [(wsmall[:, c * 32: c * 32 + 16],
                wbt[c * 128:(c + 1) * 128, :]) for c in range(NCH)] +
              [(wsmall[:, c * 32 + 16: c * 32 + 32],
                wct[c * 128:(c + 1) * 128, :]) for c in range(NCH)],
              writes=["wsmall"], tag="ldwbc")
        nts4 = tok // 512 if tok >= 512 else 1
        tss4 = min(512, tok)
        for which, dst, kdst in ((0, bsb, "bsb"), (1, csbB, "csbB")):
            for ts in range(nts4):
                t0 = ts * tss4
                for k in range(NCH):
                    p.op("tensor", lambda t, which=which, k=k, t0=t0:
                         t.matmul(pBC[0:16, 0:tss4],
                                  wsmall[:, k * 32 + which * 16:
                                         k * 32 + which * 16 + 16],
                                  xhb[:, k * tok + t0: k * tok + t0 + tss4],
                                  start=(k == 0), stop=(k == NCH - 1)),
                         reads=["wsmall", "xhb"], writes=["b_BC"],
                         tag=f"bc{which}.{ts}.{k}")
                p.op("vector", lambda v, dst=dst, t0=t0: v.tensor_copy(
                    out=dst[:, t0:t0 + tss4], in_=pBC[0:16, 0:tss4]),
                    reads=["b_BC"], writes=[kdst], tag=f"bce{which}.{ts}")
        p.dma("gpsimd", [(o_c[:], csbB[:])], reads=["csbB"], tag="stc")

        wdsbB = [ctx.sb(f"wdsb{i}", (128, NCH * 128), BF16) for i in range(2)]
        dall = ctx.sb("dall", (128, NCH * HW), F32)
        dxall = ctx.sb("dxall", (128, NCH * HW), F32)
        BbS = ctx.sb("BbS", (128, 16 * HW), BF16)
        CbS = ctx.sb("CbS", (128, 16 * HW), BF16)
        daB = [ctx.sb(f"da{i}", (128, HW), F32) for i in range(2)]
        uB = [ctx.sb(f"u{i}", (128, HW), BF16) for i in range(2)]
        hhB = [ctx.sb(f"hh{i}", (128, HW), F32) for i in range(2)]
        hbxB = [ctx.sb(f"hbx{i}", (128, HW), BF16) for i in range(2)]

        for H in range(2):
            h0 = H * HW
            for hc in range(NCH):
                wdsb, kwd = wdsbB[hc % 2], f"wdsb{hc % 2}"
                KDA, KDX = f"dal{hc}", f"dxl{hc}"
                p.dma("gpsimd",
                      [(wdsb[:, k * 128:(k + 1) * 128],
                        wdt[k * 128:(k + 1) * 128,
                            hc * 128:(hc + 1) * 128]) for k in range(NCH)],
                      writes=[kwd], tag=f"ldwd{H}.{hc}")
                for sl in range(hts):
                    s0 = sl * hss
                    bank, bk = (pS1, B1) if sl % 2 == 0 else (pS2, B2k)
                    for k in range(NCH):
                        p.op("tensor", lambda t, k=k, s0=s0, bank=bank,
                             wdsb=wdsb, h0=h0: t.matmul(
                                 bank[:, 0:hss],
                                 wdsb[:, k * 128:(k + 1) * 128],
                                 xhb[:, k * tok + h0 + s0:
                                     k * tok + h0 + s0 + hss],
                                 start=(k == 0), stop=(k == NCH - 1)),
                             reads=[kwd, "xhb"], writes=[bk],
                             tag=f"dp{H}.{hc}.{sl}.{k}")
                    p.op("scalar", lambda s, bank=bank: s.activation(
                        out=sp2[:, 0:hss], in_=bank[:, 0:hss], func=AF.Abs),
                        reads=[bk], writes=["sp2"], tag=f"spa{H}.{hc}.{sl}")
                    p.op("scalar", lambda s: s.activation(
                        out=sp2[:, 0:hss], in_=sp2[:, 0:hss], func=AF.Exp,
                        scale=consts_sb[:, 133:134]),
                        reads=["sp2", "consts"], writes=["sp2"],
                        tag=f"spe{H}.{hc}.{sl}")
                    p.op("scalar", lambda s: s.activation(
                        out=sp2[:, 0:hss], in_=sp2[:, 0:hss], func=AF.Ln,
                        bias=consts_sb[:, 0:1]),
                        reads=["sp2", "consts"], writes=["sp2"],
                        tag=f"spl{H}.{hc}.{sl}")
                    p.op("vector", lambda v, bank=bank, hc=hc, s0=s0:
                         v.scalar_tensor_tensor(
                             out=dall[:, hc * HW + s0: hc * HW + s0 + hss],
                             in0=bank[:, 0:hss],
                             scalar=0.0, in1=sp2[:, 0:hss],
                             op0=ALU.max, op1=ALU.add),
                         reads=[bk, "sp2"], writes=[KDA],
                         tag=f"spf{H}.{hc}.{sl}")
                    p.op("gpsimd", lambda g, hc=hc, s0=s0, h0=h0:
                         g.tensor_tensor(
                             out=dxall[:, hc * HW + s0: hc * HW + s0 + hss],
                             in0=dall[:, hc * HW + s0: hc * HW + s0 + hss],
                             in1=xhb[:, hc * tok + h0 + s0:
                                     hc * tok + h0 + s0 + hss],
                             op=ALU.mult),
                         reads=[KDA, "xhb"], writes=[KDX],
                         tag=f"dx{H}.{hc}.{sl}")
                p.dma("gpsimd", [(o_d[hc * 128:(hc + 1) * 128, h0:h0 + HW],
                                  dall[:, hc * HW:(hc + 1) * HW])],
                      reads=[KDA], tag=f"std{H}.{hc}")
                if H == 0:
                    p.op("vector", lambda v, hc=hc: v.reduce_sum(
                        out=dtp[:, hc:hc + 1],
                        in_=dall[:, hc * HW:(hc + 1) * HW], axis=AX.X),
                        reads=[KDA], writes=["dtp"], tag=f"dt0.{hc}")
                else:
                    p.op("vector", lambda v, hc=hc: v.reduce_sum(
                        out=dtp[:, 8 + hc:9 + hc],
                        in_=dall[:, hc * HW:(hc + 1) * HW], axis=AX.X),
                        reads=[KDA], writes=["dtp"], tag=f"dt1.{hc}")
                    p.op("vector", lambda v, hc=hc: v.tensor_tensor(
                        out=dtp[:, 8 + hc:9 + hc], in0=dtp[:, hc:hc + 1],
                        in1=dtp[:, 8 + hc:9 + hc], op=ALU.add),
                        reads=["dtp"], writes=["dtp"], tag=f"dts.{hc}")
                    p.op("vector", lambda v, hc=hc: v.tensor_scalar(
                        out=etile[:], in0=asb[:],
                        scalar1=dtp[:, 8 + hc:9 + hc],
                        scalar2=None, op0=ALU.mult),
                        reads=["dtp", "asb"], writes=["etile"], tag=f"de{hc}")
                    p.op("scalar", lambda s, hc=hc: s.activation(
                        out=dec_sb[:, hc * 16:(hc + 1) * 16], in_=etile[:],
                        func=AF.Exp),
                        reads=["etile"], writes=["dec"], tag=f"dex{hc}")
            bi = 0
            for n in range(16):
                for which, src, ksrc, dst, kdst in (
                        (0, bsb, "bsb", BbS, "BbS"),
                        (1, csbB, "csbB", CbS, "CbS")):
                    for sl in range(hts):
                        s0 = sl * hss
                        bank, bk = bcastB[bi % 4], bcastK[bi % 4]
                        bi += 1
                        p.op("tensor", lambda t, n=n, s0=s0, bank=bank,
                             src=src, h0=h0: t.matmul(
                                 bank[:, 0:hss],
                                 sel[:, n * 128:(n + 1) * 128],
                                 src[:, h0 + s0: h0 + s0 + hss],
                                 start=True, stop=True),
                             reads=["sel", ksrc], writes=[bk],
                             tag=f"bc{H}.{n}.{which}.{sl}")
                        p.op("scalar", lambda s, n=n, s0=s0, bank=bank,
                             dst=dst: s.activation(
                                 out=dst[:, n * HW + s0: n * HW + s0 + hss],
                                 in_=bank[:, 0:hss], func=AF.Copy),
                             reads=[bk], writes=[kdst],
                             tag=f"be{H}.{n}.{which}.{sl}")
            for hc in range(NCH):
                KDA, KDX = f"dal{hc}", f"dxl{hc}"
                for n in range(16):
                    ib = n % 2
                    da, uu, hh, hb = daB[ib], uB[ib], hhB[ib], hbxB[ib]
                    kda, ku, kh, khb = (f"da{ib}", f"u{ib}",
                                        f"hh{ib}", f"hbx{ib}")
                    p.op("scalar", lambda s, n=n, hc=hc, da=da: s.activation(
                        out=da[:, :], in_=dall[:, hc * HW:(hc + 1) * HW],
                        func=AF.Exp, scale=asb[:, n:n + 1]),
                        reads=[KDA, "asb"], writes=[kda],
                        tag=f"da{H}.{hc}.{n}")
                    p.op("vector", lambda v, n=n, hc=hc, uu=uu:
                         v.tensor_tensor(
                             out=uu[:, :],
                             in0=dxall[:, hc * HW:(hc + 1) * HW],
                             in1=BbS[:, n * HW:(n + 1) * HW], op=ALU.mult),
                         reads=[KDX, "BbS"], writes=[ku],
                         tag=f"u{H}.{hc}.{n}")
                    idx = hc * 16 + n
                    p.op("vector", lambda v, da=da, uu=uu, hh=hh, idx=idx,
                         H=H: v.tensor_tensor_scan(
                             out=hh[:, :], data0=da[:, :], data1=uu[:, :],
                             initial=(zcol[:, 0:1] if H == 0
                                      else hmid[:, idx:idx + 1]),
                             op0=ALU.mult, op1=ALU.add),
                         reads=[kda, ku, "zcol"] + ([] if H == 0
                                                    else ["hmid"]),
                         writes=[kh], tag=f"scan{H}.{hc}.{n}")
                    if H == 0:
                        p.op("gpsimd", lambda g, idx=idx, hh=hh:
                             g.tensor_copy(
                                 out=hmid[:, idx:idx + 1],
                                 in_=hh[:, HW - 1:HW]),
                             reads=[kh], writes=["hmid"], tag=f"hm{idx}")
                    else:
                        p.op("gpsimd", lambda g, idx=idx, hh=hh:
                             g.tensor_copy(
                                 out=acc_sb[:, idx:idx + 1],
                                 in_=hh[:, HW - 1:HW]),
                             reads=[kh], writes=["acc"], tag=f"ac{idx}")
                    p.op("vector", lambda v, n=n, hh=hh, hb=hb:
                         v.tensor_tensor(
                             out=hb[:, :], in0=hh[:, :],
                             in1=CbS[:, n * HW:(n + 1) * HW], op=ALU.mult),
                         reads=[kh, "CbS"], writes=[khb],
                         tag=f"ch{H}.{hc}.{n}")
                    if n > 0:
                        hbp, khbp = hbxB[(n - 1) % 2], f"hbx{(n - 1) % 2}"
                        for sl in range(hts):
                            s0 = sl * hss
                            p.op("tensor", lambda t, n=n, s0=s0, sl=sl,
                                 hb=hbp: t.matmul(
                                     pY[sl][:, 0:hss], identb[:],
                                     hb[:, s0:s0 + hss],
                                     start=(n == 1), stop=False),
                                 reads=[khbp, "identb"], writes=[KY[sl]],
                                 tag=f"y{H}.{hc}.{n - 1}.{sl}")
                for sl in range(hts):
                    s0 = sl * hss
                    p.op("tensor", lambda t, s0=s0, sl=sl: t.matmul(
                        pY[sl][:, 0:hss], identb[:],
                        hbxB[1][:, s0:s0 + hss],
                        start=False, stop=True),
                        reads=["hbx1", "identb"], writes=[KY[sl]],
                        tag=f"y{H}.{hc}.15.{sl}")
                for sl in range(hts):
                    s0 = sl * hss
                    p.op("vector", lambda v, hc=hc, s0=s0, sl=sl, h0=h0:
                         v.scalar_tensor_tensor(
                             out=sp1b[:, 0:hss],
                             in0=xhb[:, hc * tok + h0 + s0:
                                     hc * tok + h0 + s0 + hss],
                             scalar=dv[:, hc:hc + 1], in1=pY[sl][:, 0:hss],
                             op0=ALU.mult, op1=ALU.add),
                         reads=["xhb", "dv", KY[sl]], writes=["sp1b"],
                         tag=f"ye{H}.{hc}.{sl}")
                    p.dma("sync", [(o_y0[hc * 128:(hc + 1) * 128,
                                         h0 + s0:h0 + s0 + hss],
                                    sp1b[:, 0:hss])],
                          reads=["sp1b"], tag=f"sty{H}.{hc}.{sl}")
        p.dma("sync", [(o_acc[:], acc_sb[:]), (o_dec[:], dec_sb[:])],
              reads=["acc", "dec"], tag="stad")

        p.emit()
    return nc



def build_launch4(tok):
    nc = bass.Bass("TRN2", target_bir_lowering=False)
    tch = tok // CH
    xin = nc.dram_tensor("xin", [D, tok], F32, kind="ExternalInput").ap()
    y0in = nc.dram_tensor("y0in", [D, tok], BF16, kind="ExternalInput").ap()
    din = nc.dram_tensor("din", [D, tok], BF16, kind="ExternalInput").ap()
    cin = nc.dram_tensor("cin", [16, tok], BF16, kind="ExternalInput").ap()
    accall = nc.dram_tensor("accall", [128, NCORES * 128], F32,
                            kind="ExternalInput").ap()
    decall = nc.dram_tensor("decall", [128, NCORES * 128], F32,
                            kind="ExternalInput").ap()
    mcol8 = nc.dram_tensor("mcol8", [128, NCORES], F32,
                           kind="ExternalInput").ap()
    alog = nc.dram_tensor("alog", [1, 16], F32, kind="ExternalInput").ap()
    sel16 = nc.dram_tensor("sel16", [16, 16 * 128], F32,
                           kind="ExternalInput").ap()
    woutT = nc.dram_tensor("woutT", [D, D], F32, kind="ExternalInput").ap()
    wgate = nc.dram_tensor("wgate", [D, 6], F32, kind="ExternalInput").ap()
    atmat = nc.dram_tensor("atmat", [96, 100], F32, kind="ExternalInput").ap()
    bmat = nc.dram_tensor("bmat", [96, D], F32, kind="ExternalInput").ap()
    wtwT = nc.dram_tensor("wtwT", [D, D], F32, kind="ExternalInput").ap()
    wtbrow = nc.dram_tensor("wtbrow", [1, D], F32, kind="ExternalInput").ap()
    wdwT = nc.dram_tensor("wdwT", [D, D], F32, kind="ExternalInput").ap()
    wdb = nc.dram_tensor("wdb", [128, 8], F32, kind="ExternalInput").ap()
    scalein = nc.dram_tensor("scalein", [1, 1], F32, kind="ExternalInput").ap()
    ident = nc.dram_tensor("ident", [128, 128], F32, kind="ExternalInput").ap()
    consts = nc.dram_tensor("consts", [128, 136], F32, kind="ExternalInput").ap()
    o_x = nc.dram_tensor("xout", [D, tok], F32, kind="ExternalOutput").ap()

    ctx = Ctx(nc, tok)
    p = ctx.p
    tss, nts = ctx.tss, ctx.nts

    with ctx.es:
        consts_sb = ctx.sb("consts_sb", (128, 136), F32)
        ctx.csb = consts_sb
        ctx.ckey = "consts"
        ones_col = ctx.sb("ones_col", (128, 1), BF16)
        identb = ctx.sb("identb", (128, 128), BF16)
        xh = ctx.sb("xh", (128, NCH * tok), F32)
        ctx.prealloc_ln()
        ctx.shared("rowA", (1, tok), F32)
        ctx.shared("rowB", (1, tok), F32)
        pS1 = ctx.ps("pS1", (128, 512))
        pS2 = ctx.ps("pS2", (128, 512))
        B1, B2k = "b_S1", "b_S2"
        KXH = "xh"

        p.dma("sync", [(consts_sb[:], consts[:])], writes=["consts"], tag="ldc")
        p.op("vector", lambda v: v.tensor_copy(out=ones_col[:],
                                               in_=consts_sb[:, 0:1]),
             reads=["consts"], writes=["ones_col"], tag="mkones")
        ones_row32 = consts_sb[0:1, 2:130]
        p.dma("gpsimd", [(identb[:], ident[:])], writes=["identb"], tag="ldid")
        p.dma("sync", [(xh[:, c * tok:(c + 1) * tok],
                        xin[c * 128:(c + 1) * 128, :]) for c in range(NCH)],
              writes=[KXH], tag="ldx")

        with contextlib.ExitStack() as esA:
            ctx.alloc_es = esA
            def sbA(name, shape, dtype):
                return esA.enter_context(nc.sbuf_tensor(name, list(shape), dtype))
            asb = sbA("asb", (128, 16), F32)
            arow_sb = sbA("arow_sb", (1, 16), F32)
            sel = sbA("sel", (16, 16 * 128), BF16)
            hin = sbA("hin", (128, 128), F32)
            csb2 = sbA("csb2", (16, tok), BF16)
            ybig = sbA("ybig", (128, NCH * tok), BF16)
            wosb = sbA("wosb", (128, NCH * D), BF16)
            pY = [ctx.ps(f"pY{i}", (128, 512)) for i in range(max(4, nts))]
            KY = [f"b_Y{i}" for i in range(max(4, nts))]

            p.dma("sync", [(arow_sb[:], alog[:])],
                  writes=["arow"], tag="lds")
            p.dma("gpsimd", [(sel[:], sel16[:]), (csb2[:], cin[:])],
                  writes=["sel", "csb2"], tag="ldsc2")
            p.op("tensor", lambda t: t.matmul(
                pS1[:, 0:16], ones_row32[0:1, :], arow_sb[:],
                start=True, stop=True),
                reads=["arow", "consts"], writes=[B1], tag="abc")
            p.op("scalar", lambda s: s.activation(
                out=asb[:], in_=pS1[:, 0:16], func=AF.Exp),
                reads=[B1], writes=["asb"], tag="aexp")
            p.op("vector", lambda v: v.tensor_scalar(
                out=asb[:], in0=asb[:], scalar1=-1.0, scalar2=None,
                op0=ALU.mult), reads=["asb"], writes=["asb"], tag="aneg")
            with contextlib.ExitStack() as esC0:
                ctx.alloc_es = esC0
                accs = ctx.sb("accs", (128, 128), F32)
                decs = ctx.sb("decs", (128, 128), F32)
                mc = ctx.sb("mc", (128, NCORES), F32)
                m1 = ctx.sb("m1", (128, NCORES), F32)
                dtl = ctx.sb("dtl", (128, 128), F32)
                tmpa = ctx.sb("tmpa", (128, 128), F32)
                p.dma("sync", [(mc[:], mcol8[:])], writes=["mc"], tag="ldsc")
                p.op("vector", lambda v: v.tensor_scalar(
                    out=m1[:], in0=mc[:], scalar1=-1.0, scalar2=1.0,
                    op0=ALU.mult, op1=ALU.add),
                    reads=["mc"], writes=["m1"], tag="m1")
                p.op("vector", lambda v: v.memset(hin[:], 0.0), writes=["hin"],
                     tag="h0")
                for r in range(NCORES):
                    p.dma("sync", [(accs[:], accall[:, r * 128:(r + 1) * 128]),
                                   (decs[:], decall[:, r * 128:(r + 1) * 128])],
                          writes=["accs", "decs"], tag=f"ldad{r}")
                    p.op("vector", lambda v, r=r: v.tensor_scalar(
                        out=dtl[:], in0=decs[:, :],
                        scalar1=mc[:, r:r + 1], scalar2=m1[:, r:r + 1],
                        op0=ALU.mult, op1=ALU.add),
                        reads=["decs", "mc", "m1"], writes=["dtl"],
                        tag=f"dt{r}")
                    p.op("vector", lambda v, r=r: v.tensor_scalar(
                        out=tmpa[:], in0=accs[:, :],
                        scalar1=mc[:, r:r + 1], scalar2=None, op0=ALU.mult),
                        reads=["accs", "mc"], writes=["tmpa"], tag=f"ta{r}")
                    p.op("vector", lambda v: v.tensor_tensor(
                        out=hin[:], in0=hin[:], in1=dtl[:], op=ALU.mult),
                        reads=["hin", "dtl"], writes=["hin"], tag=f"hm{r}")
                    p.op("vector", lambda v: v.tensor_tensor(
                        out=hin[:], in0=hin[:], in1=tmpa[:], op=ALU.add),
                        reads=["hin", "tmpa"], writes=["hin"], tag=f"ha{r}")
            ctx.alloc_es = esA
            p.barrier()
            p.dma("gpsimd",
                  [(wosb[:, c * D:(c + 1) * D],
                    woutT[c * 128:(c + 1) * 128, :]) for c in range(NCH)],
                  writes=["wo"], tag="ldwo")
            esA1 = contextlib.ExitStack()
            ctx.alloc_es = esA1
            dhc = ctx.sb("dhc", (128, tok), BF16)
            cumdB = [ctx.sb(f"cumd{i}", (128, tok), F32) for i in range(2)]
            mhcB = [ctx.sb(f"mhc{i}", (128, tok), F32) for i in range(2)]
            hbxB = [ctx.sb(f"hbx{i}", (128, tok), BF16) for i in range(2)]
            zt5 = ctx.sb("zt5", (128, tss), F32)
            p.op("vector", lambda v: v.memset(zt5[:], 0.0), writes=["zt5"],
                 tag="z4")
            p.dma("scalar",
                  [(ybig[:, c * tok:(c + 1) * tok],
                    y0in[c * 128:(c + 1) * 128, :]) for c in range(NCH)],
                  writes=["ybig"], tag="ldy0")
            pB40 = ctx.ps("pB40", (128, 512))
            cbB = [pB40, pS1, pS2]
            cbK = ["b_B40", B1, B2k]
            for hc in range(NCH):
                hp = hc % 2
                cumd = cumdB[hp]
                KDH, KCU = "dhc", f"cumd{hp}"
                p.dma("sync", [(dhc[:], din[hc * 128:(hc + 1) * 128, :])],
                      writes=[KDH], tag=f"ldd{hc}")
                for sg in range(nts):
                    s0 = sg * tss
                    rd = [KDH, "zt5"] + ([KCU] if sg > 0 else [])
                    p.op("vector", lambda v, s0=s0, sg=sg, cumd=cumd:
                         v.tensor_tensor_scan(
                             out=cumd[:, s0:s0 + tss],
                             data0=dhc[:, s0:s0 + tss], data1=zt5[:, :],
                             initial=(zt5[:, 0:1] if sg == 0
                                      else cumd[:, s0 - 1:s0]),
                             op0=ALU.add, op1=ALU.add),
                         reads=rd, writes=[KCU], tag=f"cs{hc}.{sg}")
                for n in range(16):
                    ib = n % 2
                    mh, hb = mhcB[ib], hbxB[ib]
                    kmh, khb = f"mhc{ib}", f"hbx{ib}"
                    p.op("scalar", lambda s, n=n, mh=mh, cumd=cumd:
                         s.activation(
                             out=mh[:, :], in_=cumd[:, :], func=AF.Exp,
                             scale=asb[:, n:n + 1]),
                         reads=[KCU, "asb"], writes=[kmh],
                         tag=f"M{hc}.{n}")
                    def emit_cb(ts, n=n):
                        t0 = ts * tss
                        bi = (n * nts + ts) % 3
                        bank, bk = cbB[bi], cbK[bi]
                        p.op("tensor", lambda t, n=n, t0=t0, bank=bank:
                             t.matmul(bank[:, 0:tss],
                                      sel[:, n * 128:(n + 1) * 128],
                                      csb2[:, t0:t0 + tss],
                                      start=True, stop=True),
                             reads=["sel", "csb2"], writes=[bk],
                             tag=f"cb{hc}.{n}.{ts}")

                    def emit_chx(ts, n=n, mh=mh, hb=hb, kmh=kmh, khb=khb):
                        t0 = ts * tss
                        bi = (n * nts + ts) % 3
                        bank, bk = cbB[bi], cbK[bi]
                        p.op("vector", lambda v, hc=hc, n=n, t0=t0, bank=bank,
                             mh=mh, hb=hb: v.scalar_tensor_tensor(
                                 out=hb[:, t0:t0 + tss],
                                 in0=mh[:, t0:t0 + tss],
                                 scalar=hin[:, hc * 16 + n: hc * 16 + n + 1],
                                 in1=bank[:, 0:tss],
                                 op0=ALU.mult, op1=ALU.mult),
                             reads=[kmh, "hin", bk], writes=[khb],
                             tag=f"chx{hc}.{n}.{ts}")

                    for ts in range(nts):
                        emit_cb(ts)
                        if ts >= 2:
                            emit_chx(ts - 2)
                    for ts in range(max(0, nts - 2), nts):
                        emit_chx(ts)
                    if n > 0:
                        hbp, khbp = hbxB[(n - 1) % 2], f"hbx{(n - 1) % 2}"
                        for ts in range(nts):
                            t0 = ts * tss
                            p.op("tensor", lambda t, n=n, t0=t0, ts=ts,
                                 hb=hbp: t.matmul(
                                     pY[ts][:, 0:tss], identb[:],
                                     hb[:, t0:t0 + tss],
                                     start=(n == 1), stop=False),
                                 reads=[khbp, "identb"], writes=[KY[ts]],
                                 tag=f"y{hc}.{n - 1}.{ts}")
                for ts in range(nts):
                    t0 = ts * tss
                    p.op("tensor", lambda t, t0=t0, ts=ts: t.matmul(
                        pY[ts][:, 0:tss], identb[:],
                        hbxB[1][:, t0:t0 + tss],
                        start=False, stop=True),
                        reads=["hbx1", "identb"], writes=[KY[ts]],
                        tag=f"y{hc}.15.{ts}")
                for ts in range(nts):
                    t0 = ts * tss
                    p.op("vector", lambda v, hc=hc, t0=t0, ts=ts:
                         v.tensor_tensor(
                             out=ybig[:, hc * tok + t0: hc * tok + t0 + tss],
                             in0=ybig[:, hc * tok + t0: hc * tok + t0 + tss],
                             in1=pY[ts][:, 0:tss],
                             op=ALU.add),
                         reads=["ybig", KY[ts]], writes=["ybig"],
                         tag=f"yb{hc}.{ts}")
            esA1.close()
            ctx.alloc_es = esA
            p.barrier()
            zts = sbA("zts", (128, NCH * 512), F32)
            zb16 = ctx.shared("lnxb", (128, tss), BF16)
            zsq = ctx.shared("lnsq", (128, tss), BF16)
            s1r = ctx.shared("lns1", (1, tok), F32)
            s2r = ctx.shared("lns2", (1, tok), F32)
            arow5 = ctx.shared("rowA", (1, tok), F32)
            brow5 = ctx.shared("rowB", (1, tok), F32)
            t1c = sbA("t1c", (128, tss), F32)
            for ts in range(nts):
                t0 = ts * tss
                for o in range(NCH):
                    bank, bk = (pS1, B1) if o % 2 == 0 else (pS2, B2k)
                    for k in range(NCH):
                        p.op("tensor", lambda t, o=o, k=k, t0=t0, bank=bank:
                             t.matmul(bank[:, 0:tss],
                                      wosb[:, k * D + o * 128:
                                           k * D + (o + 1) * 128],
                                      ybig[:, k * tok + t0: k * tok + t0 + tss],
                                      start=(k == 0), stop=(k == NCH - 1)),
                             reads=["wo", "ybig"], writes=[bk],
                             tag=f"wo{o}.{ts}.{k}")
                    p.op("vector", lambda v, o=o, bank=bank: v.tensor_copy(
                        out=zts[:, o * tss:(o + 1) * tss], in_=bank[:, 0:tss]),
                        reads=[bk], writes=["zts"], tag=f"woe{o}.{ts}")
                for o in range(NCH):
                    zl = zts[:, o * tss:(o + 1) * tss]
                    p.op("vector", lambda v, zl=zl: v.tensor_copy(
                        out=zb16[:, 0:tss], in_=zl),
                        reads=["zts"], writes=["lnxb"], tag=f"zc{ts}.{o}")
                    p.op("scalar", lambda s, zl=zl: s.activation(
                        out=zsq[:, 0:tss], in_=zl, func=AF.Square),
                        reads=["zts"], writes=["lnsq"], tag=f"zs{ts}.{o}")
                    p.op("tensor", lambda t, o=o, t0=t0: t.matmul(
                        pS1[0:1, 0:tss], ones_col[:, 0:1], zb16[:, 0:tss],
                        start=(o == 0), stop=(o == NCH - 1)),
                        reads=["lnxb", "ones_col"], writes=[B1],
                        tag=f"zm1{ts}.{o}")
                    p.op("tensor", lambda t, o=o, t0=t0: t.matmul(
                        pS2[0:1, 0:tss], ones_col[:, 0:1], zsq[:, 0:tss],
                        start=(o == 0), stop=(o == NCH - 1)),
                        reads=["lnsq", "ones_col"], writes=[B2k],
                        tag=f"zm2{ts}.{o}")
                p.op("vector", lambda v, t0=t0: v.tensor_copy(
                    out=s1r[:, t0:t0 + tss], in_=pS1[0:1, 0:tss]),
                    reads=[B1], writes=["lns1"], tag=f"ze1{ts}")
                p.op("vector", lambda v, t0=t0: v.tensor_copy(
                    out=s2r[:, t0:t0 + tss], in_=pS2[0:1, 0:tss]),
                    reads=[B2k], writes=["lns2"], tag=f"ze2{ts}")
                p.op("vector", lambda v, t0=t0: v.tensor_scalar(
                    out=brow5[:, t0:t0 + tss], in0=s1r[:, t0:t0 + tss],
                    scalar1=-1.0 / D, scalar2=None, op0=ALU.mult),
                    reads=["lns1"], writes=["rr.b"], tag=f"zb{ts}")
                p.op("vector", lambda v, t0=t0: v.tensor_scalar(
                    out=arow5[:, t0:t0 + tss], in0=s2r[:, t0:t0 + tss],
                    scalar1=1.0 / D, scalar2=None, op0=ALU.mult),
                    reads=["lns2"], writes=["rr.a"], tag=f"za{ts}")
                p.op("vector", lambda v, t0=t0: v.tensor_tensor(
                    out=s1r[:, t0:t0 + tss], in0=brow5[:, t0:t0 + tss],
                    in1=brow5[:, t0:t0 + tss], op=ALU.mult),
                    reads=["rr.b"], writes=["lns1"], tag=f"zq{ts}")
                p.op("vector", lambda v, t0=t0: v.tensor_tensor(
                    out=arow5[:, t0:t0 + tss], in0=arow5[:, t0:t0 + tss],
                    in1=s1r[:, t0:t0 + tss], op=ALU.subtract),
                    reads=["rr.a", "lns1"], writes=["rr.a"], tag=f"zv{ts}")
                p.op("scalar", lambda s, t0=t0: s.activation(
                    out=arow5[:, t0:t0 + tss], in_=arow5[:, t0:t0 + tss],
                    func=AF.Ln, bias=consts_sb[0:1, 130:131]),
                    reads=["rr.a", "consts"], writes=["rr.a"], tag=f"zl{ts}")
                p.op("scalar", lambda s, t0=t0: s.activation(
                    out=arow5[:, t0:t0 + tss], in_=arow5[:, t0:t0 + tss],
                    func=AF.Exp, scale=consts_sb[0:1, 131:132]),
                    reads=["rr.a", "consts"], writes=["rr.a"], tag=f"zr{ts}")
                p.op("tensor", lambda t, t0=t0: t.matmul(
                    pS1[:, 0:tss], ones_row32[0:1, :],
                    arow5[:, t0:t0 + tss], start=True, stop=True),
                    reads=["rr.a", "consts"], writes=[B1], tag=f"5bA{ts}")
                p.op("tensor", lambda t, t0=t0: t.matmul(
                    pS2[:, 0:tss], ones_row32[0:1, :],
                    brow5[:, t0:t0 + tss], start=True, stop=True),
                    reads=["rr.b", "consts"], writes=[B2k], tag=f"5bB{ts}")
                for cc in range(NCH):
                    zl = zts[:, cc * tss:(cc + 1) * tss]
                    sl = xh[:, cc * tok + t0: cc * tok + t0 + tss]
                    p.op("vector", lambda v, zl=zl: v.tensor_tensor(
                        out=t1c[:], in0=zl, in1=pS2[:, 0:tss], op=ALU.add),
                        reads=["zts", B2k], writes=["t1c"],
                        tag=f"5t1.{ts}.{cc}")
                    p.op("vector", lambda v: v.tensor_tensor(
                        out=t1c[:], in0=t1c[:], in1=pS1[:, 0:tss],
                        op=ALU.mult),
                        reads=["t1c", B1], writes=["t1c"], tag=f"5t2.{ts}.{cc}")
                    p.op("vector", lambda v, sl=sl: v.tensor_tensor(
                        out=sl, in0=sl, in1=t1c[:], op=ALU.add),
                        reads=["t1c", KXH], writes=[KXH], tag=f"5t3.{ts}.{cc}")
        ctx.alloc_es = None
        p.barrier()

        ffb = ctx.sb("ffb", (128, NCH * tok), BF16)
        rows6 = {"a": ctx.shared("rowA", (1, tok), F32),
                 "b": ctx.shared("rowB", (1, tok), F32)}
        emit_ln_stats(ctx, xh, KXH, rows6, "rr", pS1, pS2, B1, B2k, ones_col,
                      tag="l4f")
        emit_ln_apply(ctx, xh, KXH, rows6, "rr", ones_row32, pS1, pS2, B1, B2k,
                      [dict(tile=ffb, key="ffb")], tag="l4fap")
        arow6 = ctx.sb("arow6", (1, tok), BF16)
        brow6 = ctx.sb("brow6", (1, tok), BF16)
        onesb = ctx.sb("onesb", (1, 128), BF16)
        p.op("gpsimd", lambda g: g.tensor_copy(out=onesb[:],
                                               in_=consts_sb[0:1, 2:130]),
             reads=["consts"], writes=["onesb"], tag="mkonesb")
        p.op("vector", lambda v: v.tensor_copy(out=arow6[:], in_=rows6["a"][:]),
             reads=["rr.a"], writes=["r6.a"], tag="cp6a")
        p.op("vector", lambda v: v.tensor_copy(out=brow6[:], in_=rows6["b"][:]),
             reads=["rr.b"], writes=["r6.b"], tag="cp6b")

        tksb = ctx.sb("tksb", (128, NCH * 600), BF16)
        valsb = ctx.sb("valsb", (100, 6 * D), BF16)
        wgsb = ctx.sb("wgsb", (128, NCH * 6), BF16)
        wbsb = ctx.sb("wbsb", (128, NCH * D), BF16)
        wdbs = ctx.sb("wdbs", (128, 8), F32)
        scol = ctx.sb("scol", (128, 1), F32)
        scsb = ctx.sb("scsb", (1, 1), F32)
        p.dma("gpsimd",
              [(wgsb[:, c * 6:(c + 1) * 6], wgate[c * 128:(c + 1) * 128, :])
               for c in range(NCH)] +
              [(wbsb[:, c * D:(c + 1) * D], wtwT[c * 128:(c + 1) * 128, :])
               for c in range(NCH)],
              writes=["wgsb", "wbsb"], tag="ldffn")
        p.dma("sync", [(wdbs[:], wdb[:]),
                       (scsb[:], scalein[:])],
              writes=["wdbs", "scsb"], tag="ldffb")
        p.op("tensor", lambda t: t.matmul(
            pS1[:, 0:1], ones_row32[0:1, :], scsb[:], start=True, stop=True),
            reads=["scsb", "consts"], writes=[B1], tag="scb")
        p.op("vector", lambda v: v.tensor_copy(out=scol[:], in_=pS1[:, 0:1]),
             reads=[B1], writes=["scol"], tag="sce")
        with contextlib.ExitStack() as esT:
            ctx.alloc_es = esT
            atsb = ctx.sb("atsb", (16, 6 * 100), BF16)
            bmsb = ctx.sb("bmsb", (16, 6 * D), BF16)
            p.dma("gpsimd",
                  [(atsb[:, e * 100:(e + 1) * 100],
                    atmat[e * 16:(e + 1) * 16, :]) for e in range(6)] +
                  [(bmsb[:, e * D:(e + 1) * D],
                    bmat[e * 16:(e + 1) * 16, :]) for e in range(6)],
                  writes=["atsb", "bmsb"], tag="ldab2")
            for e in range(6):
                for cc in range(NCH):
                    bank, bk = ((pS1, B1) if (e * NCH + cc) % 2 == 0
                                else (pS2, B2k))
                    p.op("tensor", lambda t, e=e, cc=cc, bank=bank: t.matmul(
                        bank[:, 0:100], bmsb[:, e * D + cc * 128:
                                             e * D + (cc + 1) * 128],
                        atsb[:, e * 100:(e + 1) * 100], start=True, stop=True),
                        reads=["bmsb", "atsb"], writes=[bk], tag=f"tk{e}.{cc}")
                    p.op("vector", lambda v, e=e, cc=cc, bank=bank:
                         v.tensor_copy(
                             out=tksb[:, cc * 600 + e * 100:
                                      cc * 600 + (e + 1) * 100],
                             in_=bank[:, 0:100]),
                         reads=[bk], writes=["tksb"], tag=f"tke{e}.{cc}")
        ctx.alloc_es = None
        p.barrier()
        esV = contextlib.ExitStack()
        ctx.alloc_es = esV
        vtmp = ctx.sb("vtmp", (100, 512), F32)
        wtbs = ctx.sb("wtbs", (1, D), F32)
        p.dma("sync", [(wtbs[:], wtbrow[:])], writes=["wtbs"], tag="ldwtb")
        for e in range(6):
            for half in range(2):
                bank, bk = (pS1, B1) if half == 0 else (pS2, B2k)
                for cc in range(NCH):
                    p.op("tensor", lambda t, e=e, cc=cc, half=half, bank=bank:
                         t.matmul(bank[0:100, 0:512],
                                  tksb[:, cc * 600 + e * 100:
                                       cc * 600 + (e + 1) * 100],
                                  wbsb[:, cc * D + half * 512:
                                       cc * D + half * 512 + 512],
                                  start=(cc == 0), stop=(cc == NCH - 1)),
                         reads=["tksb", "wbsb"], writes=[bk],
                         tag=f"vl{e}.{half}.{cc}")
                p.op("vector", lambda v, half=half, bank=bank: v.tensor_copy(
                    out=vtmp[:, 0:512], in_=bank[0:100, 0:512]),
                    reads=[bk], writes=["vtmp"], tag=f"vle{e}.{half}")
                bank2, bk2 = (pS2, B2k) if half == 0 else (pS1, B1)
                p.op("tensor", lambda t, half=half, bank2=bank2: t.matmul(
                    bank2[:, 0:512], ones_row32[0:1, :],
                    wtbs[0:1, half * 512: half * 512 + 512],
                    start=True, stop=True),
                    reads=["wtbs", "consts"], writes=[bk2], tag=f"vb{e}.{half}")
                p.op("vector", lambda v, e=e, half=half, bank2=bank2:
                     v.tensor_tensor(
                         out=valsb[:, e * D + half * 512:
                                   e * D + half * 512 + 512],
                         in0=vtmp[:, 0:512], in1=bank2[0:100, 0:512],
                         op=ALU.add),
                     reads=["vtmp", bk2], writes=["valsb"], tag=f"vf{e}.{half}")
            p.op("vector", lambda v, e=e: v.memset(
                valsb[0:1, e * D:(e + 1) * D], 0.0),
                writes=["valsb"], tag=f"vz{e}")
        esV.close()
        ctx.alloc_es = None
        p.barrier()
        p.dma("gpsimd",
              [(wbsb[:, c * D:(c + 1) * D], wdwT[c * 128:(c + 1) * 128, :])
               for c in range(NCH)], writes=["wbsb"], tag="ldwdw")

        PARQ = []
        for par in range(2):
            PARQ.append(dict(
                fch=ctx.sb(f"fch{par}", (128, NCH * CH), F32),
                lgt=ctx.sb(f"lgt{par}", (128, 6), F32),
                mxg=ctx.sb(f"mxg{par}", (128, 1), F32),
                g01=ctx.sb(f"g01{par}", (128, 6), F32),
                scx=ctx.sb(f"scx{par}", (128, 600), F32),
                mxe=ctx.sb(f"mxe{par}", (128, 6), F32),
                sms=ctx.sb(f"sms{par}", (128, 6), F32),
                smr=ctx.sb(f"smr{par}", (128, 6), F32),
                gnc=ctx.sb(f"gnc{par}", (128, 6), F32),
                asel=ctx.sb(f"asel{par}", (128, 600), BF16),
                aselT=ctx.sb(f"aselT{par}", (100, 6 * CH), BF16),
                dfx=ctx.sb(f"dfx{par}", (128, NCH * CH), BF16),
                cmb=[ctx.sb(f"cmb{par}_{j}", (128, CH), F32) for j in range(3)],
            ))
        lnxbS = ctx.shared("lnxb", (128, tss), BF16)
        lnsqS = ctx.shared("lnsq", (128, tss), BF16)
        lnt1S = ctx.shared("lnt1", (128, tss), F32)
        lngtS = ctx.shared("lngt", (128, tss), F32)
        lns1S = ctx.shared("lns1", (1, tok), F32)
        for par in range(2):
            PARQ[par]["zb"] = (lnxbS[:, par * CH:(par + 1) * CH]
                               if tss >= 2 * CH
                               else ctx.sb(f"zbl{par}", (128, CH), BF16))
            PARQ[par]["zq"] = (lnsqS[:, par * CH:(par + 1) * CH]
                               if tss >= 2 * CH
                               else ctx.sb(f"zsl{par}", (128, CH), BF16))
            PARQ[par]["t1"] = [
                (lnt1S[:, (2 * par + j) * CH:(2 * par + j + 1) * CH]
                 if tss >= (2 * par + j + 1) * CH
                 else ctx.sb(f"t1l{par}_{j}", (128, CH), F32))
                for j in range(2)]
            PARQ[par]["msk"] = [
                (lngtS[:, (2 * par + j) * CH:(2 * par + j + 1) * CH]
                 if tss >= (2 * par + j + 1) * CH
                 else ctx.sb(f"msk{par}_{j}", (128, CH), F32))
                for j in range(2)]
            PARQ[par]["lr"] = (lns1S[0:1, par * 3 * CH:(par + 1) * 3 * CH]
                               if tok >= 6 * CH
                               else ctx.sb(f"lrw{par}", (1, 3 * CH), F32))
        bksP = [[(pS1, B1), (pS2, B2k), (pY[2], KY[2])],
                [(pY[0], KY[0]), (pY[1], KY[1]), (pY[3], KY[3])]]
        pT1 = ctx.ps("pT1", (128, 1024), BF16)
        for c in range(tch):
            t0 = c * CH
            par = c % 2
            Q = PARQ[par]
            (bA, kA), (bB, kB), (bC, kC) = bksP[par]
            fch, scx, asel, aselT, dfx = (Q["fch"], Q["scx"], Q["asel"],
                                          Q["aselT"], Q["dfx"])
            lgt, mxg, g01, mxe, sms, smr, gnc = (Q["lgt"], Q["mxg"], Q["g01"],
                                                 Q["mxe"], Q["sms"], Q["smr"],
                                                 Q["gnc"])
            KF, KXC = f"fch{par}", f"xh.c{c}"
            KSC = [f"scx{par}.{e}" for e in range(6)]
            KAS = [f"asel{par}.{e}" for e in range(6)]
            p.op("tensor", lambda t, t0=t0, bA=bA: t.matmul(
                bA[:, 0:CH], onesb[0:1, :], arow6[:, t0:t0 + CH],
                start=True, stop=True),
                reads=["r6.a", "onesb"], writes=[kA], tag=f"6bA{c}")
            p.op("tensor", lambda t, t0=t0, bB=bB: t.matmul(
                bB[:, 0:CH], onesb[0:1, :], brow6[:, t0:t0 + CH],
                start=True, stop=True),
                reads=["r6.b", "onesb"], writes=[kB], tag=f"6bB{c}")
            for cc in range(NCH):
                sl = xh[:, cc * tok + t0: cc * tok + t0 + CH]
                p.op("vector", lambda v, sl=sl, cc=cc, fch=fch, bB=bB:
                     v.tensor_tensor(
                         out=fch[:, cc * CH:(cc + 1) * CH], in0=sl,
                         in1=bB[:, 0:CH], op=ALU.add),
                     reads=[KXH, KXC, kB], writes=[KF], tag=f"fc1{c}.{cc}")
                p.op("vector", lambda v, cc=cc, fch=fch, bA=bA:
                     v.tensor_tensor(
                         out=fch[:, cc * CH:(cc + 1) * CH],
                         in0=fch[:, cc * CH:(cc + 1) * CH],
                         in1=bA[:, 0:CH], op=ALU.mult),
                     reads=[KF, kA], writes=[KF], tag=f"fc2{c}.{cc}")
            for k in range(NCH):
                p.op("tensor", lambda t, k=k, t0=t0, bC=bC: t.matmul(
                    bC[:, 0:6], ffb[:, k * tok + t0: k * tok + t0 + CH],
                    wgsb[:, k * 6:(k + 1) * 6],
                    start=(k == 0), stop=(k == NCH - 1)),
                    reads=["ffb", "wgsb"], writes=[kC], tag=f"lg{c}.{k}")
            p.op("vector", lambda v, lgt=lgt, bC=bC: v.tensor_copy(
                out=lgt[:], in_=bC[:, 0:6]),
                reads=[kC], writes=[f"lgt{par}"], tag=f"lge{c}")
            p.op("vector", lambda v, lgt=lgt, mxg=mxg: v.reduce_max(
                out=mxg[:], in_=lgt[:], axis=AX.X),
                reads=[f"lgt{par}"], writes=[f"mxg{par}"], tag=f"mx{c}")
            p.op("vector", lambda v, lgt=lgt, mxg=mxg, g01=g01: v.tensor_scalar(
                out=g01[:], in0=lgt[:], scalar1=mxg[:, 0:1], scalar2=None,
                op0=ALU.is_equal),
                reads=[f"lgt{par}", f"mxg{par}"], writes=[f"g01{par}"],
                tag=f"oh{c}")
            for half, (lo, wdt_) in enumerate(((0, 512), (512, 88))):
                bank, bk = (bA, kA) if half == 0 else (bB, kB)
                for k in range(NCH):
                    p.op("tensor", lambda t, k=k, t0=t0, lo=lo, wdt_=wdt_,
                         bank=bank: t.matmul(
                             bank[:, 0:wdt_],
                             ffb[:, k * tok + t0: k * tok + t0 + CH],
                             tksb[:, k * 600 + lo: k * 600 + lo + wdt_],
                             start=(k == 0), stop=(k == NCH - 1)),
                         reads=["ffb", "tksb"], writes=[bk],
                         tag=f"sc{c}.{half}.{k}")
                p.op("vector", lambda v, lo=lo, wdt_=wdt_, bank=bank, scx=scx:
                     v.tensor_copy(out=scx[:, lo:lo + wdt_],
                                   in_=bank[:, 0:wdt_]),
                     reads=[bk], writes=KSC, tag=f"sce{c}.{half}")
            for e in range(6):
                p.op("vector", lambda v, e=e, scx=scx, mxe=mxe: v.reduce_max(
                    out=mxe[:, e:e + 1], in_=scx[:, e * 100:(e + 1) * 100],
                    axis=AX.X),
                    reads=[KSC[e]], writes=[f"mxe{par}"], tag=f"sm{c}.{e}")
                p.op("gpsimd", lambda g, e=e, scx=scx, mxe=mxe: g.tensor_scalar(
                    out=scx[:, e * 100:(e + 1) * 100],
                    in0=scx[:, e * 100:(e + 1) * 100],
                    scalar1=mxe[:, e:e + 1], scalar2=None, op0=ALU.subtract),
                    reads=[KSC[e], f"mxe{par}"], writes=[KSC[e]],
                    tag=f"ss{c}.{e}")
            for e in range(6):
                p.op("scalar", lambda s, e=e, scx=scx, sms=sms: s.activation(
                    out=scx[:, e * 100:(e + 1) * 100],
                    in_=scx[:, e * 100:(e + 1) * 100], func=AF.Exp,
                    scale=consts_sb[:, 135:136], accum_out=sms[:, e:e + 1]),
                    reads=[KSC[e], "consts"], writes=[KSC[e], f"sms{par}"],
                    tag=f"se{c}.{e}")
            for e in range(6):
                p.op("vector", lambda v, e=e, sms=sms, smr=smr: v.reciprocal(
                    out=smr[:, e:e + 1], in_=sms[:, e:e + 1]),
                    reads=[f"sms{par}"], writes=[f"smr{par}"], tag=f"sr{c}.{e}")
            p.op("gpsimd", lambda g, smr=smr, g01=g01, gnc=gnc:
                 g.tensor_tensor(
                     out=gnc[:, 0:6], in0=smr[:, 0:6],
                     in1=g01[:, 0:6], op=ALU.mult),
                 reads=[f"smr{par}", f"g01{par}"], writes=[f"gnc{par}"],
                 tag=f"gn{c}")
            for e in range(6):
                p.op("gpsimd", lambda g, e=e, scx=scx, asel=asel, gnc=gnc:
                     g.tensor_scalar(
                         out=asel[:, e * 100:(e + 1) * 100],
                         in0=scx[:, e * 100:(e + 1) * 100],
                         scalar1=gnc[:, e:e + 1], scalar2=None, op0=ALU.mult),
                     reads=[KSC[e], f"gnc{par}"], writes=[KAS[e]],
                     tag=f"as{c}.{e}")
            for e in range(6):
                p.op("tensor", lambda t, e=e, asel=asel: t.transpose(
                    pT1[0:100, e * CH:(e + 1) * CH],
                    asel[:, e * 100:(e + 1) * 100], identb[:]),
                    reads=[KAS[e], "identb"], writes=[f"b_T1.{e}"],
                    tag=f"tr{c}.{e}")
            for e in range(6):
                p.op("vector", lambda v, e=e, aselT=aselT: v.tensor_copy(
                    out=aselT[:, e * CH:(e + 1) * CH],
                    in_=pT1[0:100, e * CH:(e + 1) * CH]),
                    reads=[f"b_T1.{e}"], writes=[f"aselT{par}"],
                    tag=f"tre{c}.{e}")
            for dc in range(NCH):
                bank, bk = (bA, kA) if dc % 2 == 0 else (bB, kB)
                for e in range(6):
                    p.op("tensor", lambda t, dc=dc, e=e, bank=bank,
                         aselT=aselT: t.matmul(
                             bank[:, 0:CH],
                             valsb[:, e * D + dc * 128: e * D + (dc + 1) * 128],
                             aselT[:, e * CH:(e + 1) * CH],
                             start=(e == 0), stop=(e == 5)),
                         reads=["valsb", f"aselT{par}"], writes=[bk],
                         tag=f"df{c}.{dc}.{e}")
                p.op("vector", lambda v, dc=dc, bank=bank, fch=fch, dfx=dfx:
                     v.tensor_tensor(
                         out=dfx[:, dc * CH:(dc + 1) * CH],
                         in0=fch[:, dc * CH:(dc + 1) * CH],
                         in1=bank[:, 0:CH], op=ALU.add),
                     reads=[KF, bk], writes=[f"dfx{par}"], tag=f"dfe{c}.{dc}")
            def emit_mzgz(cp):
                cmb = Q["cmb"][cp % 3]
                msk = Q["msk"][cp % 2]
                kcm, kms = f"cmb{par}_{cp % 3}", f"msk{par}_{cp % 2}"
                p.op("vector", lambda v, cmb=cmb, msk=msk: v.tensor_scalar(
                    out=msk[:, 0:CH], in0=cmb[:], scalar1=0.0, scalar2=None,
                    op0=ALU.is_equal),
                    reads=[kcm], writes=[kms], tag=f"mz{c}.{cp}")
                p.op("vector", lambda v, cmb=cmb, msk=msk:
                     v.scalar_tensor_tensor(
                         out=cmb[:], in0=msk[:, 0:CH], scalar=consts_sb[:, 134:135],
                         in1=cmb[:], op0=ALU.mult, op1=ALU.add),
                     reads=[kms, kcm, "consts"], writes=[kcm],
                     tag=f"gz{c}.{cp}")

            def emit_lnbo(cp):
                cmb = Q["cmb"][cp % 3]
                kcm = f"cmb{par}_{cp % 3}"
                p.op("scalar", lambda s, cmb=cmb: s.activation(
                    out=cmb[:], in_=cmb[:], func=AF.Ln),
                    reads=[kcm], writes=[kcm], tag=f"ln{c}.{cp}")
                p.op("vector", lambda v, cp=cp, cmb=cmb, fch=fch:
                     v.tensor_tensor(
                         out=fch[:, cp * CH:(cp + 1) * CH],
                         in0=fch[:, cp * CH:(cp + 1) * CH], in1=cmb[:],
                         op=ALU.add),
                     reads=[kcm, KF], writes=[KF], tag=f"bo{c}.{cp}")

            for cp in range(NCH):
                bank, bk = (bA, kA) if cp % 2 == 0 else (bB, kB)
                cmb = Q["cmb"][cp % 3]
                kcm = f"cmb{par}_{cp % 3}"
                for dc in range(NCH):
                    p.op("tensor", lambda t, cp=cp, dc=dc, bank=bank, dfx=dfx:
                         t.matmul(
                             bank[:, 0:CH],
                             wbsb[:, dc * D + cp * 128: dc * D + (cp + 1) * 128],
                             dfx[:, dc * CH:(dc + 1) * CH],
                             start=(dc == 0), stop=(dc == NCH - 1)),
                         reads=["wbsb", f"dfx{par}"], writes=[bk],
                         tag=f"dl{c}.{cp}.{dc}")
                p.op("vector", lambda v, cp=cp, bank=bank, cmb=cmb:
                     v.tensor_scalar(
                         out=cmb[:], in0=bank[:, 0:CH],
                         scalar1=wdbs[:, cp:cp + 1], scalar2=scol[:, 0:1],
                         op0=ALU.add, op1=ALU.mult),
                     reads=[bk, "wdbs", "scol"], writes=[kcm],
                     tag=f"dle{c}.{cp}")
                p.op("scalar", lambda s, cmb=cmb: s.activation(
                    out=cmb[:], in_=cmb[:], func=AF.Exp),
                    reads=[kcm], writes=[kcm], tag=f"ex{c}.{cp}")
                if cp >= 1:
                    emit_mzgz(cp - 1)
                if cp >= 2:
                    emit_lnbo(cp - 2)
            emit_mzgz(NCH - 1)
            emit_lnbo(NCH - 2)
            emit_lnbo(NCH - 1)
            zbP, zqP = Q["zb"], Q["zq"]
            for cc in range(NCH):
                fl = fch[:, cc * CH:(cc + 1) * CH]
                p.op("vector", lambda v, fl=fl, zbP=zbP: v.tensor_copy(
                    out=zbP[:, 0:CH], in_=fl),
                    reads=[KF], writes=[f"zbl{par}"], tag=f"pc{c}.{cc}")
                p.op("scalar", lambda s, fl=fl, zqP=zqP: s.activation(
                    out=zqP[:, 0:CH], in_=fl, func=AF.Square),
                    reads=[KF], writes=[f"zsl{par}"], tag=f"ps{c}.{cc}")
                p.op("tensor", lambda t, cc=cc, zbP=zbP, bC=bC: t.matmul(
                    bC[0:1, 0:CH], ones_col[:, 0:1], zbP[:, 0:CH],
                    start=(cc == 0), stop=(cc == NCH - 1)),
                    reads=[f"zbl{par}", "ones_col"], writes=[kC],
                    tag=f"pm1{c}.{cc}")
                p.op("tensor", lambda t, cc=cc, zqP=zqP, bA=bA: t.matmul(
                    bA[0:1, 0:CH], ones_col[:, 0:1], zqP[:, 0:CH],
                    start=(cc == 0), stop=(cc == NCH - 1)),
                    reads=[f"zsl{par}", "ones_col"], writes=[kA],
                    tag=f"pm2{c}.{cc}")
            lr = Q["lr"]
            klr = f"lrw{par}"
            ar = lr[0:1, 0:CH]
            br = lr[0:1, CH:2 * CH]
            sq = lr[0:1, 2 * CH:3 * CH]
            p.op("vector", lambda v, br=br, bC=bC: v.tensor_scalar(
                out=br, in0=bC[0:1, 0:CH],
                scalar1=-1.0 / D, scalar2=None, op0=ALU.mult),
                reads=[kC], writes=[klr], tag=f"pb{c}")
            p.op("vector", lambda v, ar=ar, bA=bA: v.tensor_scalar(
                out=ar, in0=bA[0:1, 0:CH],
                scalar1=1.0 / D, scalar2=None, op0=ALU.mult),
                reads=[kA], writes=[klr], tag=f"pa{c}")
            p.op("vector", lambda v, sq=sq, br=br: v.tensor_tensor(
                out=sq, in0=br, in1=br, op=ALU.mult),
                reads=[klr], writes=[klr], tag=f"pq{c}")
            p.op("vector", lambda v, ar=ar, sq=sq: v.tensor_tensor(
                out=ar, in0=ar, in1=sq, op=ALU.subtract),
                reads=[klr], writes=[klr], tag=f"pv{c}")
            p.op("scalar", lambda s, ar=ar: s.activation(
                out=ar, in_=ar, func=AF.Ln, bias=consts_sb[0:1, 130:131]),
                reads=[klr, "consts"], writes=[klr], tag=f"pl{c}")
            p.op("scalar", lambda s, ar=ar: s.activation(
                out=ar, in_=ar, func=AF.Exp, scale=consts_sb[0:1, 131:132]),
                reads=[klr, "consts"], writes=[klr], tag=f"pr{c}")
            p.op("tensor", lambda t, ar=ar, bA=bA: t.matmul(
                bA[:, 0:CH], ones_row32[0:1, :], ar,
                start=True, stop=True),
                reads=[klr, "consts"], writes=[kA], tag=f"pA{c}")
            p.op("tensor", lambda t, br=br, bB=bB: t.matmul(
                bB[:, 0:CH], ones_row32[0:1, :], br,
                start=True, stop=True),
                reads=[klr, "consts"], writes=[kB], tag=f"pB{c}")
            for cc in range(NCH):
                fl = fch[:, cc * CH:(cc + 1) * CH]
                sl = xh[:, cc * tok + t0: cc * tok + t0 + CH]
                t1 = Q["t1"][cc % 2]
                kt1 = f"t1l{par}_{cc % 2}"
                p.op("vector", lambda v, fl=fl, t1=t1, bB=bB: v.tensor_tensor(
                    out=t1[:, 0:CH], in0=fl, in1=bB[:, 0:CH], op=ALU.add),
                    reads=[KF, kB], writes=[kt1], tag=f"q1{c}.{cc}")
                p.op("vector", lambda v, t1=t1, bA=bA: v.tensor_tensor(
                    out=t1[:, 0:CH], in0=t1[:, 0:CH], in1=bA[:, 0:CH],
                    op=ALU.mult),
                    reads=[kt1, kA], writes=[kt1], tag=f"q2{c}.{cc}")
                p.op("vector", lambda v, sl=sl, t1=t1: v.tensor_tensor(
                    out=sl, in0=sl, in1=t1[:, 0:CH], op=ALU.add),
                    reads=[kt1, KXC], writes=[KXC], tag=f"q3{c}.{cc}")

        p.dma("sync", [(o_x[c * 128:(c + 1) * 128, :],
                        xh[:, c * tok:(c + 1) * tok]) for c in range(NCH)],
              reads=[KXH] + [f"xh.c{c}" for c in range(tch)], tag="stx")

        p.emit()
    return nc



def make_consts():
    c = np.zeros((128, 136), np.float32)
    c[:, 0] = 1.0
    s = np.arange(128)[:, None]
    t = np.arange(128)[None, :]
    c[:, 2:130] = (s <= t).astype(np.float32)
    c[:, 130] = EPS_LN
    c[:, 131] = -0.5
    c[:, 132] = EPS_DEN
    c[:, 133] = -1.0
    c[:, 134] = float(np.finfo(np.float32).eps)
    c[:, 135] = 1.0 / 32.0
    return c


def col_tile(vec):
    return np.ascontiguousarray(vec.reshape(8, 128).T).astype(np.float32)


def make_sel16():
    s = np.zeros((16, 16 * 128), np.float32)
    for n in range(16):
        s[n, n * 128:(n + 1) * 128] = 1.0
    return s


_NC_CACHE = {}


def _get_nc(name, builder, tok):
    key = (name, tok)
    if key not in _NC_CACHE:
        _NC_CACHE[key] = builder(tok)
    return _NC_CACHE[key]


def kernel(**inputs):
    TOKPC = 2048
    x = np.asarray(inputs["L0_attn_x"], np.float32)
    xT = np.ascontiguousarray(np.transpose(x, (0, 2, 1)))
    consts = make_consts()
    sel16 = make_sel16()
    identm = np.eye(128, dtype=np.float32)

    def shard(i):
        b, s0 = i // 4, (i % 4) * TOKPC
        return np.ascontiguousarray(xT[b][:, s0:s0 + TOKPC])

    nc1 = _get_nc("l1", build_launch1, TOKPC)
    wq = np.ascontiguousarray(inputs["L0_attn_Wq"], np.float32)
    wk = np.ascontiguousarray(inputs["L0_attn_Wk"], np.float32)
    wv = np.ascontiguousarray(inputs["L0_attn_Wv"], np.float32)
    in1 = [{"xin": shard(i), "wq": wq, "wk": wk, "wv": wv,
            "ln1g": col_tile(np.asarray(inputs["L0_attn_ln1_g"], np.float32)),
            "ln1b": col_tile(np.asarray(inputs["L0_attn_ln1_b"], np.float32)),
            "consts": consts} for i in range(NCORES)]
    r1 = run_bass_kernel_spmd(nc1, in1, core_ids=list(range(NCORES))).results

    nc2 = _get_nc("l2", build_launch2, TOKPC)
    sall = np.ascontiguousarray(
        np.concatenate([r1[i]["stot"] for i in range(NCORES)], axis=1))
    def mk_mcol(i, rows):
        m = np.zeros((rows, NCORES), np.float32)
        g0 = (i // 4) * 4
        m[:, g0:i] = 1.0
        return m
    wo = np.ascontiguousarray(inputs["L0_attn_Wo"], np.float32)
    wf1 = np.ascontiguousarray(inputs["L0_attn_Wf1"], np.float32)
    wf2 = np.ascontiguousarray(inputs["L0_attn_Wf2"], np.float32)
    cbt = np.ascontiguousarray(np.asarray(inputs["L0_ffn_CB"], np.float32).T)
    in2 = [{"xin": shard(i), "abrows": r1[i]["abrows"], "qin": r1[i]["qout"],
            "ndin": r1[i]["numden"], "sall": sall, "mcol": mk_mcol(i, 128),
            "wo": wo, "wf1": wf1, "wf2": wf2,
            "ln2g": col_tile(np.asarray(inputs["L0_attn_ln2_g"], np.float32)),
            "ln2b": col_tile(np.asarray(inputs["L0_attn_ln2_b"], np.float32)),
            "cbt": cbt,
            "scb": col_tile(np.asarray(inputs["L0_ffn_SCB"], np.float32)),
            "fbias": col_tile(np.asarray(inputs["L0_ffn_bias"], np.float32)),
            "ident": identm, "consts": consts} for i in range(NCORES)]
    r2 = run_bass_kernel_spmd(nc2, in2, core_ids=list(range(NCORES))).results

    nc3 = _get_nc("l3", build_launch3, TOKPC)
    wxm = np.asarray(inputs["L1_attn_Wx"], np.float32)
    wbt = np.ascontiguousarray(wxm[:16].T)
    wct = np.ascontiguousarray(wxm[16:32].T)
    wdt = np.ascontiguousarray(wxm[32:].T)
    alog = np.ascontiguousarray(
        np.asarray(inputs["L1_attn_A_log"], np.float32)[None, :])
    dvec = col_tile(np.asarray(inputs["L1_attn_D"], np.float32))
    in3 = [{"xin": r2[i]["x2"], "wbt": wbt, "wct": wct, "wdt": wdt,
            "alog": alog, "dvec": dvec, "sel16": sel16, "ident": identm,
            "consts": consts} for i in range(NCORES)]
    r3 = run_bass_kernel_spmd(nc3, in3, core_ids=list(range(NCORES))).results

    nc4 = _get_nc("l4", build_launch4, TOKPC)
    accall = np.ascontiguousarray(
        np.concatenate([r3[i]["acc"] for i in range(NCORES)], axis=1))
    decall = np.ascontiguousarray(
        np.concatenate([r3[i]["dec"] for i in range(NCORES)], axis=1))
    woutT = np.ascontiguousarray(np.asarray(inputs["L1_attn_Wout"],
                                            np.float32).T)
    Am = np.asarray(inputs["L1_ffn_A"], np.float32)
    Bm = np.asarray(inputs["L1_ffn_B"], np.float32)
    atmat = np.ascontiguousarray(Am.transpose(0, 2, 1)).reshape(96, 100)
    bmat = np.ascontiguousarray(Bm.reshape(96, 1024))
    wgate = np.ascontiguousarray(np.asarray(inputs["L1_ffn_w_gate"],
                                            np.float32))
    wtwT = np.ascontiguousarray(np.asarray(inputs["L1_ffn_wt_w"],
                                           np.float32).T)
    wtbrow = np.ascontiguousarray(
        np.asarray(inputs["L1_ffn_wt_b"], np.float32)[None, :])
    wdwT = np.ascontiguousarray(np.asarray(inputs["L1_ffn_wd_w"],
                                           np.float32).T)
    wdb = col_tile(np.asarray(inputs["L1_ffn_wd_b"], np.float32))
    scalein = np.ascontiguousarray(
        np.asarray(inputs["L1_ffn_scale"], np.float32).reshape(1, 1))
    in4 = [{"xin": r2[i]["x2"], "y0in": r3[i]["y0"], "din": r3[i]["dout"],
            "cin": r3[i]["crow"], "accall": accall, "decall": decall,
            "mcol8": mk_mcol(i, 128), "alog": alog, "sel16": sel16,
            "woutT": woutT, "wgate": wgate, "atmat": atmat, "bmat": bmat,
            "wtwT": wtwT, "wtbrow": wtbrow, "wdwT": wdwT, "wdb": wdb,
            "scalein": scalein, "ident": identm, "consts": consts}
           for i in range(NCORES)]
    r4 = run_bass_kernel_spmd(nc4, in4, core_ids=list(range(NCORES))).results

    outT = np.zeros((2, D, 8192), np.float32)
    for i in range(NCORES):
        b, s0 = i // 4, (i % 4) * TOKPC
        outT[b][:, s0:s0 + TOKPC] = r4[i]["xout"]
    return np.ascontiguousarray(np.transpose(outT, (0, 2, 1)))


# version 23
# speedup vs baseline: 1.0436x; 1.0156x over previous
import contextlib
import numpy as np
import ml_dtypes

import concourse.bass as bass
import concourse.mybir as mybir
from concourse.bass_utils import run_bass_kernel_spmd

F32 = mybir.dt.float32
BF16 = mybir.dt.bfloat16
AF = mybir.ActivationFunctionType
ALU = mybir.AluOpType
AX = mybir.AxisListType

NCORES = 8
D = 1024
NCH = 8
NHEAD = 16
DH = 64
CH = 128
EPS_LN = 1e-5
EPS_DEN = 1e-6

BF = ml_dtypes.bfloat16



class Prog:
    COMPUTE_ENGINES = ("tensor", "vector", "scalar", "gpsimd")
    QUEUES = ("sync", "gpsimd", "scalar")

    def __init__(self, nc):
        self.nc = nc
        self.ops = []
        self.lastw = {}
        self.readers = {}
        self.qcount = {q: 0 for q in self.QUEUES}

    def _deps(self, reads, writes):
        deps = set()
        gb = self.lastw.get("__mem")
        if gb is not None:
            deps.add(gb)
        for k in reads:
            w = self.lastw.get(k)
            if w is not None:
                deps.add(w)
        for k in writes:
            w = self.lastw.get(k)
            if w is not None:
                deps.add(w)
            deps.update(self.readers.get(k, ()))
        return deps

    def _record(self, idx, reads, writes):
        for k in reads:
            self.readers.setdefault(k, []).append(idx)
        for k in writes:
            self.lastw[k] = idx
            self.readers[k] = []

    def op(self, engine, fn, reads=(), writes=(), tag=""):
        deps = self._deps(reads, writes)
        idx = len(self.ops)
        self.ops.append(dict(i=idx, eng=engine, kind="c", fn=fn, deps=deps,
                             tag=tag, inc=False))
        self._record(idx, reads, writes)
        return idx

    def barrier(self):
        last = {}
        for o in self.ops:
            last[o["eng"]] = o["i"]
        idxs = []
        for eng in ("sync", "gpsimd", "scalar", "tensor"):
            if eng in last or True:
                i = len(self.ops)
                deps = set(last.values())
                self.ops.append(dict(i=i, eng=eng, kind="c",
                                     fn=lambda e: e.nop(), deps=deps,
                                     tag=f"bar.{eng}", inc=False))
                idxs.append(i)
                last[eng] = i
        i = len(self.ops)
        self.ops.append(dict(i=i, eng="vector", kind="c",
                             fn=lambda e: e.nop(), deps=set(idxs),
                             tag="bar.vector", inc=False))
        self.lastw["__mem"] = i
        self.readers["__mem"] = []

    def dma(self, queue, transfers, reads=(), writes=(), tag=""):
        deps = self._deps(reads, writes)
        idx = len(self.ops)
        self.qcount[queue] += len(transfers)
        target = 16 * self.qcount[queue]
        self.ops.append(dict(i=idx, eng=queue, kind="dma", transfers=transfers,
                             deps=deps, tag=tag, inc=False))
        fidx = idx + 1
        self.ops.append(dict(i=fidx, eng=queue, kind="fence", deps=set(),
                             qtarget=target, tag=tag + ".fence", inc=False))
        for k in reads:
            self.readers.setdefault(k, []).append(fidx)
        for k in writes:
            self.lastw[k] = fidx
            self.readers[k] = []
        return fidx

    def emit(self):
        nc = self.nc
        ops = self.ops

        def waitable(d):
            return d + 1 if ops[d]["kind"] == "dma" else d

        for o in ops:
            o["deps"] = {waitable(d) for d in o["deps"] if waitable(d) != o["i"]}

        for o in ops:
            for d in o["deps"]:
                pd = ops[d]
                if pd["kind"] == "c" and pd["eng"] != o["eng"]:
                    pd["inc"] = True

        streams = {}
        for o in ops:
            streams.setdefault(o["eng"], []).append(o)
        for eng, lst in streams.items():
            c = 0
            for pos, o in enumerate(lst):
                if o.get("inc"):
                    c += 1
                o["cumval"] = c
                o["pos"] = pos

        with contextlib.ExitStack() as es:
            qsem = {q: es.enter_context(nc.semaphore(f"q_{q}"))
                    for q in self.QUEUES}
            csem = {e: es.enter_context(nc.semaphore(f"c_{e}"))
                    for e in self.COMPUTE_ENGINES + ("sync",)}
            block = es.enter_context(nc.Block())

            def emit_stream(eng_obj, eng_name):
                lst = streams.get(eng_name, [])
                high = {}
                drained = -1
                for o in lst:
                    need_drain = False
                    for d in sorted(o["deps"]):
                        pd = ops[d]
                        if pd["eng"] == eng_name:
                            if (pd["kind"] == "c" and eng_name != "tensor"
                                    and pd["pos"] > drained):
                                need_drain = True
                            continue
                        if pd["kind"] == "fence":
                            sem, val = qsem[pd["eng"]], pd["qtarget"]
                        else:
                            sem, val = csem[pd["eng"]], pd["cumval"]
                        if high.get(id(sem), 0) < val:
                            eng_obj.wait_ge(sem, val)
                            high[id(sem)] = val
                    if need_drain:
                        eng_obj.drain()
                        drained = o["pos"] - 1
                    if o["kind"] == "c":
                        inst = o["fn"](eng_obj)
                        if o.get("inc"):
                            if inst is None:
                                raise RuntimeError(f"op {o['tag']}: no inst")
                            inst.then_inc(csem[eng_name], 1)
                    elif o["kind"] == "dma":
                        for tr in o["transfers"]:
                            kw = tr[2] if len(tr) > 2 else {}
                            eng_obj.dma_start(out=tr[0], in_=tr[1], **kw) \
                                .then_inc(qsem[eng_name], 16)
                    else:
                        eng_obj.wait_ge(qsem[eng_name], o["qtarget"])

            for name in ("sync", "gpsimd", "vector", "scalar", "tensor"):
                getattr(block, name)(lambda e, n=name: emit_stream(e, n))


class Ctx:
    def __init__(self, nc, tok):
        self.nc = nc
        self.p = Prog(nc)
        self.es = contextlib.ExitStack()
        self.tok = tok
        self.tss = min(512, tok)
        self.nts = tok // self.tss
        self.tch = tok // CH
        self.uid = 0

    def sb(self, name, shape, dtype):
        es = getattr(self, "alloc_es", None) or self.es
        return es.enter_context(
            self.nc.sbuf_tensor(name, list(shape), dtype))

    def ps(self, name, shape, dtype=F32):
        return self.es.enter_context(
            self.nc.psum_tensor(name, list(shape), dtype))

    def name(self, base):
        self.uid += 1
        return f"{base}{self.uid}"

    def prealloc_ln(self):
        self.shared("lnxb", (128, self.tss), BF16)
        self.shared("lnsq", (128, self.tss), BF16)
        self.shared("lns1", (1, self.tok), F32)
        self.shared("lns2", (1, self.tok), F32)
        self.shared("lnt1", (128, self.tss), F32)
        self.shared("lngt", (128, self.tss), F32)

    def shared(self, name, shape, dtype):
        cache = getattr(self, "_shared", None)
        if cache is None:
            cache = self._shared = {}
        if name not in cache:
            if getattr(self, "scoped_shared", False):
                es = getattr(self, "alloc_es", None) or self.es
            else:
                es = self.es
            cache[name] = es.enter_context(
                self.nc.sbuf_tensor(name, list(shape), dtype))
        return cache[name]



def emit_ln_stats(ctx, src, src_key, rows, rows_key, pA, pB, bkA, bkB,
                  ones_col, eps=EPS_LN, tag="ln"):
    p = ctx.p
    tok, tss, nts = ctx.tok, ctx.tss, ctx.nts
    xb = ctx.shared("lnxb", (128, tss), BF16)
    sq = ctx.shared("lnsq", (128, tss), BF16)
    s1 = ctx.shared("lns1", (1, tok), F32)
    s2 = ctx.shared("lns2", (1, tok), F32)
    kxb, ksq, ks1, ks2 = "lnxb", "lnsq", "lns1", "lns2"
    for ts in range(nts):
        t0 = ts * tss
        for c in range(NCH):
            sl = src[:, c * tok + t0: c * tok + t0 + tss]
            p.op("vector", lambda v, sl=sl: v.tensor_copy(out=xb[:], in_=sl),
                 reads=[src_key], writes=[kxb], tag=f"{tag}.cast")
            p.op("scalar", lambda s, sl=sl: s.activation(
                out=sq[:], in_=sl, func=AF.Square),
                reads=[src_key], writes=[ksq], tag=f"{tag}.sq")
            p.op("tensor", lambda t, st=(c == 0), sp=(c == NCH - 1): t.matmul(
                pA[0:1, 0:tss], ones_col[:, 0:1], xb[:], start=st, stop=sp),
                reads=[kxb, "ones_col"], writes=[bkA], tag=f"{tag}.mm1")
            p.op("tensor", lambda t, st=(c == 0), sp=(c == NCH - 1): t.matmul(
                pB[0:1, 0:tss], ones_col[:, 0:1], sq[:], start=st, stop=sp),
                reads=[ksq, "ones_col"], writes=[bkB], tag=f"{tag}.mm2")
        p.op("vector", lambda v, t0=t0: v.tensor_copy(
            out=s1[:, t0:t0 + tss], in_=pA[0:1, 0:tss]),
            reads=[bkA], writes=[ks1], tag=f"{tag}.ev1")
        p.op("vector", lambda v, t0=t0: v.tensor_copy(
            out=s2[:, t0:t0 + tss], in_=pB[0:1, 0:tss]),
            reads=[bkB], writes=[ks2], tag=f"{tag}.ev2")
    a_row, b_row = rows["a"], rows["b"]
    ka, kb = rows_key + ".a", rows_key + ".b"
    p.op("vector", lambda v: v.tensor_scalar(
        out=b_row[:], in0=s1[:], scalar1=-1.0 / D, scalar2=None, op0=ALU.mult),
        reads=[ks1], writes=[kb], tag=f"{tag}.bneg")
    p.op("vector", lambda v: v.tensor_scalar(
        out=a_row[:], in0=s2[:], scalar1=1.0 / D, scalar2=None, op0=ALU.mult),
        reads=[ks2], writes=[ka], tag=f"{tag}.q")
    p.op("vector", lambda v: v.tensor_tensor(
        out=s1[:], in0=b_row[:], in1=b_row[:], op=ALU.mult),
        reads=[kb], writes=[ks1], tag=f"{tag}.msq")
    p.op("vector", lambda v: v.tensor_tensor(
        out=a_row[:], in0=a_row[:], in1=s1[:], op=ALU.subtract),
        reads=[ka, ks1], writes=[ka], tag=f"{tag}.var")
    csb, ckey = ctx.csb, ctx.ckey
    p.op("scalar", lambda s: s.activation(
        out=a_row[:], in_=a_row[:], func=AF.Ln, bias=csb[0:1, 130:131]),
        reads=[ka, ckey], writes=[ka], tag=f"{tag}.logv")
    p.op("scalar", lambda s: s.activation(
        out=a_row[:], in_=a_row[:], func=AF.Exp, scale=csb[0:1, 131:132]),
        reads=[ka, ckey], writes=[ka], tag=f"{tag}.rstd")


def emit_ln_apply(ctx, src, src_key, rows, rows_key, ones_row32,
                  pA, pB, bkA, bkB, outs, tag, or_key="consts"):
    p = ctx.p
    tok, tss, nts = ctx.tok, ctx.tss, ctx.nts
    t1 = ctx.shared("lnt1", (128, tss), F32)
    kt1 = "lnt1"
    for ts in range(nts):
        t0 = ts * tss
        p.op("tensor", lambda t, t0=t0: t.matmul(
            pA[:, 0:tss], ones_row32[0:1, :], rows["a"][:, t0:t0 + tss],
            start=True, stop=True),
            reads=[rows_key + ".a", or_key], writes=[bkA],
            tag=f"{tag}.bcA")
        p.op("tensor", lambda t, t0=t0: t.matmul(
            pB[:, 0:tss], ones_row32[0:1, :], rows["b"][:, t0:t0 + tss],
            start=True, stop=True),
            reads=[rows_key + ".b", or_key], writes=[bkB],
            tag=f"{tag}.bcB")
        for c in range(NCH):
            sl = src[:, c * tok + t0: c * tok + t0 + tss]
            p.op("vector", lambda v, sl=sl: v.tensor_tensor(
                out=t1[:], in0=sl, in1=pB[:, 0:tss], op=ALU.add),
                reads=[src_key, bkB], writes=[kt1], tag=f"{tag}.t1")
            for oi, o in enumerate(outs):
                osl = o["tile"][:, c * tok + t0: c * tok + t0 + tss]
                if o.get("gb") is None:
                    p.op("vector", lambda v, osl=osl: v.tensor_tensor(
                        out=osl, in0=t1[:], in1=pA[:, 0:tss], op=ALU.mult),
                        reads=[kt1, bkA], writes=[o["key"]],
                        tag=f"{tag}.ap{oi}")
                else:
                    g_col, b_col, gb_key = o["gb"]
                    tmp, ktmp = o["tmp"], o["tmpkey"]
                    p.op("vector", lambda v, tmp=tmp: v.tensor_tensor(
                        out=tmp[:], in0=t1[:], in1=pA[:, 0:tss], op=ALU.mult),
                        reads=[kt1, bkA], writes=[ktmp], tag=f"{tag}.apg{oi}")
                    p.op("vector", lambda v, osl=osl, c=c: v.tensor_scalar(
                        out=osl, in0=tmp[:], scalar1=g_col[:, c:c + 1],
                        scalar2=b_col[:, c:c + 1], op0=ALU.mult, op1=ALU.add),
                        reads=[ktmp, gb_key], writes=[o["key"]],
                        tag=f"{tag}.gb{oi}")



def build_launch1(tok):
    nc = bass.Bass("TRN2", target_bir_lowering=False)
    tch = tok // CH
    xin = nc.dram_tensor("xin", [D, tok], F32, kind="ExternalInput").ap()
    wq = nc.dram_tensor("wq", [D, D], F32, kind="ExternalInput").ap()
    wk = nc.dram_tensor("wk", [D, D], F32, kind="ExternalInput").ap()
    wv = nc.dram_tensor("wv", [D, D], F32, kind="ExternalInput").ap()
    ln1g = nc.dram_tensor("ln1g", [128, 8], F32, kind="ExternalInput").ap()
    ln1b = nc.dram_tensor("ln1b", [128, 8], F32, kind="ExternalInput").ap()
    consts = nc.dram_tensor("consts", [128, 136], F32, kind="ExternalInput").ap()
    o_abrows = nc.dram_tensor("abrows", [4, tok], F32, kind="ExternalOutput").ap()
    o_qfull = nc.dram_tensor("qout", [128, 8 * tok], BF16,
                             kind="ExternalOutput").ap()
    o_numden = nc.dram_tensor("numden", [tok, NHEAD * 65], F32,
                              kind="ExternalOutput").ap()
    o_stot = nc.dram_tensor("stot", [128, 8 * 65], F32,
                            kind="ExternalOutput").ap()
    o_dbg = nc.dram_tensor("dbg", [128, NHEAD * 65], F32,
                           kind="ExternalOutput").ap()
    o_dbg2 = nc.dram_tensor("dbg_mst", [128, NHEAD * CH], BF16,
                            kind="ExternalOutput").ap()
    o_dbg3 = nc.dram_tensor("dbg_vtm", [128, NHEAD * 65], BF16,
                            kind="ExternalOutput").ap()
    o_dbg4 = nc.dram_tensor("dbg_kfm", [64, NHEAD * CH], BF16,
                            kind="ExternalOutput").ap()
    o_dbg5 = nc.dram_tensor("dbg_ktm", [128, NHEAD * DH], BF16,
                            kind="ExternalOutput").ap()

    ctx = Ctx(nc, tok)
    p = ctx.p
    tss, nts = ctx.tss, ctx.nts

    with ctx.es:
        xt = ctx.sb("xt", (128, NCH * tok), F32)
        xab = ctx.sb("xab", (128, NCH * tok), BF16)
        qfull = ctx.sb("qfull", (128, 8 * tok), BF16)
        ones_col = ctx.sb("ones_col", (128, 1), BF16)
        mask = ctx.sb("mask", (128, CH), BF16)
        consts_sb = ctx.sb("consts_sb", (128, 136), F32)
        ctx.csb = consts_sb
        ctx.ckey = "consts"
        ctx.prealloc_ln()
        gtmp = ctx.shared("lngt", (128, tss), F32)
        g1 = ctx.sb("g1", (128, 8), F32)
        b1 = ctx.sb("b1", (128, 8), F32)
        s_run = ctx.sb("s_run", (128, 8 * 65), F32)
        s_pref = ctx.sb("s_pref", (128, 8 * 65), BF16)
        kfm = ctx.sb("kfm", (128, 8 * CH), BF16)
        ktm = ctx.sb("ktm", (128, NHEAD * DH), BF16)
        vtm = ctx.sb("vtm", (128, NHEAD * 65), BF16)
        mst = ctx.sb("mst", (128, NHEAD * CH), BF16)
        nst = ctx.sb("nst", (128, NHEAD * 65), F32)
        phit = ctx.sb("phit", (128, 512), F32)
        phie = ctx.sb("phie", (128, 512), F32)

        pA = ctx.ps("pA", (128, 512))
        pB = ctx.ps("pB", (128, 512))
        pC = ctx.ps("pC", (128, 512))
        pD = ctx.ps("pD", (128, 512))
        pE = ctx.ps("pE", (128, 512))
        pF = ctx.ps("pF", (128, 512))
        pG = ctx.ps("pG", (128, 512))
        pH = ctx.ps("pH", (128, 512))

        KX, KXAB, KQ = "xt", "xab", "qfull"
        KW, KWV = "wsb", "wvsb"
        KSRUN, KSPREF = "srun", "spref"
        KKFM, KKTM, KVTM, KMST, KNST = "kfm", "ktm", "vtm", "mst", "nst"
        KPH1, KPH2 = "phit", "phie"
        BA, BB, BC, BD, BE, BFb, BG, BH = ("b_A", "b_B", "b_C", "b_D", "b_E", "b_F", "b_G", "b_H")

        p.dma("sync", [(xt[:, c * tok:(c + 1) * tok],
                        xin[c * 128:(c + 1) * 128, :]) for c in range(NCH)],
              writes=[KX], tag="ldx")
        p.dma("sync", [(consts_sb[:], consts[:]),
                       (g1[:], ln1g[:]), (b1[:], ln1b[:])],
              writes=["consts", "g1b1"], tag="ldc")
        p.op("vector", lambda v: v.tensor_copy(
            out=ones_col[:], in_=consts_sb[:, 0:1]),
            reads=["consts"], writes=["ones_col"], tag="mkones")
        p.op("vector", lambda v: v.tensor_copy(
            out=mask[:], in_=consts_sb[:, 2:130]),
            reads=["consts"], writes=["mask"], tag="mkmask")
        ones_row32 = consts_sb[0:1, 2:130]

        with contextlib.ExitStack() as esL:
            ctx.alloc_es = esL
            rows0 = {"a": ctx.sb("r0a", (1, tok), F32),
                     "b": ctx.sb("r0b", (1, tok), F32)}
            rows1 = {"a": ctx.sb("r1a", (1, tok), F32),
                     "b": ctx.sb("r1b", (1, tok), F32)}
            emit_ln_stats(ctx, xt, KX, rows0, "r0", pA, pB, BA, BB, ones_col,
                          tag="ln0")
            emit_ln_apply(ctx, xt, KX, rows0, "r0", ones_row32, pA, pB, BA, BB,
                          [dict(tile=xt, key=KX)], tag="ln0ap")
            emit_ln_stats(ctx, xt, KX, rows1, "r1", pA, pB, BA, BB, ones_col,
                          tag="ln1")
            emit_ln_apply(ctx, xt, KX, rows1, "r1", ones_row32, pA, pB, BA, BB,
                          [dict(tile=xab, key=KXAB, gb=(g1, b1, "g1b1"),
                                tmp=gtmp, tmpkey="lngt")], tag="ln1ap")
            p.dma("sync", [(o_abrows[0:1, :], rows0["a"][:]),
                           (o_abrows[1:2, :], rows0["b"][:]),
                           (o_abrows[2:3, :], rows1["a"][:]),
                           (o_abrows[3:4, :], rows1["b"][:])],
                  reads=["r0.a", "r0.b", "r1.a", "r1.b"], tag="stab")
        ctx.alloc_es = None
        p.barrier()
        wsb = ctx.sb("wsb", (128, NCH * D), BF16)
        wvsb = ctx.sb("wvsb", (128, NCH * D), BF16)
        p.dma("gpsimd",
              [(wsb[:, c * D:(c + 1) * D], wq[c * 128:(c + 1) * 128, :])
               for c in range(NCH)], writes=[KW], tag="ldwq")

        def phi_to(bank, bkey, dsts, dkey, nsz, tag, full=False):
            p.op("vector", lambda v: v.tensor_scalar(
                out=phit[:, 0:nsz], in0=bank[:, 0:nsz], scalar1=0.0,
                scalar2=None, op0=ALU.min),
                reads=[bkey], writes=[KPH1], tag=f"{tag}.min")
            p.op("scalar", lambda s: s.activation(
                out=phie[:, 0:nsz], in_=phit[:, 0:nsz], func=AF.Exp),
                reads=[KPH1], writes=[KPH2], tag=f"{tag}.exp")
            if full:
                dst = dsts[0][0]
                p.op("vector", lambda v, dst=dst: v.scalar_tensor_tensor(
                    out=dst, in0=bank[:, 0:nsz], scalar=0.0,
                    in1=phie[:, 0:nsz], op0=ALU.max, op1=ALU.add),
                    reads=[bkey, KPH2], writes=[dkey], tag=f"{tag}.sttF")
                return
            for (dst, lo) in dsts:
                p.op("vector", lambda v, dst=dst, lo=lo: v.scalar_tensor_tensor(
                    out=dst, in0=bank[lo:lo + 64, 0:nsz], scalar=0.0,
                    in1=phie[lo:lo + 64, 0:nsz], op0=ALU.max, op1=ALU.add),
                    reads=[bkey, KPH2], writes=[dkey], tag=f"{tag}.stt{lo}")

        for m in range(NCH):
            for ts in range(nts):
                bank, bkey = ((pC, BC) if (m * nts + ts) % 2 == 0 else (pD, BD))
                t0 = ts * tss
                for k in range(NCH):
                    lhs = wsb[:, k * D + m * 128:(k * D + (m + 1) * 128)]
                    rhs = xab[:, k * tok + t0: k * tok + t0 + tss]
                    p.op("tensor", lambda t, lhs=lhs, rhs=rhs, k=k, bank=bank:
                         t.matmul(bank[:, 0:tss], lhs, rhs, start=(k == 0),
                                  stop=(k == NCH - 1)),
                         reads=[KW, KXAB], writes=[bkey], tag=f"qp{m}.{ts}")
                phi_to(bank, bkey,
                       [(qfull[:, m * tok + t0: m * tok + t0 + tss], 0)],
                       KQ, tss, f"qphi{m}.{ts}", full=True)
        p.dma("sync", [(o_qfull[:, :], qfull[:, :])], reads=[KQ], tag="stq")

        p.dma("gpsimd",
              [(wsb[:, c * D:(c + 1) * D], wk[c * 128:(c + 1) * 128, :])
               for c in range(NCH)], writes=[KW], tag="ldwk")
        p.dma("gpsimd",
              [(wvsb[:, c * D:(c + 1) * D], wv[c * 128:(c + 1) * 128, :])
               for c in range(NCH)], writes=[KWV], tag="ldwv")

        p.op("vector", lambda v: v.memset(s_run[:], 0.0), writes=[KSRUN],
             tag="s0")

        for c in range(tch):
            t0 = c * CH
            for m in range(NCH):
                bank, bkey = (pC, BC) if m % 2 == 0 else (pD, BD)
                for k in range(NCH):
                    lhs = wsb[:, k * D + m * 128: k * D + (m + 1) * 128]
                    rhs = xab[:, k * tok + t0: k * tok + t0 + CH]
                    p.op("tensor", lambda t, lhs=lhs, rhs=rhs, k=k, bank=bank:
                         t.matmul(bank[:, 0:CH], lhs, rhs, start=(k == 0),
                                  stop=(k == NCH - 1)),
                         reads=[KW, KXAB], writes=[bkey], tag=f"kp{c}.{m}")
                phi_to(bank, bkey, [(kfm[:, m * CH:(m + 1) * CH], 0)],
                       KKFM, CH, f"kphi{c}.{m}", full=True)
            for half in range(2):
                bank, bkey = (pE, BE) if half == 0 else (pF, BFb)
                for k in range(NCH):
                    lhs = xab[:, k * tok + t0: k * tok + t0 + CH]
                    rhs = wsb[:, k * D + half * 512: k * D + half * 512 + 512]
                    p.op("tensor", lambda t, lhs=lhs, rhs=rhs, k=k, bank=bank:
                         t.matmul(bank[:, 0:512], lhs, rhs, start=(k == 0),
                                  stop=(k == NCH - 1)),
                         reads=[KW, KXAB], writes=[bkey], tag=f"ktp{c}.{half}")
                p.op("vector", lambda v, bank=bank: v.tensor_scalar(
                    out=phit[:, 0:512], in0=bank[:, 0:512], scalar1=0.0,
                    scalar2=None, op0=ALU.min),
                    reads=[bkey], writes=[KPH1], tag=f"ktm{c}.{half}.min")
                p.op("scalar", lambda s: s.activation(
                    out=phie[:, 0:512], in_=phit[:, 0:512], func=AF.Exp),
                    reads=[KPH1], writes=[KPH2], tag=f"ktm{c}.{half}.exp")
                p.op("vector", lambda v, bank=bank, half=half:
                     v.scalar_tensor_tensor(
                         out=ktm[:, half * 512:(half + 1) * 512],
                         in0=bank[:, 0:512], scalar=0.0, in1=phie[:, 0:512],
                         op0=ALU.max, op1=ALU.add),
                     reads=[bkey, KPH2], writes=[KKTM], tag=f"ktm{c}.{half}.stt")
            for half in range(2):
                bank, bkey = (pE, BE) if half == 0 else (pF, BFb)
                for k in range(NCH):
                    lhs = xab[:, k * tok + t0: k * tok + t0 + CH]
                    rhs = wvsb[:, k * D + half * 512: k * D + half * 512 + 512]
                    p.op("tensor", lambda t, lhs=lhs, rhs=rhs, k=k, bank=bank:
                         t.matmul(bank[:, 0:512], lhs, rhs, start=(k == 0),
                                  stop=(k == NCH - 1)),
                         reads=[KWV, KXAB], writes=[bkey], tag=f"vp{c}.{half}")
                for hh in range(8):
                    h = half * 8 + hh
                    p.op("vector", lambda v, bank=bank, hh=hh, h=h:
                         v.tensor_copy(out=vtm[:, h * 65: h * 65 + 64],
                                       in_=bank[:, hh * 64:(hh + 1) * 64]),
                         reads=[bkey], writes=[KVTM], tag=f"vev{c}.{h}")
            for h in range(NHEAD):
                p.op("vector", lambda v, h=h: v.memset(
                    vtm[:, h * 65 + 64: h * 65 + 65], 1.0),
                    writes=[KVTM], tag=f"vone{c}.{h}")

            p.op("vector", lambda v: v.tensor_copy(out=s_pref[:], in_=s_run[:]),
                 reads=[KSRUN], writes=[KSPREF], tag=f"spref{c}")

            for h in range(NHEAD):
                hp, be = h // 2, 64 * (h % 2)
                sb_bank, sbk = (pC, BC) if h % 2 == 0 else (pD, BD)
                p.op("tensor", lambda t, hp=hp, be=be, sb_bank=sb_bank, t0=t0:
                     t.matmul(
                         sb_bank[:, 0:CH],
                         kfm[be:be + 64, hp * CH:(hp + 1) * CH],
                         qfull[be:be + 64, hp * tok + t0: hp * tok + t0 + CH],
                         start=True, stop=True),
                     reads=[KKFM, KQ], writes=[sbk], tag=f"sc{c}.{h}")
                p.op("vector", lambda v, h=h, sb_bank=sb_bank: v.tensor_tensor(
                    out=mst[:, h * CH:(h + 1) * CH], in0=sb_bank[:, 0:CH],
                    in1=mask[:], op=ALU.mult),
                    reads=[sbk, "mask"], writes=[KMST], tag=f"msk{c}.{h}")
                nb, nbk = (pE, BE) if h % 2 == 0 else (pF, BFb)
                ib, ibk = (pG, BG) if h % 2 == 0 else (pH, BH)
                p.op("tensor", lambda t, h=h, nb=nb: t.matmul(
                    nb[:, 0:65], mst[:, h * CH:(h + 1) * CH],
                    vtm[:, h * 65:(h + 1) * 65], start=True, stop=True),
                    reads=[KMST, KVTM], writes=[nbk], tag=f"numi{c}.{h}")
                p.op("tensor", lambda t, hp=hp, be=be, ib=ib, t0=t0: t.matmul(
                    ib[:, 0:65],
                    qfull[be:be + 64, hp * tok + t0: hp * tok + t0 + CH],
                    s_pref[be:be + 64, hp * 65:(hp + 1) * 65],
                    start=True, stop=True),
                    reads=[KQ, KSPREF], writes=[ibk], tag=f"numl{c}.{h}")
                p.op("vector", lambda v, h=h, nb=nb: v.tensor_copy(
                    out=nst[:, h * 65:(h + 1) * 65], in_=nb[:, 0:65]),
                    reads=[nbk], writes=[KNST], tag=f"numev{c}.{h}")
                p.op("vector", lambda v, h=h, ib=ib: v.tensor_tensor(
                    out=nst[:, h * 65:(h + 1) * 65],
                    in0=nst[:, h * 65:(h + 1) * 65],
                    in1=ib[:, 0:65], op=ALU.add),
                    reads=[ibk, KNST], writes=[KNST], tag=f"numad{c}.{h}")
                stb, stk = (pA, BA) if h % 2 == 0 else (pB, BB)
                p.op("tensor", lambda t, h=h, stb=stb: t.matmul(
                    stb[0:64, 0:65], ktm[:, h * DH:(h + 1) * DH],
                    vtm[:, h * 65:(h + 1) * 65], start=True, stop=True),
                    reads=[KKTM, KVTM], writes=[stk], tag=f"sup{c}.{h}")
                p.op("vector", lambda v, hp=hp, be=be, stb=stb: v.tensor_tensor(
                    out=s_run[be:be + 64, hp * 65:(hp + 1) * 65],
                    in0=s_run[be:be + 64, hp * 65:(hp + 1) * 65],
                    in1=stb[0:64, 0:65], op=ALU.add),
                    reads=[stk, KSRUN], writes=[KSRUN], tag=f"sacc{c}.{h}")
            p.dma("sync", [(o_numden[t0:t0 + CH, :], nst[:, :])],
                  reads=[KNST], tag=f"stnd{c}")
            if c == 0:
                p.dma("sync", [(o_dbg[:, :], nst[:, :]),
                               (o_dbg2[:, :], mst[:, :]),
                               (o_dbg3[:, :], vtm[:, :]),
                               (o_dbg4[:, :], kfm[:, :]),
                               (o_dbg5[:, :], ktm[:, :])],
                      reads=[KNST, KMST, KVTM, KKFM, KKTM], tag="stdbg")

        p.dma("sync", [(o_stot[:], s_run[:])], reads=[KSRUN], tag="ststot")

        p.emit()
    return nc




def build_launch2(tok):
    nc = bass.Bass("TRN2", target_bir_lowering=False)
    tch = tok // CH
    xin = nc.dram_tensor("xin", [D, tok], F32, kind="ExternalInput").ap()
    abrows = nc.dram_tensor("abrows", [4, tok], F32, kind="ExternalInput").ap()
    qin = nc.dram_tensor("qin", [128, 8 * tok], BF16, kind="ExternalInput").ap()
    ndin = nc.dram_tensor("ndin", [tok, NHEAD * 65], F32, kind="ExternalInput").ap()
    sall = nc.dram_tensor("sall", [128, NCORES * 8 * 65], F32,
                          kind="ExternalInput").ap()
    mcol = nc.dram_tensor("mcol", [128, NCORES], F32, kind="ExternalInput").ap()
    wo = nc.dram_tensor("wo", [D, D], F32, kind="ExternalInput").ap()
    wf1 = nc.dram_tensor("wf1", [D, 4 * D], F32, kind="ExternalInput").ap()
    wf2 = nc.dram_tensor("wf2", [4 * D, D], F32, kind="ExternalInput").ap()
    ln2g = nc.dram_tensor("ln2g", [128, 8], F32, kind="ExternalInput").ap()
    ln2b = nc.dram_tensor("ln2b", [128, 8], F32, kind="ExternalInput").ap()
    cbt = nc.dram_tensor("cbt", [D, D], F32, kind="ExternalInput").ap()
    scb = nc.dram_tensor("scb", [128, 8], F32, kind="ExternalInput").ap()
    fbias = nc.dram_tensor("fbias", [128, 8], F32, kind="ExternalInput").ap()
    ident = nc.dram_tensor("ident", [128, 128], F32, kind="ExternalInput").ap()
    consts = nc.dram_tensor("consts", [128, 136], F32, kind="ExternalInput").ap()
    o_x2 = nc.dram_tensor("x2", [D, tok], F32, kind="ExternalOutput").ap()

    ctx = Ctx(nc, tok)
    p = ctx.p
    tss, nts = ctx.tss, ctx.nts

    with ctx.es:
        consts_sb = ctx.sb("consts_sb", (128, 136), F32)
        ctx.csb = consts_sb
        ctx.ckey = "consts"
        ones_col = ctx.sb("ones_col", (128, 1), BF16)
        identb = ctx.sb("identb", (128, 128), BF16)
        xh = ctx.sb("xh", (128, NCH * tok), F32)
        ctx.prealloc_ln()
        gtmp = ctx.shared("lngt", (128, tss), F32)
        spre = ctx.sb("spre", (128, 8 * 65), BF16)
        mc = ctx.sb("mc", (128, NCORES), F32)
        g2 = ctx.sb("g2", (128, 8), F32)
        b2 = ctx.sb("b2", (128, 8), F32)
        t1b = ctx.sb("t1b", (128, tss), F32)
        xs = ctx.sb("xs", (128, tss), F32)

        pS1 = ctx.ps("pS1", (128, 512))
        pS2 = ctx.ps("pS2", (128, 512))
        pT1 = ctx.ps("pT1", (128, 1024), BF16)
        pT2 = ctx.ps("pT2", (128, 1024), BF16)
        pW1 = ctx.ps("pW1", (128, 512))
        pW2 = ctx.ps("pW2", (128, 512))

        KXH = "xh"
        B1, B2k, BT1, BT2, BW1, BW2 = "b_S1", "b_S2", "b_T1", "b_T2", "b_W1", "b_W2"

        p.dma("sync", [(consts_sb[:], consts[:])], writes=["consts"], tag="ldc")
        p.op("vector", lambda v: v.tensor_copy(
            out=ones_col[:], in_=consts_sb[:, 0:1]),
            reads=["consts"], writes=["ones_col"], tag="mkones")
        ones_row32 = consts_sb[0:1, 2:130]
        p.dma("gpsimd", [(identb[:], ident[:])], writes=["identb"], tag="ldid")
        p.dma("sync", [(xh[:, c * tok:(c + 1) * tok],
                        xin[c * 128:(c + 1) * 128, :]) for c in range(NCH)],
              writes=[KXH], tag="ldx")
        p.dma("sync", [(mc[:], mcol[:]), (g2[:], ln2g[:]), (b2[:], ln2b[:])],
              writes=["mc", "g2b2"], tag="ldr")

        with contextlib.ExitStack() as es2:
            ctx.alloc_es = es2
            sall_sb = ctx.sb("sall_sb", (128, NCORES * 8 * 65), F32)
            sacc = ctx.sb("sacc", (128, 8 * 65), F32)
            p.dma("sync", [(sall_sb[:], sall[:])], writes=["sall"], tag="ldsall")
            p.op("vector", lambda v: v.memset(sacc[:], 0.0), writes=["sacc"],
                 tag="sacc0")
            W = 8 * 65
            for r in range(NCORES):
                p.op("vector", lambda v, r=r: v.scalar_tensor_tensor(
                    out=sacc[:], in0=sall_sb[:, r * W:(r + 1) * W],
                    scalar=mc[:, r:r + 1], in1=sacc[:],
                    op0=ALU.mult, op1=ALU.add),
                    reads=["sall", "mc", "sacc"], writes=["sacc"], tag=f"sm{r}")
            p.op("vector", lambda v: v.tensor_copy(out=spre[:], in_=sacc[:]),
                 reads=["sacc"], writes=["spre"], tag="spreb")
        ctx.alloc_es = None
        p.barrier()

        with contextlib.ExitStack() as esB:
            ctx.alloc_es = esB
            rows = {"a": ctx.sb("rowa", (1, tok), F32),
                    "b": ctx.sb("rowb", (1, tok), F32)}
            p.dma("sync", [(rows["a"][:], abrows[0:1, :]),
                           (rows["b"][:], abrows[1:2, :])],
                  writes=["r.a", "r.b"], tag="ldab")
            emit_ln_apply(ctx, xh, KXH, rows, "r", ones_row32, pS1, pS2,
                          B1, B2k, [dict(tile=xh, key=KXH)], tag="l2ln0")
        ctx.alloc_es = None
        p.barrier()

        with contextlib.ExitStack() as esC:
            ctx.alloc_es = esC
            wosb = ctx.sb("wosb", (128, NCH * D), BF16)
            qch = ctx.sb("qch", (128, 8 * CH), BF16)
            ndch = ctx.sb("ndch", (128, NHEAD * 65), F32)
            nsum = ctx.sb("nsum", (128, 65), F32)
            dclip = ctx.sb("dclip", (128, 1), F32)
            rden = ctx.sb("rden", (128, 1), F32)
            attm = ctx.sb("attm", (128, NCH * CH), BF16)
            attf = ctx.sb("attf", (128, NCH * CH), BF16)
            p.dma("gpsimd",
                  [(wosb[:, c * D:(c + 1) * D], wo[c * 128:(c + 1) * 128, :])
                   for c in range(NCH)], writes=["wo"], tag="ldwo")
            for c in range(tch):
                t0 = c * CH
                p.dma("sync", [(qch[:], qin[:, :].rearrange(
                    "p (h t) -> p h t", h=8)[:, :, t0:t0 + CH])],
                      writes=["qch"], tag=f"ldq{c}")
                p.dma("sync", [(ndch[:], ndin[t0:t0 + CH, :])],
                      writes=["ndch"], tag=f"ldnd{c}")
                for h in range(NHEAD):
                    hp, be = h // 2, 64 * (h % 2)
                    bank, bk = (pS1, B1) if h % 2 == 0 else (pS2, B2k)
                    p.op("tensor", lambda t, hp=hp, be=be, bank=bank: t.matmul(
                        bank[:, 0:65], qch[be:be + 64, hp * CH:(hp + 1) * CH],
                        spre[be:be + 64, hp * 65:(hp + 1) * 65],
                        start=True, stop=True),
                        reads=["qch", "spre"], writes=[bk], tag=f"int{c}.{h}")
                    p.op("vector", lambda v, h=h, bank=bank: v.tensor_tensor(
                        out=nsum[:], in0=ndch[:, h * 65:(h + 1) * 65],
                        in1=bank[:, 0:65], op=ALU.add),
                        reads=["ndch", bk], writes=["nsum"], tag=f"nsum{c}.{h}")
                    p.op("vector", lambda v: v.tensor_scalar(
                        out=dclip[:], in0=nsum[:, 64:65], scalar1=EPS_DEN,
                        scalar2=None, op0=ALU.max),
                        reads=["nsum"], writes=["dclip"], tag=f"dcl{c}.{h}")
                    p.op("vector", lambda v: v.reciprocal(out=rden[:],
                                                          in_=dclip[:]),
                         reads=["dclip"], writes=["rden"], tag=f"rd{c}.{h}")
                    p.op("vector", lambda v, h=h: v.tensor_scalar(
                        out=attm[:, h * 64:(h + 1) * 64], in0=nsum[:, 0:64],
                        scalar1=rden[:, 0:1], scalar2=None, op0=ALU.mult),
                        reads=["nsum", "rden"], writes=["attm"],
                        tag=f"att{c}.{h}")
                for j in range(NCH):
                    bank, bk = (pT1, BT1) if j % 2 == 0 else (pT2, BT2)
                    p.op("tensor", lambda t, j=j, bank=bank: t.transpose(
                        bank[:, 0:CH], attm[:, j * CH:(j + 1) * CH],
                        identb[:]),
                        reads=["attm", "identb"], writes=[bk], tag=f"tr{c}.{j}")
                    p.op("vector", lambda v, j=j, bank=bank: v.tensor_copy(
                        out=attf[:, j * CH:(j + 1) * CH], in_=bank[:, 0:CH]),
                        reads=[bk], writes=["attf"], tag=f"trev{c}.{j}")
                for o in range(NCH):
                    bank, bk = (pW1, BW1) if o % 2 == 0 else (pW2, BW2)
                    for j in range(NCH):
                        p.op("tensor", lambda t, o=o, j=j, bank=bank: t.matmul(
                            bank[:, 0:CH],
                            wosb[:, j * D + o * 128: j * D + (o + 1) * 128],
                            attf[:, j * CH:(j + 1) * CH],
                            start=(j == 0), stop=(j == NCH - 1)),
                            reads=["wo", "attf"], writes=[bk],
                            tag=f"wo{c}.{o}.{j}")
                    sl = xh[:, o * tok + t0: o * tok + t0 + CH]
                    p.op("vector", lambda v, sl=sl, bank=bank: v.tensor_tensor(
                        out=sl, in0=sl, in1=bank[:, 0:CH], op=ALU.add),
                        reads=[bk, KXH], writes=[KXH], tag=f"xp{c}.{o}")
        ctx.alloc_es = None
        p.barrier()

        with contextlib.ExitStack() as esD:
            ctx.alloc_es = esD
            h2b = ctx.sb("h2b", (128, NCH * tok), BF16)
            with contextlib.ExitStack() as esD1:
                ctx.alloc_es = esD1
                rows1 = {"a": ctx.sb("r1a", (1, tok), F32),
                         "b": ctx.sb("r1b", (1, tok), F32)}
                emit_ln_stats(ctx, xh, KXH, rows1, "r1", pS1, pS2, B1, B2k,
                              ones_col, tag="ln2")
                emit_ln_apply(ctx, xh, KXH, rows1, "r1", ones_row32, pS1, pS2,
                              B1, B2k,
                              [dict(tile=h2b, key="h2b", gb=(g2, b2, "g2b2"),
                                    tmp=gtmp, tmpkey="lngt")], tag="ln2ap")
            ctx.alloc_es = esD
            p.barrier()
            wbigA = ctx.sb("wbigA", (128, NCH * 1024), BF16)
            wbigB = ctx.sb("wbigB", (128, 8 * 1024), BF16)
            gsb = ctx.sb("gsb", (128, 8 * tok), BF16)
            for q in range(4):
                p.dma("gpsimd",
                      [(wbigA[:, k * 1024:(k + 1) * 1024],
                        wf1[k * 128:(k + 1) * 128,
                            q * 1024:(q + 1) * 1024]) for k in range(NCH)],
                      writes=["wbigA"], tag=f"ldf1.{q}")
                for ts in range(nts):
                    t0 = ts * tss
                    for mm in range(8):
                        bank, bk = (pS1, B1) if mm % 2 == 0 else (pS2, B2k)
                        for k in range(NCH):
                            p.op("tensor",
                                 lambda t, mm=mm, k=k, bank=bank, t0=t0:
                                 t.matmul(bank[:, 0:tss],
                                          wbigA[:, k * 1024 + mm * 128:
                                                k * 1024 + (mm + 1) * 128],
                                          h2b[:, k * tok + t0:
                                              k * tok + t0 + tss],
                                          start=(k == 0), stop=(k == NCH - 1)),
                                 reads=["wbigA", "h2b"], writes=[bk],
                                 tag=f"f1.{q}.{ts}.{mm}")
                        p.op("scalar", lambda s, mm=mm, bank=bank, t0=t0:
                             s.activation(
                                 out=gsb[:, mm * tok + t0: mm * tok + t0 + tss],
                                 in_=bank[:, 0:tss], func=AF.Gelu),
                             reads=[bk], writes=["gsb"], tag=f"gel.{q}.{ts}.{mm}")
                p.dma("gpsimd",
                      [(wbigB[:, kk * 1024:(kk + 1) * 1024],
                        wf2[(q * 8 + kk) * 128:(q * 8 + kk + 1) * 128, :])
                       for kk in range(8)],
                      writes=["wbigB"], tag=f"ldf2.{q}")
                for ts in range(nts):
                    t0 = ts * tss
                    for o in range(NCH):
                        bank, bk = (pW1, BW1) if o % 2 == 0 else (pW2, BW2)
                        for kk in range(8):
                            p.op("tensor",
                                 lambda t, o=o, kk=kk, bank=bank, t0=t0:
                                 t.matmul(
                                     bank[:, 0:tss],
                                     wbigB[:, kk * 1024 + o * 128:
                                           kk * 1024 + (o + 1) * 128],
                                     gsb[:, kk * tok + t0: kk * tok + t0 + tss],
                                     start=(kk == 0), stop=(kk == 7)),
                                 reads=["wbigB", "gsb"], writes=[bk],
                                 tag=f"f2.{q}.{ts}.{o}")
                        sl = xh[:, o * tok + t0: o * tok + t0 + tss]
                        p.op("vector", lambda v, sl=sl, bank=bank:
                             v.tensor_tensor(
                                 out=sl, in0=sl, in1=bank[:, 0:tss],
                                 op=ALU.add),
                             reads=[bk, KXH], writes=[KXH],
                             tag=f"xf.{q}.{ts}.{o}")
        ctx.alloc_es = None
        p.barrier()

        with contextlib.ExitStack() as esF:
            ctx.alloc_es = esF
            rows2 = {"a": ctx.sb("r2a", (1, tok), F32),
                     "b": ctx.sb("r2b", (1, tok), F32)}
            emit_ln_stats(ctx, xh, KXH, rows2, "r2", pS1, pS2, B1, B2k,
                          ones_col, tag="pln0")
            for ts in range(nts):
                t0 = ts * tss
                p.op("tensor", lambda t, t0=t0: t.matmul(
                    pS1[:, 0:tss], ones_row32[0:1, :],
                    rows2["a"][:, t0:t0 + tss], start=True, stop=True),
                    reads=["r2.a", "consts"], writes=[B1], tag=f"pbA{ts}")
                p.op("tensor", lambda t, t0=t0: t.matmul(
                    pS2[:, 0:tss], ones_row32[0:1, :],
                    rows2["b"][:, t0:t0 + tss], start=True, stop=True),
                    reads=["r2.b", "consts"], writes=[B2k], tag=f"pbB{ts}")
                for c in range(NCH):
                    sl = xh[:, c * tok + t0: c * tok + t0 + tss]
                    p.dma("sync", [(xs[:, 0:tss],
                                    xin[c * 128:(c + 1) * 128, t0:t0 + tss])],
                          writes=["xs"], tag=f"ldxs{ts}.{c}")
                    p.op("vector", lambda v, sl=sl: v.tensor_tensor(
                        out=t1b[:, 0:tss], in0=sl, in1=pS2[:, 0:tss],
                        op=ALU.add),
                        reads=[KXH, B2k], writes=["t1b"], tag=f"pt1.{ts}.{c}")
                    p.op("vector", lambda v: v.tensor_tensor(
                        out=t1b[:, 0:tss], in0=t1b[:, 0:tss],
                        in1=pS1[:, 0:tss], op=ALU.mult),
                        reads=["t1b", B1], writes=["t1b"], tag=f"pt2.{ts}.{c}")
                    p.op("vector", lambda v, sl=sl: v.tensor_tensor(
                        out=sl, in0=t1b[:, 0:tss], in1=xs[:, 0:tss],
                        op=ALU.add),
                        reads=["t1b", "xs"], writes=[KXH], tag=f"pt3.{ts}.{c}")
        ctx.alloc_es = None
        p.barrier()

        zzb = ctx.sb("zzb", (128, NCH * tok), F32)
        with contextlib.ExitStack() as esG:
            ctx.alloc_es = esG
            xfb = ctx.sb("xfb", (128, NCH * tok), BF16)
            with contextlib.ExitStack() as esG1:
                ctx.alloc_es = esG1
                rows3 = {"a": ctx.sb("r3a", (1, tok), F32),
                         "b": ctx.sb("r3b", (1, tok), F32)}
                emit_ln_stats(ctx, xh, KXH, rows3, "r3", pS1, pS2, B1, B2k,
                              ones_col, tag="fln")
                emit_ln_apply(ctx, xh, KXH, rows3, "r3", ones_row32, pS1, pS2,
                              B1, B2k, [dict(tile=xfb, key="xfb")],
                              tag="flnap")
            ctx.alloc_es = esG
            p.barrier()
            cbt_sb = ctx.sb("cbt_sb", (128, NCH * D), BF16)
            scb_sb = ctx.sb("scb_sb", (128, 8), F32)
            fb_sb = ctx.sb("fb_sb", (128, 8), F32)
            p.dma("gpsimd",
                  [(cbt_sb[:, c * D:(c + 1) * D],
                    cbt[c * 128:(c + 1) * 128, :]) for c in range(NCH)],
                  writes=["cbt"], tag="ldcbt")
            p.dma("sync", [(scb_sb[:], scb[:]), (fb_sb[:], fbias[:])],
                  writes=["scbf"], tag="ldscb")
            p.op("vector", lambda v: v.tensor_scalar(
                out=scb_sb[:], in0=scb_sb[:], scalar1=1.0 / 127.0,
                scalar2=None, op0=ALU.mult),
                reads=["scbf"], writes=["scbf"], tag="scbdiv")
            for ts in range(nts):
                t0 = ts * tss
                for m in range(NCH):
                    bank, bk = (pW1, BW1) if m % 2 == 0 else (pW2, BW2)
                    for k in range(NCH):
                        p.op("tensor", lambda t, m=m, k=k, bank=bank, t0=t0:
                             t.matmul(bank[:, 0:tss],
                                      cbt_sb[:, k * D + m * 128:
                                             k * D + (m + 1) * 128],
                                      xfb[:, k * tok + t0: k * tok + t0 + tss],
                                      start=(k == 0), stop=(k == NCH - 1)),
                             reads=["cbt", "xfb"], writes=[bk],
                             tag=f"cb.{ts}.{m}")
                    p.op("vector", lambda v, m=m, bank=bank, t0=t0:
                         v.tensor_scalar(
                             out=zzb[:, m * tok + t0: m * tok + t0 + tss],
                             in0=bank[:, 0:tss], scalar1=scb_sb[:, m:m + 1],
                             scalar2=fb_sb[:, m:m + 1],
                             op0=ALU.mult, op1=ALU.add),
                         reads=[bk, "scbf"], writes=["zzb"],
                         tag=f"cbe.{ts}.{m}")
        ctx.alloc_es = None
        p.barrier()
        with contextlib.ExitStack() as esH:
            ctx.alloc_es = esH
            rows4 = {"a": ctx.sb("r4a", (1, tok), F32),
                     "b": ctx.sb("r4b", (1, tok), F32)}
            emit_ln_stats(ctx, zzb, "zzb", rows4, "r4", pS1, pS2, B1, B2k,
                          ones_col, tag="fln2")
            for ts in range(nts):
                t0 = ts * tss
                p.op("tensor", lambda t, t0=t0: t.matmul(
                    pS1[:, 0:tss], ones_row32[0:1, :],
                    rows4["a"][:, t0:t0 + tss], start=True, stop=True),
                    reads=["r4.a", "consts"], writes=[B1], tag=f"fbA{ts}")
                p.op("tensor", lambda t, t0=t0: t.matmul(
                    pS2[:, 0:tss], ones_row32[0:1, :],
                    rows4["b"][:, t0:t0 + tss], start=True, stop=True),
                    reads=["r4.b", "consts"], writes=[B2k], tag=f"fbB{ts}")
                for c in range(NCH):
                    zl = zzb[:, c * tok + t0: c * tok + t0 + tss]
                    sl = xh[:, c * tok + t0: c * tok + t0 + tss]
                    p.op("vector", lambda v, zl=zl: v.tensor_tensor(
                        out=t1b[:, 0:tss], in0=zl, in1=pS2[:, 0:tss],
                        op=ALU.add),
                        reads=["zzb", B2k], writes=["t1b"], tag=f"ft1.{ts}.{c}")
                    p.op("vector", lambda v: v.tensor_tensor(
                        out=t1b[:, 0:tss], in0=t1b[:, 0:tss],
                        in1=pS1[:, 0:tss], op=ALU.mult),
                        reads=["t1b", B1], writes=["t1b"], tag=f"ft2.{ts}.{c}")
                    p.op("vector", lambda v, sl=sl: v.tensor_tensor(
                        out=sl, in0=sl, in1=t1b[:, 0:tss], op=ALU.add),
                        reads=["t1b", KXH], writes=[KXH], tag=f"ft3.{ts}.{c}")
        ctx.alloc_es = None
        p.dma("sync", [(o_x2[c * 128:(c + 1) * 128, :],
                        xh[:, c * tok:(c + 1) * tok]) for c in range(NCH)],
              reads=[KXH], tag="stx2")

        p.emit()
    return nc



def build_launch3(tok):
    nc = bass.Bass("TRN2", target_bir_lowering=False)
    xin = nc.dram_tensor("xin", [D, tok], F32, kind="ExternalInput").ap()
    wbt = nc.dram_tensor("wbt", [D, 16], F32, kind="ExternalInput").ap()
    wct = nc.dram_tensor("wct", [D, 16], F32, kind="ExternalInput").ap()
    wdt = nc.dram_tensor("wdt", [D, D], F32, kind="ExternalInput").ap()
    alog = nc.dram_tensor("alog", [1, 16], F32, kind="ExternalInput").ap()
    dvec = nc.dram_tensor("dvec", [128, 8], F32, kind="ExternalInput").ap()
    sel16 = nc.dram_tensor("sel16", [16, 16 * 128], F32, kind="ExternalInput").ap()
    ident = nc.dram_tensor("ident", [128, 128], F32, kind="ExternalInput").ap()
    consts = nc.dram_tensor("consts", [128, 136], F32, kind="ExternalInput").ap()
    o_y0 = nc.dram_tensor("y0", [D, tok], BF16, kind="ExternalOutput").ap()
    o_d = nc.dram_tensor("dout", [D, tok], BF16, kind="ExternalOutput").ap()
    o_c = nc.dram_tensor("crow", [16, tok], BF16, kind="ExternalOutput").ap()
    o_acc = nc.dram_tensor("acc", [128, 128], F32, kind="ExternalOutput").ap()
    o_dec = nc.dram_tensor("dec", [128, 128], F32, kind="ExternalOutput").ap()

    ctx = Ctx(nc, tok)
    p = ctx.p
    HW = tok // 2
    hss = min(512, HW)
    hts = HW // hss

    with ctx.es:
        ctx.scoped_shared = True
        consts_sb = ctx.sb("consts_sb", (128, 136), F32)
        ctx.csb = consts_sb
        ctx.ckey = "consts"
        ones_col = ctx.sb("ones_col", (128, 1), BF16)
        identb = ctx.sb("identb", (128, 128), BF16)
        bsb = ctx.sb("bsb", (16, tok), BF16)
        csbB = ctx.sb("csbB", (16, tok), BF16)
        sel = ctx.sb("sel", (16, 16 * 128), BF16)
        asb = ctx.sb("asb", (128, 16), F32)
        dv = ctx.sb("dv", (128, 8), F32)
        acc_sb = ctx.sb("acc_sb", (128, 128), F32)
        dec_sb = ctx.sb("dec_sb", (128, 128), F32)
        dtp = ctx.sb("dtp", (128, 16), F32)
        etile = ctx.sb("etile", (128, 16), F32)
        sp2 = ctx.sb("sp2", (128, 512), F32)
        sp1b = ctx.sb("sp1b", (128, 512), BF16)
        zcol = ctx.sb("zcol", (128, 1), F32)
        arow_sb = ctx.sb("arow_sb", (1, 16), F32)
        wsmall = ctx.sb("wsmall", (128, NCH * 32), BF16)
        hmid = ctx.sb("hmid", (128, 128), F32)
        xhb = ctx.sb("xhb", (128, NCH * tok), BF16)

        pS1 = ctx.ps("pS1", (128, 512))
        pS2 = ctx.ps("pS2", (128, 512))
        pBC = ctx.ps("pBC", (128, 512))
        pB40 = ctx.ps("pB40", (128, 512))
        pY = [ctx.ps(f"pY{i}", (128, 512)) for i in range(4)]
        KY = [f"b_Y{i}" for i in range(4)]
        B1, B2k = "b_S1", "b_S2"
        bcastB = [pBC, pB40, pY[2], pY[3]]
        bcastK = ["b_BC", "b_B40", KY[2], KY[3]]

        p.dma("sync", [(consts_sb[:], consts[:]), (dv[:], dvec[:]),
                       (arow_sb[:], alog[:])],
              writes=["consts", "dv", "arow"], tag="ldc")
        p.dma("gpsimd", [(sel[:], sel16[:]), (identb[:], ident[:])],
              writes=["sel", "identb"], tag="ldsel")
        p.op("vector", lambda v: v.tensor_copy(out=ones_col[:],
                                               in_=consts_sb[:, 0:1]),
             reads=["consts"], writes=["ones_col"], tag="mkones")
        p.op("vector", lambda v: v.memset(zcol[:], 0.0), writes=["zcol"],
             tag="zc")
        ones_row32 = consts_sb[0:1, 2:130]

        with contextlib.ExitStack() as esA:
            ctx.alloc_es = esA
            xh = ctx.sb("xh", (128, NCH * tok), F32)
            KXH = "xh"
            p.dma("sync", [(xh[:, c * tok:(c + 1) * tok],
                            xin[c * 128:(c + 1) * 128, :]) for c in range(NCH)],
                  writes=[KXH], tag="ldx")
            rows = {"a": ctx.sb("ra", (1, tok), F32),
                    "b": ctx.sb("rb", (1, tok), F32)}
            emit_ln_stats(ctx, xh, KXH, rows, "r", pS1, pS2, B1, B2k, ones_col,
                          tag="l3ln")
            emit_ln_apply(ctx, xh, KXH, rows, "r", ones_row32, pS1, pS2,
                          B1, B2k, [dict(tile=xhb, key="xhb")], tag="l3lnap")
        ctx.alloc_es = None
        ctx._shared = {}
        p.barrier()

        p.op("tensor", lambda t: t.matmul(
            pS1[:, 0:16], ones_row32[0:1, :], arow_sb[:],
            start=True, stop=True),
            reads=["arow", "consts"], writes=[B1], tag="abc")
        p.op("scalar", lambda s: s.activation(
            out=asb[:], in_=pS1[:, 0:16], func=AF.Exp),
            reads=[B1], writes=["asb"], tag="aexp")
        p.op("vector", lambda v: v.tensor_scalar(
            out=asb[:], in0=asb[:], scalar1=-1.0, scalar2=None,
            op0=ALU.mult), reads=["asb"], writes=["asb"], tag="aneg")

        p.dma("gpsimd",
              [(wsmall[:, c * 32: c * 32 + 16],
                wbt[c * 128:(c + 1) * 128, :]) for c in range(NCH)] +
              [(wsmall[:, c * 32 + 16: c * 32 + 32],
                wct[c * 128:(c + 1) * 128, :]) for c in range(NCH)],
              writes=["wsmall"], tag="ldwbc")
        nts4 = tok // 512 if tok >= 512 else 1
        tss4 = min(512, tok)
        for which, dst, kdst in ((0, bsb, "bsb"), (1, csbB, "csbB")):
            for ts in range(nts4):
                t0 = ts * tss4
                for k in range(NCH):
                    p.op("tensor", lambda t, which=which, k=k, t0=t0:
                         t.matmul(pBC[0:16, 0:tss4],
                                  wsmall[:, k * 32 + which * 16:
                                         k * 32 + which * 16 + 16],
                                  xhb[:, k * tok + t0: k * tok + t0 + tss4],
                                  start=(k == 0), stop=(k == NCH - 1)),
                         reads=["wsmall", "xhb"], writes=["b_BC"],
                         tag=f"bc{which}.{ts}.{k}")
                p.op("vector", lambda v, dst=dst, t0=t0: v.tensor_copy(
                    out=dst[:, t0:t0 + tss4], in_=pBC[0:16, 0:tss4]),
                    reads=["b_BC"], writes=[kdst], tag=f"bce{which}.{ts}")
        p.dma("gpsimd", [(o_c[:], csbB[:])], reads=["csbB"], tag="stc")

        wdsbB = [ctx.sb(f"wdsb{i}", (128, NCH * 128), BF16) for i in range(2)]
        dall = ctx.sb("dall", (128, NCH * HW), F32)
        dxall = ctx.sb("dxall", (128, NCH * HW), F32)
        BbS = ctx.sb("BbS", (128, 16 * HW), BF16)
        CbS = ctx.sb("CbS", (128, 16 * HW), BF16)
        daB = [ctx.sb(f"da{i}", (128, HW), F32) for i in range(2)]
        uB = [ctx.sb(f"u{i}", (128, HW), BF16) for i in range(2)]
        hhB = [ctx.sb(f"hh{i}", (128, HW), F32) for i in range(2)]
        hbxB = [ctx.sb(f"hbx{i}", (128, HW), BF16) for i in range(2)]

        for H in range(2):
            h0 = H * HW

            def emit_dblock(hc, H=H, h0=h0):
                wdsb, kwd = wdsbB[hc % 2], f"wdsb{hc % 2}"
                KDA, KDX = f"dal{hc}", f"dxl{hc}"
                p.dma("gpsimd",
                      [(wdsb[:, k * 128:(k + 1) * 128],
                        wdt[k * 128:(k + 1) * 128,
                            hc * 128:(hc + 1) * 128]) for k in range(NCH)],
                      writes=[kwd], tag=f"ldwd{H}.{hc}")
                for sl in range(hts):
                    s0 = sl * hss
                    bank, bk = (pS1, B1) if sl % 2 == 0 else (pS2, B2k)
                    for k in range(NCH):
                        p.op("tensor", lambda t, k=k, s0=s0, bank=bank,
                             wdsb=wdsb, h0=h0: t.matmul(
                                 bank[:, 0:hss],
                                 wdsb[:, k * 128:(k + 1) * 128],
                                 xhb[:, k * tok + h0 + s0:
                                     k * tok + h0 + s0 + hss],
                                 start=(k == 0), stop=(k == NCH - 1)),
                             reads=[kwd, "xhb"], writes=[bk],
                             tag=f"dp{H}.{hc}.{sl}.{k}")
                    p.op("scalar", lambda s, bank=bank: s.activation(
                        out=sp2[:, 0:hss], in_=bank[:, 0:hss], func=AF.Abs),
                        reads=[bk], writes=["sp2"], tag=f"spa{H}.{hc}.{sl}")
                    p.op("scalar", lambda s: s.activation(
                        out=sp2[:, 0:hss], in_=sp2[:, 0:hss], func=AF.Exp,
                        scale=consts_sb[:, 133:134]),
                        reads=["sp2", "consts"], writes=["sp2"],
                        tag=f"spe{H}.{hc}.{sl}")
                    p.op("scalar", lambda s: s.activation(
                        out=sp2[:, 0:hss], in_=sp2[:, 0:hss], func=AF.Ln,
                        bias=consts_sb[:, 0:1]),
                        reads=["sp2", "consts"], writes=["sp2"],
                        tag=f"spl{H}.{hc}.{sl}")
                    p.op("vector", lambda v, bank=bank, hc=hc, s0=s0:
                         v.scalar_tensor_tensor(
                             out=dall[:, hc * HW + s0: hc * HW + s0 + hss],
                             in0=bank[:, 0:hss],
                             scalar=0.0, in1=sp2[:, 0:hss],
                             op0=ALU.max, op1=ALU.add),
                         reads=[bk, "sp2"], writes=[KDA],
                         tag=f"spf{H}.{hc}.{sl}")
                    p.op("gpsimd", lambda g, hc=hc, s0=s0, h0=h0:
                         g.tensor_tensor(
                             out=dxall[:, hc * HW + s0: hc * HW + s0 + hss],
                             in0=dall[:, hc * HW + s0: hc * HW + s0 + hss],
                             in1=xhb[:, hc * tok + h0 + s0:
                                     hc * tok + h0 + s0 + hss],
                             op=ALU.mult),
                         reads=[KDA, "xhb"], writes=[KDX],
                         tag=f"dx{H}.{hc}.{sl}")
                p.dma("gpsimd", [(o_d[hc * 128:(hc + 1) * 128, h0:h0 + HW],
                                  dall[:, hc * HW:(hc + 1) * HW])],
                      reads=[KDA], tag=f"std{H}.{hc}")
                if H == 0:
                    p.op("vector", lambda v, hc=hc: v.reduce_sum(
                        out=dtp[:, hc:hc + 1],
                        in_=dall[:, hc * HW:(hc + 1) * HW], axis=AX.X),
                        reads=[KDA], writes=["dtp"], tag=f"dt0.{hc}")
                else:
                    p.op("vector", lambda v, hc=hc: v.reduce_sum(
                        out=dtp[:, 8 + hc:9 + hc],
                        in_=dall[:, hc * HW:(hc + 1) * HW], axis=AX.X),
                        reads=[KDA], writes=["dtp"], tag=f"dt1.{hc}")
                    p.op("vector", lambda v, hc=hc: v.tensor_tensor(
                        out=dtp[:, 8 + hc:9 + hc], in0=dtp[:, hc:hc + 1],
                        in1=dtp[:, 8 + hc:9 + hc], op=ALU.add),
                        reads=["dtp"], writes=["dtp"], tag=f"dts.{hc}")
                    p.op("vector", lambda v, hc=hc: v.tensor_scalar(
                        out=etile[:], in0=asb[:],
                        scalar1=dtp[:, 8 + hc:9 + hc],
                        scalar2=None, op0=ALU.mult),
                        reads=["dtp", "asb"], writes=["etile"], tag=f"de{hc}")
                    p.op("scalar", lambda s, hc=hc: s.activation(
                        out=dec_sb[:, hc * 16:(hc + 1) * 16], in_=etile[:],
                        func=AF.Exp),
                        reads=["etile"], writes=["dec"], tag=f"dex{hc}")

            def emit_scanblock(hc, H=H, h0=h0):
                KDA, KDX = f"dal{hc}", f"dxl{hc}"
                for n in range(16):
                    ib = n % 2
                    da, uu, hh, hb = daB[ib], uB[ib], hhB[ib], hbxB[ib]
                    kda, ku, kh, khb = (f"da{ib}", f"u{ib}",
                                        f"hh{ib}", f"hbx{ib}")
                    p.op("scalar", lambda s, n=n, hc=hc, da=da: s.activation(
                        out=da[:, :], in_=dall[:, hc * HW:(hc + 1) * HW],
                        func=AF.Exp, scale=asb[:, n:n + 1]),
                        reads=[KDA, "asb"], writes=[kda],
                        tag=f"da{H}.{hc}.{n}")
                    p.op("vector", lambda v, n=n, hc=hc, uu=uu:
                         v.tensor_tensor(
                             out=uu[:, :],
                             in0=dxall[:, hc * HW:(hc + 1) * HW],
                             in1=BbS[:, n * HW:(n + 1) * HW], op=ALU.mult),
                         reads=[KDX, "BbS"], writes=[ku],
                         tag=f"u{H}.{hc}.{n}")
                    idx = hc * 16 + n
                    p.op("vector", lambda v, da=da, uu=uu, hh=hh, idx=idx,
                         H=H: v.tensor_tensor_scan(
                             out=hh[:, :], data0=da[:, :], data1=uu[:, :],
                             initial=(zcol[:, 0:1] if H == 0
                                      else hmid[:, idx:idx + 1]),
                             op0=ALU.mult, op1=ALU.add),
                         reads=[kda, ku, "zcol"] + ([] if H == 0
                                                    else ["hmid"]),
                         writes=[kh], tag=f"scan{H}.{hc}.{n}")
                    if H == 0:
                        p.op("gpsimd", lambda g, idx=idx, hh=hh:
                             g.tensor_copy(
                                 out=hmid[:, idx:idx + 1],
                                 in_=hh[:, HW - 1:HW]),
                             reads=[kh], writes=["hmid"], tag=f"hm{idx}")
                    else:
                        p.op("gpsimd", lambda g, idx=idx, hh=hh:
                             g.tensor_copy(
                                 out=acc_sb[:, idx:idx + 1],
                                 in_=hh[:, HW - 1:HW]),
                             reads=[kh], writes=["acc"], tag=f"ac{idx}")
                    p.op("vector", lambda v, n=n, hh=hh, hb=hb:
                         v.tensor_tensor(
                             out=hb[:, :], in0=hh[:, :],
                             in1=CbS[:, n * HW:(n + 1) * HW], op=ALU.mult),
                         reads=[kh, "CbS"], writes=[khb],
                         tag=f"ch{H}.{hc}.{n}")
                    if n > 0:
                        hbp, khbp = hbxB[(n - 1) % 2], f"hbx{(n - 1) % 2}"
                        for sl in range(hts):
                            s0 = sl * hss
                            p.op("tensor", lambda t, n=n, s0=s0, sl=sl,
                                 hb=hbp: t.matmul(
                                     pY[sl][:, 0:hss], identb[:],
                                     hb[:, s0:s0 + hss],
                                     start=(n == 1), stop=False),
                                 reads=[khbp, "identb"], writes=[KY[sl]],
                                 tag=f"y{H}.{hc}.{n - 1}.{sl}")
                for sl in range(hts):
                    s0 = sl * hss
                    p.op("tensor", lambda t, s0=s0, sl=sl, hc=hc: t.matmul(
                        pY[sl][:, 0:hss], identb[:],
                        hbxB[1][:, s0:s0 + hss],
                        start=False, stop=True),
                        reads=["hbx1", "identb"], writes=[KY[sl]],
                        tag=f"y{H}.{hc}.15.{sl}")
                for sl in range(hts):
                    s0 = sl * hss
                    p.op("vector", lambda v, hc=hc, s0=s0, sl=sl, h0=h0:
                         v.scalar_tensor_tensor(
                             out=sp1b[:, 0:hss],
                             in0=xhb[:, hc * tok + h0 + s0:
                                     hc * tok + h0 + s0 + hss],
                             scalar=dv[:, hc:hc + 1], in1=pY[sl][:, 0:hss],
                             op0=ALU.mult, op1=ALU.add),
                         reads=["xhb", "dv", KY[sl]], writes=["sp1b"],
                         tag=f"ye{H}.{hc}.{sl}")
                    p.dma("sync", [(o_y0[hc * 128:(hc + 1) * 128,
                                         h0 + s0:h0 + s0 + hss],
                                    sp1b[:, 0:hss])],
                          reads=["sp1b"], tag=f"sty{H}.{hc}.{sl}")

            def emit_bcasts(H=H, h0=h0, bistart=0):
                bi = bistart
                for n in range(16):
                    for which, src2, ksrc, dst, kdst in (
                            (0, bsb, "bsb", BbS, "BbS"),
                            (1, csbB, "csbB", CbS, "CbS")):
                        for sl in range(hts):
                            s0 = sl * hss
                            bank, bk = bcastB[bi % 4], bcastK[bi % 4]
                            bi += 1
                            p.op("tensor", lambda t, n=n, s0=s0, bank=bank,
                                 src2=src2, h0=h0: t.matmul(
                                     bank[:, 0:hss],
                                     sel[:, n * 128:(n + 1) * 128],
                                     src2[:, h0 + s0: h0 + s0 + hss],
                                     start=True, stop=True),
                                 reads=["sel", ksrc], writes=[bk],
                                 tag=f"bc{H}.{n}.{which}.{sl}")
                            p.op("scalar", lambda s, n=n, s0=s0, bank=bank,
                                 dst=dst: s.activation(
                                     out=dst[:, n * HW + s0:
                                             n * HW + s0 + hss],
                                     in_=bank[:, 0:hss], func=AF.Copy),
                                 reads=[bk], writes=[kdst],
                                 tag=f"be{H}.{n}.{which}.{sl}")

            emit_bcasts()
            emit_dblock(0)
            for hc in range(NCH):
                if hc + 1 < NCH:
                    emit_dblock(hc + 1)
                emit_scanblock(hc)

        p.dma("sync", [(o_acc[:], acc_sb[:]), (o_dec[:], dec_sb[:])],
              reads=["acc", "dec"], tag="stad")

        p.emit()
    return nc



def build_launch4(tok):
    nc = bass.Bass("TRN2", target_bir_lowering=False)
    tch = tok // CH
    xin = nc.dram_tensor("xin", [D, tok], F32, kind="ExternalInput").ap()
    y0in = nc.dram_tensor("y0in", [D, tok], BF16, kind="ExternalInput").ap()
    din = nc.dram_tensor("din", [D, tok], BF16, kind="ExternalInput").ap()
    cin = nc.dram_tensor("cin", [16, tok], BF16, kind="ExternalInput").ap()
    accall = nc.dram_tensor("accall", [128, NCORES * 128], F32,
                            kind="ExternalInput").ap()
    decall = nc.dram_tensor("decall", [128, NCORES * 128], F32,
                            kind="ExternalInput").ap()
    mcol8 = nc.dram_tensor("mcol8", [128, NCORES], F32,
                           kind="ExternalInput").ap()
    alog = nc.dram_tensor("alog", [1, 16], F32, kind="ExternalInput").ap()
    sel16 = nc.dram_tensor("sel16", [16, 16 * 128], F32,
                           kind="ExternalInput").ap()
    woutT = nc.dram_tensor("woutT", [D, D], F32, kind="ExternalInput").ap()
    wgate = nc.dram_tensor("wgate", [D, 6], F32, kind="ExternalInput").ap()
    atmat = nc.dram_tensor("atmat", [96, 100], F32, kind="ExternalInput").ap()
    bmat = nc.dram_tensor("bmat", [96, D], F32, kind="ExternalInput").ap()
    wtwT = nc.dram_tensor("wtwT", [D, D], F32, kind="ExternalInput").ap()
    wtbrow = nc.dram_tensor("wtbrow", [1, D], F32, kind="ExternalInput").ap()
    wdwT = nc.dram_tensor("wdwT", [D, D], F32, kind="ExternalInput").ap()
    wdb = nc.dram_tensor("wdb", [128, 8], F32, kind="ExternalInput").ap()
    scalein = nc.dram_tensor("scalein", [1, 1], F32, kind="ExternalInput").ap()
    ident = nc.dram_tensor("ident", [128, 128], F32, kind="ExternalInput").ap()
    consts = nc.dram_tensor("consts", [128, 136], F32, kind="ExternalInput").ap()
    o_x = nc.dram_tensor("xout", [D, tok], F32, kind="ExternalOutput").ap()

    ctx = Ctx(nc, tok)
    p = ctx.p
    tss, nts = ctx.tss, ctx.nts

    with ctx.es:
        consts_sb = ctx.sb("consts_sb", (128, 136), F32)
        ctx.csb = consts_sb
        ctx.ckey = "consts"
        ones_col = ctx.sb("ones_col", (128, 1), BF16)
        identb = ctx.sb("identb", (128, 128), BF16)
        xh = ctx.sb("xh", (128, NCH * tok), F32)
        ctx.prealloc_ln()
        ctx.shared("rowA", (1, tok), F32)
        ctx.shared("rowB", (1, tok), F32)
        pS1 = ctx.ps("pS1", (128, 512))
        pS2 = ctx.ps("pS2", (128, 512))
        B1, B2k = "b_S1", "b_S2"
        KXH = "xh"

        p.dma("sync", [(consts_sb[:], consts[:])], writes=["consts"], tag="ldc")
        p.op("vector", lambda v: v.tensor_copy(out=ones_col[:],
                                               in_=consts_sb[:, 0:1]),
             reads=["consts"], writes=["ones_col"], tag="mkones")
        ones_row32 = consts_sb[0:1, 2:130]
        p.dma("gpsimd", [(identb[:], ident[:])], writes=["identb"], tag="ldid")
        p.dma("sync", [(xh[:, c * tok:(c + 1) * tok],
                        xin[c * 128:(c + 1) * 128, :]) for c in range(NCH)],
              writes=[KXH], tag="ldx")

        with contextlib.ExitStack() as esA:
            ctx.alloc_es = esA
            def sbA(name, shape, dtype):
                return esA.enter_context(nc.sbuf_tensor(name, list(shape), dtype))
            asb = sbA("asb", (128, 16), F32)
            arow_sb = sbA("arow_sb", (1, 16), F32)
            sel = sbA("sel", (16, 16 * 128), BF16)
            hin = sbA("hin", (128, 128), F32)
            csb2 = sbA("csb2", (16, tok), BF16)
            ybig = sbA("ybig", (128, NCH * tok), BF16)
            wosb = sbA("wosb", (128, NCH * D), BF16)
            pY = [ctx.ps(f"pY{i}", (128, 512)) for i in range(max(4, nts))]
            KY = [f"b_Y{i}" for i in range(max(4, nts))]

            p.dma("sync", [(arow_sb[:], alog[:])],
                  writes=["arow"], tag="lds")
            p.dma("gpsimd", [(sel[:], sel16[:]), (csb2[:], cin[:])],
                  writes=["sel", "csb2"], tag="ldsc2")
            p.op("tensor", lambda t: t.matmul(
                pS1[:, 0:16], ones_row32[0:1, :], arow_sb[:],
                start=True, stop=True),
                reads=["arow", "consts"], writes=[B1], tag="abc")
            p.op("scalar", lambda s: s.activation(
                out=asb[:], in_=pS1[:, 0:16], func=AF.Exp),
                reads=[B1], writes=["asb"], tag="aexp")
            p.op("vector", lambda v: v.tensor_scalar(
                out=asb[:], in0=asb[:], scalar1=-1.0, scalar2=None,
                op0=ALU.mult), reads=["asb"], writes=["asb"], tag="aneg")
            with contextlib.ExitStack() as esC0:
                ctx.alloc_es = esC0
                accs = ctx.sb("accs", (128, 128), F32)
                decs = ctx.sb("decs", (128, 128), F32)
                mc = ctx.sb("mc", (128, NCORES), F32)
                m1 = ctx.sb("m1", (128, NCORES), F32)
                dtl = ctx.sb("dtl", (128, 128), F32)
                tmpa = ctx.sb("tmpa", (128, 128), F32)
                p.dma("sync", [(mc[:], mcol8[:])], writes=["mc"], tag="ldsc")
                p.op("vector", lambda v: v.tensor_scalar(
                    out=m1[:], in0=mc[:], scalar1=-1.0, scalar2=1.0,
                    op0=ALU.mult, op1=ALU.add),
                    reads=["mc"], writes=["m1"], tag="m1")
                p.op("vector", lambda v: v.memset(hin[:], 0.0), writes=["hin"],
                     tag="h0")
                for r in range(NCORES):
                    p.dma("sync", [(accs[:], accall[:, r * 128:(r + 1) * 128]),
                                   (decs[:], decall[:, r * 128:(r + 1) * 128])],
                          writes=["accs", "decs"], tag=f"ldad{r}")
                    p.op("vector", lambda v, r=r: v.tensor_scalar(
                        out=dtl[:], in0=decs[:, :],
                        scalar1=mc[:, r:r + 1], scalar2=m1[:, r:r + 1],
                        op0=ALU.mult, op1=ALU.add),
                        reads=["decs", "mc", "m1"], writes=["dtl"],
                        tag=f"dt{r}")
                    p.op("vector", lambda v, r=r: v.tensor_scalar(
                        out=tmpa[:], in0=accs[:, :],
                        scalar1=mc[:, r:r + 1], scalar2=None, op0=ALU.mult),
                        reads=["accs", "mc"], writes=["tmpa"], tag=f"ta{r}")
                    p.op("vector", lambda v: v.tensor_tensor(
                        out=hin[:], in0=hin[:], in1=dtl[:], op=ALU.mult),
                        reads=["hin", "dtl"], writes=["hin"], tag=f"hm{r}")
                    p.op("vector", lambda v: v.tensor_tensor(
                        out=hin[:], in0=hin[:], in1=tmpa[:], op=ALU.add),
                        reads=["hin", "tmpa"], writes=["hin"], tag=f"ha{r}")
            ctx.alloc_es = esA
            p.barrier()
            p.dma("gpsimd",
                  [(wosb[:, c * D:(c + 1) * D],
                    woutT[c * 128:(c + 1) * 128, :]) for c in range(NCH)],
                  writes=["wo"], tag="ldwo")
            esA1 = contextlib.ExitStack()
            ctx.alloc_es = esA1
            dhc = ctx.sb("dhc", (128, tok), BF16)
            cumdB = [ctx.sb(f"cumd{i}", (128, tok), F32) for i in range(2)]
            mhcB = [ctx.sb(f"mhc{i}", (128, tok), F32) for i in range(2)]
            hbxB = [ctx.sb(f"hbx{i}", (128, tok), BF16) for i in range(2)]
            zt5 = ctx.sb("zt5", (128, tss), F32)
            p.op("vector", lambda v: v.memset(zt5[:], 0.0), writes=["zt5"],
                 tag="z4")
            p.dma("scalar",
                  [(ybig[:, c * tok:(c + 1) * tok],
                    y0in[c * 128:(c + 1) * 128, :]) for c in range(NCH)],
                  writes=["ybig"], tag="ldy0")
            pB40 = ctx.ps("pB40", (128, 512))
            cbB = [pB40, pS1, pS2]
            cbK = ["b_B40", B1, B2k]
            for hc in range(NCH):
                hp = hc % 2
                cumd = cumdB[hp]
                KDH, KCU = "dhc", f"cumd{hp}"
                p.dma("sync", [(dhc[:], din[hc * 128:(hc + 1) * 128, :])],
                      writes=[KDH], tag=f"ldd{hc}")
                for sg in range(nts):
                    s0 = sg * tss
                    rd = [KDH, "zt5"] + ([KCU] if sg > 0 else [])
                    p.op("vector", lambda v, s0=s0, sg=sg, cumd=cumd:
                         v.tensor_tensor_scan(
                             out=cumd[:, s0:s0 + tss],
                             data0=dhc[:, s0:s0 + tss], data1=zt5[:, :],
                             initial=(zt5[:, 0:1] if sg == 0
                                      else cumd[:, s0 - 1:s0]),
                             op0=ALU.add, op1=ALU.add),
                         reads=rd, writes=[KCU], tag=f"cs{hc}.{sg}")
                for n in range(16):
                    ib = n % 2
                    mh, hb = mhcB[ib], hbxB[ib]
                    kmh, khb = f"mhc{ib}", f"hbx{ib}"
                    p.op("scalar", lambda s, n=n, mh=mh, cumd=cumd:
                         s.activation(
                             out=mh[:, :], in_=cumd[:, :], func=AF.Exp,
                             scale=asb[:, n:n + 1]),
                         reads=[KCU, "asb"], writes=[kmh],
                         tag=f"M{hc}.{n}")
                    def emit_cb(ts, n=n):
                        t0 = ts * tss
                        bi = (n * nts + ts) % 3
                        bank, bk = cbB[bi], cbK[bi]
                        p.op("tensor", lambda t, n=n, t0=t0, bank=bank:
                             t.matmul(bank[:, 0:tss],
                                      sel[:, n * 128:(n + 1) * 128],
                                      csb2[:, t0:t0 + tss],
                                      start=True, stop=True),
                             reads=["sel", "csb2"], writes=[bk],
                             tag=f"cb{hc}.{n}.{ts}")

                    def emit_chx(ts, n=n, mh=mh, hb=hb, kmh=kmh, khb=khb):
                        t0 = ts * tss
                        bi = (n * nts + ts) % 3
                        bank, bk = cbB[bi], cbK[bi]
                        p.op("vector", lambda v, hc=hc, n=n, t0=t0, bank=bank,
                             mh=mh, hb=hb: v.scalar_tensor_tensor(
                                 out=hb[:, t0:t0 + tss],
                                 in0=mh[:, t0:t0 + tss],
                                 scalar=hin[:, hc * 16 + n: hc * 16 + n + 1],
                                 in1=bank[:, 0:tss],
                                 op0=ALU.mult, op1=ALU.mult),
                             reads=[kmh, "hin", bk], writes=[khb],
                             tag=f"chx{hc}.{n}.{ts}")

                    for ts in range(nts):
                        emit_cb(ts)
                        if ts >= 2:
                            emit_chx(ts - 2)
                    for ts in range(max(0, nts - 2), nts):
                        emit_chx(ts)
                    if n > 0:
                        hbp, khbp = hbxB[(n - 1) % 2], f"hbx{(n - 1) % 2}"
                        for ts in range(nts):
                            t0 = ts * tss
                            p.op("tensor", lambda t, n=n, t0=t0, ts=ts,
                                 hb=hbp: t.matmul(
                                     pY[ts][:, 0:tss], identb[:],
                                     hb[:, t0:t0 + tss],
                                     start=(n == 1), stop=False),
                                 reads=[khbp, "identb"], writes=[KY[ts]],
                                 tag=f"y{hc}.{n - 1}.{ts}")
                for ts in range(nts):
                    t0 = ts * tss
                    p.op("tensor", lambda t, t0=t0, ts=ts: t.matmul(
                        pY[ts][:, 0:tss], identb[:],
                        hbxB[1][:, t0:t0 + tss],
                        start=False, stop=True),
                        reads=["hbx1", "identb"], writes=[KY[ts]],
                        tag=f"y{hc}.15.{ts}")
                for ts in range(nts):
                    t0 = ts * tss
                    p.op("vector", lambda v, hc=hc, t0=t0, ts=ts:
                         v.tensor_tensor(
                             out=ybig[:, hc * tok + t0: hc * tok + t0 + tss],
                             in0=ybig[:, hc * tok + t0: hc * tok + t0 + tss],
                             in1=pY[ts][:, 0:tss],
                             op=ALU.add),
                         reads=["ybig", KY[ts]], writes=["ybig"],
                         tag=f"yb{hc}.{ts}")
            esA1.close()
            ctx.alloc_es = esA
            p.barrier()
            zts = sbA("zts", (128, NCH * 512), F32)
            zb16 = ctx.shared("lnxb", (128, tss), BF16)
            zsq = ctx.shared("lnsq", (128, tss), BF16)
            s1r = ctx.shared("lns1", (1, tok), F32)
            s2r = ctx.shared("lns2", (1, tok), F32)
            arow5 = ctx.shared("rowA", (1, tok), F32)
            brow5 = ctx.shared("rowB", (1, tok), F32)
            t1c = sbA("t1c", (128, tss), F32)
            for ts in range(nts):
                t0 = ts * tss
                for o in range(NCH):
                    bank, bk = (pS1, B1) if o % 2 == 0 else (pS2, B2k)
                    for k in range(NCH):
                        p.op("tensor", lambda t, o=o, k=k, t0=t0, bank=bank:
                             t.matmul(bank[:, 0:tss],
                                      wosb[:, k * D + o * 128:
                                           k * D + (o + 1) * 128],
                                      ybig[:, k * tok + t0: k * tok + t0 + tss],
                                      start=(k == 0), stop=(k == NCH - 1)),
                             reads=["wo", "ybig"], writes=[bk],
                             tag=f"wo{o}.{ts}.{k}")
                    p.op("vector", lambda v, o=o, bank=bank: v.tensor_copy(
                        out=zts[:, o * tss:(o + 1) * tss], in_=bank[:, 0:tss]),
                        reads=[bk], writes=["zts"], tag=f"woe{o}.{ts}")
                for o in range(NCH):
                    zl = zts[:, o * tss:(o + 1) * tss]
                    p.op("vector", lambda v, zl=zl: v.tensor_copy(
                        out=zb16[:, 0:tss], in_=zl),
                        reads=["zts"], writes=["lnxb"], tag=f"zc{ts}.{o}")
                    p.op("scalar", lambda s, zl=zl: s.activation(
                        out=zsq[:, 0:tss], in_=zl, func=AF.Square),
                        reads=["zts"], writes=["lnsq"], tag=f"zs{ts}.{o}")
                    p.op("tensor", lambda t, o=o, t0=t0: t.matmul(
                        pS1[0:1, 0:tss], ones_col[:, 0:1], zb16[:, 0:tss],
                        start=(o == 0), stop=(o == NCH - 1)),
                        reads=["lnxb", "ones_col"], writes=[B1],
                        tag=f"zm1{ts}.{o}")
                    p.op("tensor", lambda t, o=o, t0=t0: t.matmul(
                        pS2[0:1, 0:tss], ones_col[:, 0:1], zsq[:, 0:tss],
                        start=(o == 0), stop=(o == NCH - 1)),
                        reads=["lnsq", "ones_col"], writes=[B2k],
                        tag=f"zm2{ts}.{o}")
                p.op("vector", lambda v, t0=t0: v.tensor_copy(
                    out=s1r[:, t0:t0 + tss], in_=pS1[0:1, 0:tss]),
                    reads=[B1], writes=["lns1"], tag=f"ze1{ts}")
                p.op("vector", lambda v, t0=t0: v.tensor_copy(
                    out=s2r[:, t0:t0 + tss], in_=pS2[0:1, 0:tss]),
                    reads=[B2k], writes=["lns2"], tag=f"ze2{ts}")
                p.op("vector", lambda v, t0=t0: v.tensor_scalar(
                    out=brow5[:, t0:t0 + tss], in0=s1r[:, t0:t0 + tss],
                    scalar1=-1.0 / D, scalar2=None, op0=ALU.mult),
                    reads=["lns1"], writes=["rr.b"], tag=f"zb{ts}")
                p.op("vector", lambda v, t0=t0: v.tensor_scalar(
                    out=arow5[:, t0:t0 + tss], in0=s2r[:, t0:t0 + tss],
                    scalar1=1.0 / D, scalar2=None, op0=ALU.mult),
                    reads=["lns2"], writes=["rr.a"], tag=f"za{ts}")
                p.op("vector", lambda v, t0=t0: v.tensor_tensor(
                    out=s1r[:, t0:t0 + tss], in0=brow5[:, t0:t0 + tss],
                    in1=brow5[:, t0:t0 + tss], op=ALU.mult),
                    reads=["rr.b"], writes=["lns1"], tag=f"zq{ts}")
                p.op("vector", lambda v, t0=t0: v.tensor_tensor(
                    out=arow5[:, t0:t0 + tss], in0=arow5[:, t0:t0 + tss],
                    in1=s1r[:, t0:t0 + tss], op=ALU.subtract),
                    reads=["rr.a", "lns1"], writes=["rr.a"], tag=f"zv{ts}")
                p.op("scalar", lambda s, t0=t0: s.activation(
                    out=arow5[:, t0:t0 + tss], in_=arow5[:, t0:t0 + tss],
                    func=AF.Ln, bias=consts_sb[0:1, 130:131]),
                    reads=["rr.a", "consts"], writes=["rr.a"], tag=f"zl{ts}")
                p.op("scalar", lambda s, t0=t0: s.activation(
                    out=arow5[:, t0:t0 + tss], in_=arow5[:, t0:t0 + tss],
                    func=AF.Exp, scale=consts_sb[0:1, 131:132]),
                    reads=["rr.a", "consts"], writes=["rr.a"], tag=f"zr{ts}")
                p.op("tensor", lambda t, t0=t0: t.matmul(
                    pS1[:, 0:tss], ones_row32[0:1, :],
                    arow5[:, t0:t0 + tss], start=True, stop=True),
                    reads=["rr.a", "consts"], writes=[B1], tag=f"5bA{ts}")
                p.op("tensor", lambda t, t0=t0: t.matmul(
                    pS2[:, 0:tss], ones_row32[0:1, :],
                    brow5[:, t0:t0 + tss], start=True, stop=True),
                    reads=["rr.b", "consts"], writes=[B2k], tag=f"5bB{ts}")
                for cc in range(NCH):
                    zl = zts[:, cc * tss:(cc + 1) * tss]
                    sl = xh[:, cc * tok + t0: cc * tok + t0 + tss]
                    p.op("vector", lambda v, zl=zl: v.tensor_tensor(
                        out=t1c[:], in0=zl, in1=pS2[:, 0:tss], op=ALU.add),
                        reads=["zts", B2k], writes=["t1c"],
                        tag=f"5t1.{ts}.{cc}")
                    p.op("vector", lambda v: v.tensor_tensor(
                        out=t1c[:], in0=t1c[:], in1=pS1[:, 0:tss],
                        op=ALU.mult),
                        reads=["t1c", B1], writes=["t1c"], tag=f"5t2.{ts}.{cc}")
                    p.op("vector", lambda v, sl=sl: v.tensor_tensor(
                        out=sl, in0=sl, in1=t1c[:], op=ALU.add),
                        reads=["t1c", KXH], writes=[KXH], tag=f"5t3.{ts}.{cc}")
        ctx.alloc_es = None
        p.barrier()

        ffb = ctx.sb("ffb", (128, NCH * tok), BF16)
        rows6 = {"a": ctx.shared("rowA", (1, tok), F32),
                 "b": ctx.shared("rowB", (1, tok), F32)}
        emit_ln_stats(ctx, xh, KXH, rows6, "rr", pS1, pS2, B1, B2k, ones_col,
                      tag="l4f")
        emit_ln_apply(ctx, xh, KXH, rows6, "rr", ones_row32, pS1, pS2, B1, B2k,
                      [dict(tile=ffb, key="ffb")], tag="l4fap")
        arow6 = ctx.sb("arow6", (1, tok), BF16)
        brow6 = ctx.sb("brow6", (1, tok), BF16)
        onesb = ctx.sb("onesb", (1, 128), BF16)
        p.op("gpsimd", lambda g: g.tensor_copy(out=onesb[:],
                                               in_=consts_sb[0:1, 2:130]),
             reads=["consts"], writes=["onesb"], tag="mkonesb")
        p.op("vector", lambda v: v.tensor_copy(out=arow6[:], in_=rows6["a"][:]),
             reads=["rr.a"], writes=["r6.a"], tag="cp6a")
        p.op("vector", lambda v: v.tensor_copy(out=brow6[:], in_=rows6["b"][:]),
             reads=["rr.b"], writes=["r6.b"], tag="cp6b")

        tksb = ctx.sb("tksb", (128, NCH * 600), BF16)
        valsb = ctx.sb("valsb", (100, 6 * D), BF16)
        wgsb = ctx.sb("wgsb", (128, NCH * 6), BF16)
        wbsb = ctx.sb("wbsb", (128, NCH * D), BF16)
        wdbs = ctx.sb("wdbs", (128, 8), F32)
        scol = ctx.sb("scol", (128, 1), F32)
        scsb = ctx.sb("scsb", (1, 1), F32)
        p.dma("gpsimd",
              [(wgsb[:, c * 6:(c + 1) * 6], wgate[c * 128:(c + 1) * 128, :])
               for c in range(NCH)] +
              [(wbsb[:, c * D:(c + 1) * D], wtwT[c * 128:(c + 1) * 128, :])
               for c in range(NCH)],
              writes=["wgsb", "wbsb"], tag="ldffn")
        p.dma("sync", [(wdbs[:], wdb[:]),
                       (scsb[:], scalein[:])],
              writes=["wdbs", "scsb"], tag="ldffb")
        p.op("tensor", lambda t: t.matmul(
            pS1[:, 0:1], ones_row32[0:1, :], scsb[:], start=True, stop=True),
            reads=["scsb", "consts"], writes=[B1], tag="scb")
        p.op("vector", lambda v: v.tensor_copy(out=scol[:], in_=pS1[:, 0:1]),
             reads=[B1], writes=["scol"], tag="sce")
        with contextlib.ExitStack() as esT:
            ctx.alloc_es = esT
            atsb = ctx.sb("atsb", (16, 6 * 100), BF16)
            bmsb = ctx.sb("bmsb", (16, 6 * D), BF16)
            p.dma("gpsimd",
                  [(atsb[:, e * 100:(e + 1) * 100],
                    atmat[e * 16:(e + 1) * 16, :]) for e in range(6)] +
                  [(bmsb[:, e * D:(e + 1) * D],
                    bmat[e * 16:(e + 1) * 16, :]) for e in range(6)],
                  writes=["atsb", "bmsb"], tag="ldab2")
            for e in range(6):
                for cc in range(NCH):
                    bank, bk = ((pS1, B1) if (e * NCH + cc) % 2 == 0
                                else (pS2, B2k))
                    p.op("tensor", lambda t, e=e, cc=cc, bank=bank: t.matmul(
                        bank[:, 0:100], bmsb[:, e * D + cc * 128:
                                             e * D + (cc + 1) * 128],
                        atsb[:, e * 100:(e + 1) * 100], start=True, stop=True),
                        reads=["bmsb", "atsb"], writes=[bk], tag=f"tk{e}.{cc}")
                    p.op("vector", lambda v, e=e, cc=cc, bank=bank:
                         v.tensor_copy(
                             out=tksb[:, cc * 600 + e * 100:
                                      cc * 600 + (e + 1) * 100],
                             in_=bank[:, 0:100]),
                         reads=[bk], writes=["tksb"], tag=f"tke{e}.{cc}")
        ctx.alloc_es = None
        p.barrier()
        esV = contextlib.ExitStack()
        ctx.alloc_es = esV
        vtmp = ctx.sb("vtmp", (100, 512), F32)
        wtbs = ctx.sb("wtbs", (1, D), F32)
        p.dma("sync", [(wtbs[:], wtbrow[:])], writes=["wtbs"], tag="ldwtb")
        for e in range(6):
            for half in range(2):
                bank, bk = (pS1, B1) if half == 0 else (pS2, B2k)
                for cc in range(NCH):
                    p.op("tensor", lambda t, e=e, cc=cc, half=half, bank=bank:
                         t.matmul(bank[0:100, 0:512],
                                  tksb[:, cc * 600 + e * 100:
                                       cc * 600 + (e + 1) * 100],
                                  wbsb[:, cc * D + half * 512:
                                       cc * D + half * 512 + 512],
                                  start=(cc == 0), stop=(cc == NCH - 1)),
                         reads=["tksb", "wbsb"], writes=[bk],
                         tag=f"vl{e}.{half}.{cc}")
                p.op("vector", lambda v, half=half, bank=bank: v.tensor_copy(
                    out=vtmp[:, 0:512], in_=bank[0:100, 0:512]),
                    reads=[bk], writes=["vtmp"], tag=f"vle{e}.{half}")
                bank2, bk2 = (pS2, B2k) if half == 0 else (pS1, B1)
                p.op("tensor", lambda t, half=half, bank2=bank2: t.matmul(
                    bank2[:, 0:512], ones_row32[0:1, :],
                    wtbs[0:1, half * 512: half * 512 + 512],
                    start=True, stop=True),
                    reads=["wtbs", "consts"], writes=[bk2], tag=f"vb{e}.{half}")
                p.op("vector", lambda v, e=e, half=half, bank2=bank2:
                     v.tensor_tensor(
                         out=valsb[:, e * D + half * 512:
                                   e * D + half * 512 + 512],
                         in0=vtmp[:, 0:512], in1=bank2[0:100, 0:512],
                         op=ALU.add),
                     reads=["vtmp", bk2], writes=["valsb"], tag=f"vf{e}.{half}")
            p.op("vector", lambda v, e=e: v.memset(
                valsb[0:1, e * D:(e + 1) * D], 0.0),
                writes=["valsb"], tag=f"vz{e}")
        esV.close()
        ctx.alloc_es = None
        p.barrier()
        p.dma("gpsimd",
              [(wbsb[:, c * D:(c + 1) * D], wdwT[c * 128:(c + 1) * 128, :])
               for c in range(NCH)], writes=["wbsb"], tag="ldwdw")

        PARQ = []
        for par in range(2):
            PARQ.append(dict(
                fch=ctx.sb(f"fch{par}", (128, NCH * CH), F32),
                lgt=ctx.sb(f"lgt{par}", (128, 6), F32),
                mxg=ctx.sb(f"mxg{par}", (128, 1), F32),
                g01=ctx.sb(f"g01{par}", (128, 6), F32),
                scx=ctx.sb(f"scx{par}", (128, 600), F32),
                mxe=ctx.sb(f"mxe{par}", (128, 6), F32),
                sms=ctx.sb(f"sms{par}", (128, 6), F32),
                smr=ctx.sb(f"smr{par}", (128, 6), F32),
                gnc=ctx.sb(f"gnc{par}", (128, 6), F32),
                asel=ctx.sb(f"asel{par}", (128, 600), BF16),
                aselT=ctx.sb(f"aselT{par}", (100, 6 * CH), BF16),
                dfx=ctx.sb(f"dfx{par}", (128, NCH * CH), BF16),
                cmb=[ctx.sb(f"cmb{par}_{j}", (128, CH), F32) for j in range(3)],
            ))
        lnxbS = ctx.shared("lnxb", (128, tss), BF16)
        lnsqS = ctx.shared("lnsq", (128, tss), BF16)
        lnt1S = ctx.shared("lnt1", (128, tss), F32)
        lngtS = ctx.shared("lngt", (128, tss), F32)
        lns1S = ctx.shared("lns1", (1, tok), F32)
        for par in range(2):
            PARQ[par]["zb"] = (lnxbS[:, par * CH:(par + 1) * CH]
                               if tss >= 2 * CH
                               else ctx.sb(f"zbl{par}", (128, CH), BF16))
            PARQ[par]["zq"] = (lnsqS[:, par * CH:(par + 1) * CH]
                               if tss >= 2 * CH
                               else ctx.sb(f"zsl{par}", (128, CH), BF16))
            PARQ[par]["t1"] = [
                (lnt1S[:, (2 * par + j) * CH:(2 * par + j + 1) * CH]
                 if tss >= (2 * par + j + 1) * CH
                 else ctx.sb(f"t1l{par}_{j}", (128, CH), F32))
                for j in range(2)]
            PARQ[par]["msk"] = [
                (lngtS[:, (2 * par + j) * CH:(2 * par + j + 1) * CH]
                 if tss >= (2 * par + j + 1) * CH
                 else ctx.sb(f"msk{par}_{j}", (128, CH), F32))
                for j in range(2)]
            PARQ[par]["lr"] = (lns1S[0:1, par * 3 * CH:(par + 1) * 3 * CH]
                               if tok >= 6 * CH
                               else ctx.sb(f"lrw{par}", (1, 3 * CH), F32))
        bksP = [[(pS1, B1), (pS2, B2k), (pY[2], KY[2])],
                [(pY[0], KY[0]), (pY[1], KY[1]), (pY[3], KY[3])]]
        pT1 = ctx.ps("pT1", (128, 1024), BF16)
        for c in range(tch):
            t0 = c * CH
            par = c % 2
            Q = PARQ[par]
            (bA, kA), (bB, kB), (bC, kC) = bksP[par]
            fch, scx, asel, aselT, dfx = (Q["fch"], Q["scx"], Q["asel"],
                                          Q["aselT"], Q["dfx"])
            lgt, mxg, g01, mxe, sms, smr, gnc = (Q["lgt"], Q["mxg"], Q["g01"],
                                                 Q["mxe"], Q["sms"], Q["smr"],
                                                 Q["gnc"])
            KF, KXC = f"fch{par}", f"xh.c{c}"
            KSC = [f"scx{par}.{e}" for e in range(6)]
            KAS = [f"asel{par}.{e}" for e in range(6)]
            p.op("tensor", lambda t, t0=t0, bA=bA: t.matmul(
                bA[:, 0:CH], onesb[0:1, :], arow6[:, t0:t0 + CH],
                start=True, stop=True),
                reads=["r6.a", "onesb"], writes=[kA], tag=f"6bA{c}")
            p.op("tensor", lambda t, t0=t0, bB=bB: t.matmul(
                bB[:, 0:CH], onesb[0:1, :], brow6[:, t0:t0 + CH],
                start=True, stop=True),
                reads=["r6.b", "onesb"], writes=[kB], tag=f"6bB{c}")
            for cc in range(NCH):
                sl = xh[:, cc * tok + t0: cc * tok + t0 + CH]
                p.op("vector", lambda v, sl=sl, cc=cc, fch=fch, bB=bB:
                     v.tensor_tensor(
                         out=fch[:, cc * CH:(cc + 1) * CH], in0=sl,
                         in1=bB[:, 0:CH], op=ALU.add),
                     reads=[KXH, KXC, kB], writes=[KF], tag=f"fc1{c}.{cc}")
                p.op("vector", lambda v, cc=cc, fch=fch, bA=bA:
                     v.tensor_tensor(
                         out=fch[:, cc * CH:(cc + 1) * CH],
                         in0=fch[:, cc * CH:(cc + 1) * CH],
                         in1=bA[:, 0:CH], op=ALU.mult),
                     reads=[KF, kA], writes=[KF], tag=f"fc2{c}.{cc}")
            for k in range(NCH):
                p.op("tensor", lambda t, k=k, t0=t0, bC=bC: t.matmul(
                    bC[:, 0:6], ffb[:, k * tok + t0: k * tok + t0 + CH],
                    wgsb[:, k * 6:(k + 1) * 6],
                    start=(k == 0), stop=(k == NCH - 1)),
                    reads=["ffb", "wgsb"], writes=[kC], tag=f"lg{c}.{k}")
            p.op("vector", lambda v, lgt=lgt, bC=bC: v.tensor_copy(
                out=lgt[:], in_=bC[:, 0:6]),
                reads=[kC], writes=[f"lgt{par}"], tag=f"lge{c}")
            p.op("vector", lambda v, lgt=lgt, mxg=mxg: v.reduce_max(
                out=mxg[:], in_=lgt[:], axis=AX.X),
                reads=[f"lgt{par}"], writes=[f"mxg{par}"], tag=f"mx{c}")
            p.op("vector", lambda v, lgt=lgt, mxg=mxg, g01=g01: v.tensor_scalar(
                out=g01[:], in0=lgt[:], scalar1=mxg[:, 0:1], scalar2=None,
                op0=ALU.is_equal),
                reads=[f"lgt{par}", f"mxg{par}"], writes=[f"g01{par}"],
                tag=f"oh{c}")
            for half, (lo, wdt_) in enumerate(((0, 512), (512, 88))):
                bank, bk = (bA, kA) if half == 0 else (bB, kB)
                for k in range(NCH):
                    p.op("tensor", lambda t, k=k, t0=t0, lo=lo, wdt_=wdt_,
                         bank=bank: t.matmul(
                             bank[:, 0:wdt_],
                             ffb[:, k * tok + t0: k * tok + t0 + CH],
                             tksb[:, k * 600 + lo: k * 600 + lo + wdt_],
                             start=(k == 0), stop=(k == NCH - 1)),
                         reads=["ffb", "tksb"], writes=[bk],
                         tag=f"sc{c}.{half}.{k}")
                p.op("vector", lambda v, lo=lo, wdt_=wdt_, bank=bank, scx=scx:
                     v.tensor_copy(out=scx[:, lo:lo + wdt_],
                                   in_=bank[:, 0:wdt_]),
                     reads=[bk], writes=KSC, tag=f"sce{c}.{half}")
            for e in range(6):
                p.op("vector", lambda v, e=e, scx=scx, mxe=mxe: v.reduce_max(
                    out=mxe[:, e:e + 1], in_=scx[:, e * 100:(e + 1) * 100],
                    axis=AX.X),
                    reads=[KSC[e]], writes=[f"mxe{par}"], tag=f"sm{c}.{e}")
                p.op("gpsimd", lambda g, e=e, scx=scx, mxe=mxe: g.tensor_scalar(
                    out=scx[:, e * 100:(e + 1) * 100],
                    in0=scx[:, e * 100:(e + 1) * 100],
                    scalar1=mxe[:, e:e + 1], scalar2=None, op0=ALU.subtract),
                    reads=[KSC[e], f"mxe{par}"], writes=[KSC[e]],
                    tag=f"ss{c}.{e}")
            for e in range(6):
                p.op("scalar", lambda s, e=e, scx=scx, sms=sms: s.activation(
                    out=scx[:, e * 100:(e + 1) * 100],
                    in_=scx[:, e * 100:(e + 1) * 100], func=AF.Exp,
                    scale=consts_sb[:, 135:136], accum_out=sms[:, e:e + 1]),
                    reads=[KSC[e], "consts"], writes=[KSC[e], f"sms{par}"],
                    tag=f"se{c}.{e}")
            for e in range(6):
                p.op("vector", lambda v, e=e, sms=sms, smr=smr: v.reciprocal(
                    out=smr[:, e:e + 1], in_=sms[:, e:e + 1]),
                    reads=[f"sms{par}"], writes=[f"smr{par}"], tag=f"sr{c}.{e}")
            p.op("gpsimd", lambda g, smr=smr, g01=g01, gnc=gnc:
                 g.tensor_tensor(
                     out=gnc[:, 0:6], in0=smr[:, 0:6],
                     in1=g01[:, 0:6], op=ALU.mult),
                 reads=[f"smr{par}", f"g01{par}"], writes=[f"gnc{par}"],
                 tag=f"gn{c}")
            for e in range(6):
                p.op("gpsimd", lambda g, e=e, scx=scx, asel=asel, gnc=gnc:
                     g.tensor_scalar(
                         out=asel[:, e * 100:(e + 1) * 100],
                         in0=scx[:, e * 100:(e + 1) * 100],
                         scalar1=gnc[:, e:e + 1], scalar2=None, op0=ALU.mult),
                     reads=[KSC[e], f"gnc{par}"], writes=[KAS[e]],
                     tag=f"as{c}.{e}")
            for e in range(6):
                p.op("tensor", lambda t, e=e, asel=asel: t.transpose(
                    pT1[0:100, e * CH:(e + 1) * CH],
                    asel[:, e * 100:(e + 1) * 100], identb[:]),
                    reads=[KAS[e], "identb"], writes=[f"b_T1.{e}"],
                    tag=f"tr{c}.{e}")
            for e in range(6):
                p.op("vector", lambda v, e=e, aselT=aselT: v.tensor_copy(
                    out=aselT[:, e * CH:(e + 1) * CH],
                    in_=pT1[0:100, e * CH:(e + 1) * CH]),
                    reads=[f"b_T1.{e}"], writes=[f"aselT{par}"],
                    tag=f"tre{c}.{e}")
            for dc in range(NCH):
                bank, bk = (bA, kA) if dc % 2 == 0 else (bB, kB)
                for e in range(6):
                    p.op("tensor", lambda t, dc=dc, e=e, bank=bank,
                         aselT=aselT: t.matmul(
                             bank[:, 0:CH],
                             valsb[:, e * D + dc * 128: e * D + (dc + 1) * 128],
                             aselT[:, e * CH:(e + 1) * CH],
                             start=(e == 0), stop=(e == 5)),
                         reads=["valsb", f"aselT{par}"], writes=[bk],
                         tag=f"df{c}.{dc}.{e}")
                p.op("vector", lambda v, dc=dc, bank=bank, fch=fch, dfx=dfx:
                     v.tensor_tensor(
                         out=dfx[:, dc * CH:(dc + 1) * CH],
                         in0=fch[:, dc * CH:(dc + 1) * CH],
                         in1=bank[:, 0:CH], op=ALU.add),
                     reads=[KF, bk], writes=[f"dfx{par}"], tag=f"dfe{c}.{dc}")
            def emit_mzgz(cp):
                cmb = Q["cmb"][cp % 3]
                msk = Q["msk"][cp % 2]
                kcm, kms = f"cmb{par}_{cp % 3}", f"msk{par}_{cp % 2}"
                p.op("vector", lambda v, cmb=cmb, msk=msk: v.tensor_scalar(
                    out=msk[:, 0:CH], in0=cmb[:], scalar1=0.0, scalar2=None,
                    op0=ALU.is_equal),
                    reads=[kcm], writes=[kms], tag=f"mz{c}.{cp}")
                p.op("vector", lambda v, cmb=cmb, msk=msk:
                     v.scalar_tensor_tensor(
                         out=cmb[:], in0=msk[:, 0:CH], scalar=consts_sb[:, 134:135],
                         in1=cmb[:], op0=ALU.mult, op1=ALU.add),
                     reads=[kms, kcm, "consts"], writes=[kcm],
                     tag=f"gz{c}.{cp}")

            def emit_lnbo(cp):
                cmb = Q["cmb"][cp % 3]
                kcm = f"cmb{par}_{cp % 3}"
                p.op("scalar", lambda s, cmb=cmb: s.activation(
                    out=cmb[:], in_=cmb[:], func=AF.Ln),
                    reads=[kcm], writes=[kcm], tag=f"ln{c}.{cp}")
                p.op("vector", lambda v, cp=cp, cmb=cmb, fch=fch:
                     v.tensor_tensor(
                         out=fch[:, cp * CH:(cp + 1) * CH],
                         in0=fch[:, cp * CH:(cp + 1) * CH], in1=cmb[:],
                         op=ALU.add),
                     reads=[kcm, KF], writes=[KF], tag=f"bo{c}.{cp}")

            for cp in range(NCH):
                bank, bk = (bA, kA) if cp % 2 == 0 else (bB, kB)
                cmb = Q["cmb"][cp % 3]
                kcm = f"cmb{par}_{cp % 3}"
                for dc in range(NCH):
                    p.op("tensor", lambda t, cp=cp, dc=dc, bank=bank, dfx=dfx:
                         t.matmul(
                             bank[:, 0:CH],
                             wbsb[:, dc * D + cp * 128: dc * D + (cp + 1) * 128],
                             dfx[:, dc * CH:(dc + 1) * CH],
                             start=(dc == 0), stop=(dc == NCH - 1)),
                         reads=["wbsb", f"dfx{par}"], writes=[bk],
                         tag=f"dl{c}.{cp}.{dc}")
                p.op("vector", lambda v, cp=cp, bank=bank, cmb=cmb:
                     v.tensor_scalar(
                         out=cmb[:], in0=bank[:, 0:CH],
                         scalar1=wdbs[:, cp:cp + 1], scalar2=scol[:, 0:1],
                         op0=ALU.add, op1=ALU.mult),
                     reads=[bk, "wdbs", "scol"], writes=[kcm],
                     tag=f"dle{c}.{cp}")
                p.op("scalar", lambda s, cmb=cmb: s.activation(
                    out=cmb[:], in_=cmb[:], func=AF.Exp),
                    reads=[kcm], writes=[kcm], tag=f"ex{c}.{cp}")
                if cp >= 1:
                    emit_mzgz(cp - 1)
                if cp >= 2:
                    emit_lnbo(cp - 2)
            emit_mzgz(NCH - 1)
            emit_lnbo(NCH - 2)
            emit_lnbo(NCH - 1)
            zbP, zqP = Q["zb"], Q["zq"]
            for cc in range(NCH):
                fl = fch[:, cc * CH:(cc + 1) * CH]
                p.op("vector", lambda v, fl=fl, zbP=zbP: v.tensor_copy(
                    out=zbP[:, 0:CH], in_=fl),
                    reads=[KF], writes=[f"zbl{par}"], tag=f"pc{c}.{cc}")
                p.op("scalar", lambda s, fl=fl, zqP=zqP: s.activation(
                    out=zqP[:, 0:CH], in_=fl, func=AF.Square),
                    reads=[KF], writes=[f"zsl{par}"], tag=f"ps{c}.{cc}")
                p.op("tensor", lambda t, cc=cc, zbP=zbP, bC=bC: t.matmul(
                    bC[0:1, 0:CH], ones_col[:, 0:1], zbP[:, 0:CH],
                    start=(cc == 0), stop=(cc == NCH - 1)),
                    reads=[f"zbl{par}", "ones_col"], writes=[kC],
                    tag=f"pm1{c}.{cc}")
                p.op("tensor", lambda t, cc=cc, zqP=zqP, bA=bA: t.matmul(
                    bA[0:1, 0:CH], ones_col[:, 0:1], zqP[:, 0:CH],
                    start=(cc == 0), stop=(cc == NCH - 1)),
                    reads=[f"zsl{par}", "ones_col"], writes=[kA],
                    tag=f"pm2{c}.{cc}")
            lr = Q["lr"]
            klr = f"lrw{par}"
            ar = lr[0:1, 0:CH]
            br = lr[0:1, CH:2 * CH]
            sq = lr[0:1, 2 * CH:3 * CH]
            p.op("vector", lambda v, br=br, bC=bC: v.tensor_scalar(
                out=br, in0=bC[0:1, 0:CH],
                scalar1=-1.0 / D, scalar2=None, op0=ALU.mult),
                reads=[kC], writes=[klr], tag=f"pb{c}")
            p.op("vector", lambda v, ar=ar, bA=bA: v.tensor_scalar(
                out=ar, in0=bA[0:1, 0:CH],
                scalar1=1.0 / D, scalar2=None, op0=ALU.mult),
                reads=[kA], writes=[klr], tag=f"pa{c}")
            p.op("vector", lambda v, sq=sq, br=br: v.tensor_tensor(
                out=sq, in0=br, in1=br, op=ALU.mult),
                reads=[klr], writes=[klr], tag=f"pq{c}")
            p.op("vector", lambda v, ar=ar, sq=sq: v.tensor_tensor(
                out=ar, in0=ar, in1=sq, op=ALU.subtract),
                reads=[klr], writes=[klr], tag=f"pv{c}")
            p.op("scalar", lambda s, ar=ar: s.activation(
                out=ar, in_=ar, func=AF.Ln, bias=consts_sb[0:1, 130:131]),
                reads=[klr, "consts"], writes=[klr], tag=f"pl{c}")
            p.op("scalar", lambda s, ar=ar: s.activation(
                out=ar, in_=ar, func=AF.Exp, scale=consts_sb[0:1, 131:132]),
                reads=[klr, "consts"], writes=[klr], tag=f"pr{c}")
            p.op("tensor", lambda t, ar=ar, bA=bA: t.matmul(
                bA[:, 0:CH], ones_row32[0:1, :], ar,
                start=True, stop=True),
                reads=[klr, "consts"], writes=[kA], tag=f"pA{c}")
            p.op("tensor", lambda t, br=br, bB=bB: t.matmul(
                bB[:, 0:CH], ones_row32[0:1, :], br,
                start=True, stop=True),
                reads=[klr, "consts"], writes=[kB], tag=f"pB{c}")
            for cc in range(NCH):
                fl = fch[:, cc * CH:(cc + 1) * CH]
                sl = xh[:, cc * tok + t0: cc * tok + t0 + CH]
                t1 = Q["t1"][cc % 2]
                kt1 = f"t1l{par}_{cc % 2}"
                p.op("vector", lambda v, fl=fl, t1=t1, bB=bB: v.tensor_tensor(
                    out=t1[:, 0:CH], in0=fl, in1=bB[:, 0:CH], op=ALU.add),
                    reads=[KF, kB], writes=[kt1], tag=f"q1{c}.{cc}")
                p.op("vector", lambda v, t1=t1, bA=bA: v.tensor_tensor(
                    out=t1[:, 0:CH], in0=t1[:, 0:CH], in1=bA[:, 0:CH],
                    op=ALU.mult),
                    reads=[kt1, kA], writes=[kt1], tag=f"q2{c}.{cc}")
                p.op("vector", lambda v, sl=sl, t1=t1: v.tensor_tensor(
                    out=sl, in0=sl, in1=t1[:, 0:CH], op=ALU.add),
                    reads=[kt1, KXC], writes=[KXC], tag=f"q3{c}.{cc}")

        p.dma("sync", [(o_x[c * 128:(c + 1) * 128, :],
                        xh[:, c * tok:(c + 1) * tok]) for c in range(NCH)],
              reads=[KXH] + [f"xh.c{c}" for c in range(tch)], tag="stx")

        p.emit()
    return nc



def make_consts():
    c = np.zeros((128, 136), np.float32)
    c[:, 0] = 1.0
    s = np.arange(128)[:, None]
    t = np.arange(128)[None, :]
    c[:, 2:130] = (s <= t).astype(np.float32)
    c[:, 130] = EPS_LN
    c[:, 131] = -0.5
    c[:, 132] = EPS_DEN
    c[:, 133] = -1.0
    c[:, 134] = float(np.finfo(np.float32).eps)
    c[:, 135] = 1.0 / 32.0
    return c


def col_tile(vec):
    return np.ascontiguousarray(vec.reshape(8, 128).T).astype(np.float32)


def make_sel16():
    s = np.zeros((16, 16 * 128), np.float32)
    for n in range(16):
        s[n, n * 128:(n + 1) * 128] = 1.0
    return s


_NC_CACHE = {}


def _get_nc(name, builder, tok):
    key = (name, tok)
    if key not in _NC_CACHE:
        _NC_CACHE[key] = builder(tok)
    return _NC_CACHE[key]


def kernel(**inputs):
    TOKPC = 2048
    x = np.asarray(inputs["L0_attn_x"], np.float32)
    xT = np.ascontiguousarray(np.transpose(x, (0, 2, 1)))
    consts = make_consts()
    sel16 = make_sel16()
    identm = np.eye(128, dtype=np.float32)

    def shard(i):
        b, s0 = i // 4, (i % 4) * TOKPC
        return np.ascontiguousarray(xT[b][:, s0:s0 + TOKPC])

    nc1 = _get_nc("l1", build_launch1, TOKPC)
    wq = np.ascontiguousarray(inputs["L0_attn_Wq"], np.float32)
    wk = np.ascontiguousarray(inputs["L0_attn_Wk"], np.float32)
    wv = np.ascontiguousarray(inputs["L0_attn_Wv"], np.float32)
    in1 = [{"xin": shard(i), "wq": wq, "wk": wk, "wv": wv,
            "ln1g": col_tile(np.asarray(inputs["L0_attn_ln1_g"], np.float32)),
            "ln1b": col_tile(np.asarray(inputs["L0_attn_ln1_b"], np.float32)),
            "consts": consts} for i in range(NCORES)]
    r1 = run_bass_kernel_spmd(nc1, in1, core_ids=list(range(NCORES))).results

    nc2 = _get_nc("l2", build_launch2, TOKPC)
    sall = np.ascontiguousarray(
        np.concatenate([r1[i]["stot"] for i in range(NCORES)], axis=1))
    def mk_mcol(i, rows):
        m = np.zeros((rows, NCORES), np.float32)
        g0 = (i // 4) * 4
        m[:, g0:i] = 1.0
        return m
    wo = np.ascontiguousarray(inputs["L0_attn_Wo"], np.float32)
    wf1 = np.ascontiguousarray(inputs["L0_attn_Wf1"], np.float32)
    wf2 = np.ascontiguousarray(inputs["L0_attn_Wf2"], np.float32)
    cbt = np.ascontiguousarray(np.asarray(inputs["L0_ffn_CB"], np.float32).T)
    in2 = [{"xin": shard(i), "abrows": r1[i]["abrows"], "qin": r1[i]["qout"],
            "ndin": r1[i]["numden"], "sall": sall, "mcol": mk_mcol(i, 128),
            "wo": wo, "wf1": wf1, "wf2": wf2,
            "ln2g": col_tile(np.asarray(inputs["L0_attn_ln2_g"], np.float32)),
            "ln2b": col_tile(np.asarray(inputs["L0_attn_ln2_b"], np.float32)),
            "cbt": cbt,
            "scb": col_tile(np.asarray(inputs["L0_ffn_SCB"], np.float32)),
            "fbias": col_tile(np.asarray(inputs["L0_ffn_bias"], np.float32)),
            "ident": identm, "consts": consts} for i in range(NCORES)]
    r2 = run_bass_kernel_spmd(nc2, in2, core_ids=list(range(NCORES))).results

    nc3 = _get_nc("l3", build_launch3, TOKPC)
    wxm = np.asarray(inputs["L1_attn_Wx"], np.float32)
    wbt = np.ascontiguousarray(wxm[:16].T)
    wct = np.ascontiguousarray(wxm[16:32].T)
    wdt = np.ascontiguousarray(wxm[32:].T)
    alog = np.ascontiguousarray(
        np.asarray(inputs["L1_attn_A_log"], np.float32)[None, :])
    dvec = col_tile(np.asarray(inputs["L1_attn_D"], np.float32))
    in3 = [{"xin": r2[i]["x2"], "wbt": wbt, "wct": wct, "wdt": wdt,
            "alog": alog, "dvec": dvec, "sel16": sel16, "ident": identm,
            "consts": consts} for i in range(NCORES)]
    r3 = run_bass_kernel_spmd(nc3, in3, core_ids=list(range(NCORES))).results

    nc4 = _get_nc("l4", build_launch4, TOKPC)
    accall = np.ascontiguousarray(
        np.concatenate([r3[i]["acc"] for i in range(NCORES)], axis=1))
    decall = np.ascontiguousarray(
        np.concatenate([r3[i]["dec"] for i in range(NCORES)], axis=1))
    woutT = np.ascontiguousarray(np.asarray(inputs["L1_attn_Wout"],
                                            np.float32).T)
    Am = np.asarray(inputs["L1_ffn_A"], np.float32)
    Bm = np.asarray(inputs["L1_ffn_B"], np.float32)
    atmat = np.ascontiguousarray(Am.transpose(0, 2, 1)).reshape(96, 100)
    bmat = np.ascontiguousarray(Bm.reshape(96, 1024))
    wgate = np.ascontiguousarray(np.asarray(inputs["L1_ffn_w_gate"],
                                            np.float32))
    wtwT = np.ascontiguousarray(np.asarray(inputs["L1_ffn_wt_w"],
                                           np.float32).T)
    wtbrow = np.ascontiguousarray(
        np.asarray(inputs["L1_ffn_wt_b"], np.float32)[None, :])
    wdwT = np.ascontiguousarray(np.asarray(inputs["L1_ffn_wd_w"],
                                           np.float32).T)
    wdb = col_tile(np.asarray(inputs["L1_ffn_wd_b"], np.float32))
    scalein = np.ascontiguousarray(
        np.asarray(inputs["L1_ffn_scale"], np.float32).reshape(1, 1))
    in4 = [{"xin": r2[i]["x2"], "y0in": r3[i]["y0"], "din": r3[i]["dout"],
            "cin": r3[i]["crow"], "accall": accall, "decall": decall,
            "mcol8": mk_mcol(i, 128), "alog": alog, "sel16": sel16,
            "woutT": woutT, "wgate": wgate, "atmat": atmat, "bmat": bmat,
            "wtwT": wtwT, "wtbrow": wtbrow, "wdwT": wdwT, "wdb": wdb,
            "scalein": scalein, "ident": identm, "consts": consts}
           for i in range(NCORES)]
    r4 = run_bass_kernel_spmd(nc4, in4, core_ids=list(range(NCORES))).results

    outT = np.zeros((2, D, 8192), np.float32)
    for i in range(NCORES):
        b, s0 = i // 4, (i % 4) * TOKPC
        outT[b][:, s0:s0 + TOKPC] = r4[i]["xout"]
    return np.ascontiguousarray(np.transpose(outT, (0, 2, 1)))
